# version 17
# baseline (speedup 1.0000x reference)
from contextlib import ExitStack
import numpy as np
import concourse.bass as bass
import concourse.mybir as mybir
from concourse.bass_utils import run_bass_kernel_spmd
from concourse.bass_core_rust import IndirectOffsetOnAxis

F32 = mybir.dt.float32
BF16 = mybir.dt.bfloat16
I32 = mybir.dt.int32
U8 = mybir.dt.uint8
AF = mybir.ActivationFunctionType
ALU = mybir.AluOpType
AX = mybir.AxisListType

NEG = -1e30
DTSZ = {F32: 4, BF16: 2, I32: 4}


class Arena:
    def __init__(self, ap, nbytes):
        self.ap = ap
        self.nbytes = nbytes
        self.off = 0

    def mark(self):
        return self.off

    def reset(self, mark=0):
        self.off = mark

    def al(self, shape, dtype):
        sz = DTSZ[dtype]
        free = 1
        for d in shape[1:]:
            free *= d
        nb = free * sz
        off = (self.off + 127) & ~127
        assert off + nb <= self.nbytes, f"arena overflow: need {nb} at {off}, cap {self.nbytes}"
        self.off = off + nb
        v = self.ap[:, off:off + nb].bitcast(dtype)
        if len(shape) == 3:
            v = v.rearrange("p (a b) -> p a b", a=shape[1])
        elif len(shape) == 4:
            v = v.rearrange("p (a b c) -> p a b c", a=shape[1], b=shape[2])
        if shape[0] < 128:
            v = v[0:shape[0]]
        return v


def build_program(cfg):
    T = cfg["T"]
    R = cfg["R"]
    D = 1024
    H = 16
    DH = 64
    C = 64
    NCH = T // C
    NT = T // 128
    CAP = T // 4
    NCAP = CAP // 128
    DFFP = cfg["DFFP"]
    KFF = DFFP // 128
    NQS = CAP // 64
    NKB = CAP // 128
    dbg = cfg.get("debug", False)
    strict = cfg.get("strict", False)
    PH = cfg.get("phases", 99)
    NO_WEAVE = cfg.get("no_weave", False)
    NO_ACT_EVAC = cfg.get("no_act_evac", False)
    NO_ACT_CHAIN = cfg.get("no_act_chain", False) or NO_ACT_EVAC
    NO_ACT_ATTNT = cfg.get("no_act_attnT", False) or NO_ACT_EVAC
    NO_4D = cfg.get("no_4d", False)
    DMA_CHAIN = cfg.get("dma_chain", False)
    assert NCAP >= 1 and NKB >= 1 and T % 512 == 0

    class StopEmission(Exception):
        pass

    nc = bass.Bass("TRN2", target_bir_lowering=False,
                   detect_race_conditions=strict)

    x_in = nc.dram_tensor("x", [R, T, D], F32, kind="ExternalInput").ap()
    wqT_d = nc.dram_tensor("wqT", [D, D], F32, kind="ExternalInput").ap()
    wkT_d = nc.dram_tensor("wkT", [D, D], F32, kind="ExternalInput").ap()
    wvT_d = nc.dram_tensor("wvT", [D, D], F32, kind="ExternalInput").ap()
    woT_d = nc.dram_tensor("woT", [D, D], F32, kind="ExternalInput").ap()
    ab_raw = nc.dram_tensor("ab_raw", [1, 2 * H], F32, kind="ExternalInput").ap()
    wrouter_d = nc.dram_tensor("wrouter", [1, D], F32, kind="ExternalInput").ap()
    wqkvT_d = nc.dram_tensor("wqkvT", [D, 3 * D], F32, kind="ExternalInput").ap()
    woutT_d = nc.dram_tensor("woutT", [D, D], F32, kind="ExternalInput").ap()
    g12_d = nc.dram_tensor("g12", [16, 128], F32, kind="ExternalInput").ap()
    w1T_d = nc.dram_tensor("w1T", [D, DFFP], F32, kind="ExternalInput").ap()
    w2T_d = nc.dram_tensor("w2T", [D, DFFP], F32, kind="ExternalInput").ap()
    w3T_d = nc.dram_tensor("w3T", [DFFP, D], F32, kind="ExternalInput").ap()
    ident_bf_d = nc.dram_tensor("ident_bf", [128, 128], BF16, kind="ExternalInput").ap()
    ident_f32_d = nc.dram_tensor("ident_f32", [128, 128], F32, kind="ExternalInput").ap()
    onesrow_d = nc.dram_tensor("onesrow", [1, 128], F32, kind="ExternalInput").ap()
    ones128_d = nc.dram_tensor("ones128", [128, 1], F32, kind="ExternalInput").ap()
    ones64c_d = nc.dram_tensor("ones64c", [128, 2], F32, kind="ExternalInput").ap()
    ones2T_d = nc.dram_tensor("ones2T", [2, 128], F32, kind="ExternalInput").ap()
    causU_d = nc.dram_tensor("causU", [64, C], F32, kind="ExternalInput").ap()
    causL_d = nc.dram_tensor("causL", [64, C], F32, kind="ExternalInput").ap()
    dmask_d = nc.dram_tensor("dmask", [64, C], F32, kind="ExternalInput").ap()
    fmb_d = nc.dram_tensor("fmbase", [64, CAP + 64 * NQS], F32, kind="ExternalInput").ap()
    y_out = nc.dram_tensor("y", [R, T, D], F32, kind="ExternalOutput").ap()
    DUMP_QK = cfg.get("dump_qk", False)
    if DUMP_QK:
        qk_dump = nc.dram_tensor("qk_dump", [4, 128, T], BF16, kind="ExternalOutput").ap()

    xn_dram = nc.dram_tensor("xn_dram", [R, T, D], F32, kind="Internal").ap()
    xnb_dram = nc.dram_tensor("xnb_dram", [R, T, D], BF16, kind="Internal").ap()
    sc_dram = nc.dram_tensor("sc_dram", [R, NT, 128], F32, kind="Internal").ap()
    xsel_dram = [nc.dram_tensor(f"xsel_dram{i}", [CAP, D], F32, kind="Internal").ap() for i in range(R)]
    xselb_dram = nc.dram_tensor("xselb_dram", [R, CAP, D], BF16, kind="Internal").ap()
    hnb_dram = nc.dram_tensor("hnb_dram", [R, CAP, D], BF16, kind="Internal").ap()
    dp_dram = [nc.dram_tensor(f"dp_dram{i}", [CAP, D], F32, kind="Internal").ap() for i in range(R)]

    dbg_out = {}
    if dbg:
        def dbg_t(name, shape, dt=F32):
            dbg_out[name] = nc.dram_tensor("dbg_" + name, shape, dt, kind="ExternalOutput").ap()
        dbg_t("xn", [R, T, D])
        dbg_t("ranks", [R, 128, NT])
        dbg_t("xsel", [R, CAP, D])
        dbg_t("qT0", [R, 128, T], BF16)
        dbg_t("kTn0", [R, 128, T], BF16)
        dbg_t("coT0", [R, 128, T], BF16)
        dbg_t("s_attn", [R, NT, 128, D], BF16)
        dbg_t("qkvT0", [R, 128, 3 * CAP], BF16)
        dbg_t("oT0", [R, 128, CAP], BF16)
        dbg_t("zT0", [R, 128, CAP])
        dbg_t("dp", [R, CAP, D])
        dbg_t("probe16", [128, 8192], BF16)
        dbg_t("probe32", [128, 8192])

    ctx = ExitStack()
    sbf = lambda n, s, d=F32: ctx.enter_context(nc.sbuf_tensor(n, s, d))

    with ctx:
        id_bf = sbf("id_bf", [128, 128], BF16)
        id_f32 = sbf("id_f32", [128, 128], F32)
        onesrow_sb = sbf("onesrow_sb", [1, 128])
        ones128_sb = sbf("ones128_sb", [128, 1])
        ones64c_sb = sbf("ones64c_sb", [128, 2])
        ones2T_sb = sbf("ones2T_sb", [2, 128])
        causU_sb = sbf("causU_sb", [64, C])
        causL_sb = sbf("causL_sb", [64, C])
        dmask_sb = sbf("dmask_sb", [64, C])
        fmb_sb = sbf("fmb_sb", [64, CAP + 64 * NQS])
        wr_row = sbf("wr_row", [1, D])
        wrb_sb = sbf("wrb_sb", [128, D])
        ab_sb = sbf("ab_sb", [1, 2 * H])
        abH_sb = sbf("abH_sb", [128, 2 * H])
        mUL_sb = sbf("mUL_sb", [64, H, 2, C], BF16)
        g1c_sb = sbf("g1c_sb", [128, 8])
        g2c_sb = sbf("g2c_sb", [128, 8])
        gr_sb = sbf("gr_sb", [16, 128])
        stat = sbf("stat", [128, 8])
        sccol = sbf("sccol", [128, NT])
        rkcol = sbf("rkcol", [128, NT])
        rki32 = sbf("rki32", [128, NT], I32)
        scT_sb = sbf("scT_sb", [16, 128])
        Yt_sb = sbf("Yt_sb", [64, 2, C], BF16)
        M32 = sbf("M32", [128, 2, C])
        M16 = sbf("M16", [128, 2, C], BF16)
        co_f = sbf("co_f", [128, C])
        sm_sb = sbf("sm_sb", [64, 2, C])
        ssum = sbf("ssum", [64, 2])
        srec = sbf("srec", [64, 2])
        chNWL = sbf("chNWL", [64, 4, 2, 3 * C], BF16)
        VB2 = sbf("VB2", [64, 4, 2, C], BF16)
        attn2 = sbf("attn2", [64, 4, 2, C], BF16)
        attnT2 = sbf("attnT2", [64, 4, 2, C], BF16)
        fS_sb = sbf("fS_sb", [64, 2, CAP])
        fsum_sb = sbf("fsum_sb", [64, 2])
        frec_sb = sbf("frec_sb", [64, 2])
        fattn_sb = sbf("fattn_sb", [64, 2, CAP], BF16)
        vnat_sb = sbf("vnat_sb", [128, NKB, 2, DH], BF16)
        fst_sb = sbf("fst_sb", [128, NCAP, 4])
        ssq2_sb = sbf("ssq2_sb", [1, CAP])
        rstd2_sb = sbf("rstd2_sb", [1, CAP])
        rbc2_sb = sbf("rbc2_sb", [128, CAP])
        su_sb = sbf("su_sb", [128, CAP], BF16)
        vv_sb = sbf("vv_sb", [128, CAP], BF16)
        zsq_sb = [sbf(f"zsq_sb{i}", [128, CAP]) for i in range(2)]
        sg_t = sbf("sg_t", [128, D])
        dbgt = sbf("dbgt", [64, 640]) if dbg else None
        s_sb = sbf("s_sb", [128, NT, D], BF16)

        ARENA_KB = cfg.get("arena_kb", 123)
        AR = sbf("AR", [128, ARENA_KB * 1024], U8)
        ar = Arena(AR[:], ARENA_KB * 1024)

        pbW = [ctx.enter_context(nc.psum_tensor(f"pbW{i}", [128, 1024], F32)) for i in range(4)]
        pb = [pbW[i // 2][:, (i % 2) * 512:(i % 2) * 512 + 512] for i in range(8)]

        sems = {}
        counts = {}
        ops = {e: [] for e in ("sync", "vector", "scalar", "gpsimd", "tensor")}
        lastw = {e: {} for e in ops}

        def _sem(name):
            if name not in sems:
                sems[name] = ctx.enter_context(nc.semaphore("s_" + name))
                counts[name] = 0
            return sems[name]

        def emit(engine, fn):
            ops[engine].append(fn)

        def wait(engine, sem_name, val):
            if val is None or val <= 0:
                return
            s = _sem(sem_name)
            if lastw[engine].get(sem_name, 0) >= val:
                return
            lastw[engine][sem_name] = val
            emit(engine, lambda e, s=s, v=val: e.wait_ge(s, v))

        def _mk(engine, sem_name, by, fn):
            s = _sem(sem_name)
            counts[sem_name] += by
            emit(engine, lambda e, f=fn, s=s, b=by: f(e).then_inc(s, b))
            return counts[sem_name]

        def sdma(st, out, in_, **kw):
            return _mk("sync", st, 16, lambda e: e.dma_start(out=out, in_=in_, **kw))

        def adma(st, out, in_, **kw):
            return _mk("scalar", st, 16, lambda e: e.dma_start(out=out, in_=in_, **kw))

        def gdma(st, out, in_, **kw):
            return _mk("gpsimd", st, 16, lambda e: e.dma_start(out=out, in_=in_, **kw))

        def pe(fn):
            return _mk("tensor", "pe", 1, fn)

        def dve(fn):
            if strict:
                wait("vector", "dv", counts.get("dv", 0))
            return _mk("vector", "dv", 1, fn)

        def act(fn):
            if strict:
                wait("scalar", "ac", counts.get("ac", 0))
            return _mk("scalar", "ac", 1, fn)

        def stx(st, out, in_):
            return _mk("sync", st, 16, lambda e: e.dma_start(out=out, in_=in_))

        def MM(bank_ap, lhsT, rhs, start, tile=None):
            return pe(lambda e: e.matmul(bank_ap, lhsT, rhs, start=start, stop=True,
                                         skip_group_check=not start,
                                         tile_position=tile))

        def pe_dv_barrier():
            wait("tensor", "dv", counts.get("dv", 0))

        def pe_ac_barrier():
            wait("tensor", "ac", counts.get("ac", 0))

        phase_marks = []
        dbg_out["phase_marks"] = phase_marks

        def mark_phase(name):
            phase_marks.append((name, {e: len(ops[e]) for e in ops}))

        def all_engine_barrier():
            snap = dict(counts)
            for e in ops:
                for name, val in snap.items():
                    wait(e, name, val)

        bc_cell = [None]

        def _init_bc(e):
            bc_cell[0] = e.alloc_register("bc_reg")
            return e.reg_mov(bc_cell[0], CAP - 1)

        emit("gpsimd", lambda e: _init_bc(e))

        sdma("cs", id_bf[:], ident_bf_d[:, :])
        sdma("cs", id_f32[:], ident_f32_d[:, :])
        sdma("cs", onesrow_sb[:], onesrow_d[:, :])
        sdma("cs", ones128_sb[:], ones128_d[:, :])
        sdma("cs", ones64c_sb[:], ones64c_d[:, :])
        sdma("cs", ones2T_sb[:], ones2T_d[:, :])
        sdma("cs", causU_sb[:], causU_d[:, :])
        sdma("cs", causL_sb[:], causL_d[:, :])
        sdma("cs", dmask_sb[:], dmask_d[:, :])
        sdma("cs", fmb_sb[:], fmb_d[:, :])
        sdma("cs", gr_sb[:], g12_d[:, :])
        sdma("cs", wr_row[:], wrouter_d[:, :])
        v_c = sdma("cs", ab_sb[:], ab_raw[:, :])

        wait("tensor", "cs", v_c)
        for h in range(D // 512):
            b = pb[6 + (h % 2)]
            v = pe(lambda e, h=h, b=b: e.matmul(b[:, :], onesrow_sb[0:1, :],
                                                wr_row[0:1, h * 512:(h + 1) * 512],
                                                start=True, stop=True))
            wait("vector", "pe", v)
            dve(lambda e, h=h, b=b: e.tensor_copy(wrb_sb[:, h * 512:(h + 1) * 512], b[:, :]))
        pe_dv_barrier()
        v = pe(lambda e: e.matmul(pb[6][0:128, 0:16], gr_sb[:], id_f32[0:16, 0:16],
                                  start=True, stop=True))
        wait("vector", "pe", v)
        dve(lambda e: e.tensor_copy(g1c_sb[:], pb[6][:, 0:8]))
        v_g12 = dve(lambda e: e.tensor_copy(g2c_sb[:], pb[6][:, 8:16]))

        wait("scalar", "cs", v_c)
        v = act(lambda e: e.activation(ab_sb[:], ab_sb[:], AF.Sigmoid))
        wait("vector", "ac", v)
        dve(lambda e: e.tensor_scalar(ab_sb[:, 0:H], ab_sb[:, 0:H], 0.5, None, ALU.mult))
        v_ab = dve(lambda e: e.tensor_scalar(ab_sb[:, H:2 * H], ab_sb[:, H:2 * H], 2.0, None, ALU.mult))
        wait("tensor", "dv", v_ab)
        pe_dv_barrier()
        v = pe(lambda e: e.matmul(pb[6][0:128, 0:32], onesrow_sb[0:1, 0:128],
                                  ab_sb[0:1, 0:32], start=True, stop=True))
        wait("vector", "pe", v)
        dve(lambda e: e.tensor_copy(abH_sb[:], pb[6][0:128, 0:32]))
        for h in range(H):
            dve(lambda e, h=h: e.tensor_tensor(
                mUL_sb[:, h, 0, :], causU_sb[:], abH_sb[0:64, h:h + 1].to_broadcast([64, C]), ALU.mult))
            dve(lambda e, h=h: e.tensor_tensor(
                mUL_sb[:, h, 1, :], causL_sb[:], abH_sb[0:64, h:h + 1].to_broadcast([64, C]), ALU.mult))
        v_consts_dv = counts.get("dv", 0)

        def ckpt(n):
            if PH < n:
                raise StopEmission()
        try:
          for r in range(R):
            ckpt(1)
            mark_phase(f'ln_r{r}')
            ar.reset(0)
            coT_sb = ar.al([128, 8, T], BF16)
            mk_wo = ar.mark()
            qT_sb = ar.al([128, T], BF16)
            kTn_sb = ar.al([128, T], BF16)
            vT_sb = ar.al([128, T], BF16)
            kraw_sb = kTn_sb
            wsl = [[ar.al([128, 8, 128], BF16) for _ in range(2)] for _ in range(3)]
            nrm_sb = ar.al([2, T], F32)
            rn_sb = ar.al([2, T], F32)
            ln_bytes = (2 * 4 * D + 2 * 4 * D + 2 * 2 * D + 4 * D
                        + 4 * max(T, D) + 4 * T + 4 * T + 1024)
            if ar.off < ln_bytes:
                ar.reset(ln_bytes)
            xnT = ar.al([128, 8, T], BF16)
            mk_kn = ar.mark()
            ksq_sb = ar.al([128, T], F32)
            rbc_sb = ar.al([128, T], F32)
            ar.reset(mk_kn)
            kaT_sb = ar.al([128, T], BF16)
            ar.reset(0)
            x_t = [ar.al([128, D], F32) for _ in range(2)]
            xn_t = [ar.al([128, D], F32) for _ in range(2)]
            xnb_t = [ar.al([128, D], BF16) for _ in range(2)]
            xsq_t = ar.al([128, D], F32)
            trash = ar.al([128, max(T, D)], F32)
            sbcast = ar.al([128, T], F32)
            srow = ar.al([1, T], F32)

            all_engine_barrier()
            v_sqred_prev = 0
            v_xn_st = {}
            xt_free = {0: (0, 0), 1: (0, 0)}
            xn_free = {0: (0, 0, 0), 1: (0, 0, 0)}
            for j in range(NT):
                bi = j % 2
                dv0, ac0 = xt_free[bi]
                wait("sync", "dv", dv0)
                wait("sync", "ac", ac0)
                v_x = sdma(f"x{bi}", x_t[bi][:], x_in[r, j * 128:(j + 1) * 128, :])
                wait("scalar", f"x{bi}", v_x)
                wait("scalar", "dv", v_sqred_prev)
                v_sq = act(lambda e, bi=bi: e.activation(xsq_t[:], x_t[bi][:], AF.Square,
                                                         accum_out=stat[:, 1:2]))
                wait("vector", f"x{bi}", v_x)
                ld0, gp0, db0 = xn_free[bi]
                wait("vector", f"xs{bi}", ld0)
                wait("vector", f"xb{bi}", gp0)
                wait("vector", f"db{bi}", db0)
                dve(lambda e, bi=bi: e.reduce_sum(stat[:, 0:1], x_t[bi][:], axis=AX.X))
                wait("vector", "ac", v_sq)
                dve(lambda e: e.tensor_scalar(stat[:, 2:3], stat[:, 0:1], 1.0 / D, None, ALU.mult))
                v_sqred_prev = dve(lambda e: e.tensor_scalar(stat[:, 3:4], stat[:, 1:2], 1.0 / D, None, ALU.mult))
                dve(lambda e: e.tensor_tensor(stat[:, 4:5], stat[:, 2:3], stat[:, 2:3], ALU.mult))
                dve(lambda e: e.tensor_tensor(stat[:, 5:6], stat[:, 3:4], stat[:, 4:5], ALU.subtract))
                v_var = dve(lambda e: e.tensor_scalar(stat[:, 5:6], stat[:, 5:6], 1e-5, None, ALU.add))
                wait("scalar", "dv", v_var)
                v_std = act(lambda e: e.activation(stat[:, 6:7], stat[:, 5:6], AF.Sqrt))
                wait("vector", "ac", v_std)
                dve(lambda e: e.reciprocal(stat[:, 7:8], stat[:, 6:7]))
                v_xn = dve(lambda e, bi=bi: e.tensor_scalar(
                    xn_t[bi][:], x_t[bi][:], stat[:, 2:3], stat[:, 7:8], ALU.subtract, ALU.mult))
                dve(lambda e, bi=bi: e.tensor_tensor(trash[:, 0:D], xn_t[bi][:], wrb_sb[:], ALU.mult))
                dve(lambda e, j=j: e.reduce_sum(sccol[:, j:j + 1], trash[:, 0:D], axis=AX.X))
                v_cast = dve(lambda e, bi=bi: e.tensor_copy(xnb_t[bi][:], xn_t[bi][:]))
                wait("sync", "dv", v_xn)
                v_xn_st[j] = sdma(f"xs{bi}", xn_dram[r, j * 128:(j + 1) * 128, :], xn_t[bi][:])
                wait("gpsimd", "dv", v_cast)
                v_xnb = gdma(f"xb{bi}", xnb_dram[r, j * 128:(j + 1) * 128, :], xnb_t[bi][:])
                v_dbg = 0
                if dbg:
                    v_dbg = stx(f"db{bi}", dbg_out["xn"][r, j * 128:(j + 1) * 128, :], xn_t[bi][:])
                xt_free[bi] = (v_cast, v_sq)
                xn_free[bi] = (v_xn_st[j], v_xnb, v_dbg)
            v_scores = counts.get("dv", 0)

            wait("scalar", "xb0", counts.get("xb0", 0))
            wait("scalar", "xb1", counts.get("xb1", 0))
            v_xnT = 0
            for j in range(8):
                v_xnT = adma("xnt", xnT[:, j, :], xnb_dram[r, :, j * 128:(j + 1) * 128], transpose=True)

            ckpt(2)
            mark_phase(f'ranks_r{r}')
            wait("tensor", "dv", v_scores)
            v = pe(lambda e: e.matmul(pb[6][0:NT, 0:128], sccol[:], id_f32[:, 0:128],
                                      start=True, stop=True))
            wait("vector", "pe", v)
            v = dve(lambda e: e.tensor_copy(scT_sb[0:NT, :], pb[6][0:NT, 0:128]))
            wait("sync", "dv", v)
            v = sdma("scm", sc_dram[r], scT_sb[0:NT, :])
            wait("sync", "scm", v)
            v_srow = sdma("scm", srow[:, :], sc_dram[r].rearrange("j p -> (j p)").unsqueeze(0))
            wait("tensor", "scm", v_srow)
            for h in range(T // 512):
                b = pb[6 + (h % 2)]
                pe_dv_barrier()
                v = pe(lambda e, h=h, b=b: e.matmul(b[:, :], onesrow_sb[0:1, :],
                                                    srow[0:1, h * 512:(h + 1) * 512],
                                                    start=True, stop=True))
                wait("vector", "pe", v)
                dve(lambda e, h=h, b=b: e.tensor_copy(sbcast[:, h * 512:(h + 1) * 512], b[:, :]))
            wait("vector", "dv", counts.get("dv", 0))
            for j in range(NT):
                dve(lambda e, j=j: e.tensor_scalar(
                    trash[:, 0:T], sbcast[:, 0:T], sccol[:, j:j + 1], 0.0,
                    ALU.is_gt, ALU.add, accum_out=rkcol[:, j:j + 1]))
            v_rk = dve(lambda e: e.tensor_copy(rki32[:], rkcol[:]))
            if dbg:
                wait("sync", "dv", v_rk)
                stx("dbg", dbg_out["ranks"][r], rkcol[:])
            wait("gpsimd", "dv", v_rk)
            wait("gpsimd", "xs0", counts.get("xs0", 0))
            wait("gpsimd", "xs1", counts.get("xs1", 0))
            wait("sync", "xs0", counts.get("xs0", 0))
            wait("sync", "xs1", counts.get("xs1", 0))
            wait("sync", "xb0", counts.get("xb0", 0))
            wait("sync", "xb1", counts.get("xb1", 0))
            wait("sync", "db0", counts.get("db0", 0))
            wait("sync", "db1", counts.get("db1", 0))
            v_scat = {0: 0, 1: 0}
            for j in range(NT):
                bi = j % 2
                wait("sync", f"sc{bi}", v_scat[bi])
                v_stage = sdma(f"x{bi}", xn_t[bi][:], xn_dram[r, j * 128:(j + 1) * 128, :])
                wait("gpsimd", f"x{bi}", v_stage)
                v_scat[bi] = _mk("gpsimd", f"sc{bi}", 16, lambda e, j=j, bi=bi, r=r: e.indirect_dma_start(
                    out=xsel_dram[r],
                    out_offset=IndirectOffsetOnAxis(rki32[:, j:j + 1], 0),
                    in_=xn_t[bi][:],
                    in_offset=None,
                    bounds_check=bc_cell[0],
                    oob_is_err=False))

            ckpt(3)
            mark_phase(f'dn_setup_r{r}')
            all_engine_barrier()
            ckpt(3.05)

            wait("tensor", "xnt", v_xnT)
            wait("tensor", "dv", v_consts_dv)
            wsl_free = {}
            for hp in range(8):
                bi = hp % 2
                if dbg:
                    wait("vector", "dbg", counts.get("dbg", 0))
                wait("gpsimd", "pe", wsl_free.get(bi, 0))
                gdma(f"wa{bi}", wsl[0][bi][:], wqT_d.rearrange("(k p) o -> p k o", p=128)[:, :, hp * 128:(hp + 1) * 128])
                gdma(f"wa{bi}", wsl[1][bi][:], wkT_d.rearrange("(k p) o -> p k o", p=128)[:, :, hp * 128:(hp + 1) * 128])
                v_w = gdma(f"wa{bi}", wsl[2][bi][:], wvT_d.rearrange("(k p) o -> p k o", p=128)[:, :, hp * 128:(hp + 1) * 128])
                wait("tensor", f"wa{bi}", v_w)
                ckpt(3.3)

                def proj_pass(wi, evac):
                    for tt in range(T // 512):
                        b = pb[6 + (tt % 2)]
                        pe_dv_barrier()
                        pe_ac_barrier()
                        ckpt(3.31)
                        for kt in range(8):
                            pe(lambda e, b=b, kt=kt, tt=tt, wi=wi, bi=bi: e.matmul(
                                b[:, :], wsl[wi][bi][:, kt, :],
                                xnT[:, kt, tt * 512:(tt + 1) * 512],
                                start=(kt == 0), stop=(kt == 7)))
                            ckpt(3.32 + kt / 100.0)
                        ckpt(3.33)
                        evac(b, tt, counts.get("pe", 0))
                        ckpt(3.35)

                def ev_q(b, tt, v):
                    wait("vector", "pe", v)
                    dve(lambda e, b=b, tt=tt: e.tensor_copy(qT_sb[:, tt * 512:(tt + 1) * 512], b[:, :]))

                def ev_k(b, tt, v):
                    wait("vector", "pe", v)
                    vc = dve(lambda e, b=b, tt=tt: e.tensor_copy(kraw_sb[:, tt * 512:(tt + 1) * 512], b[:, :]))
                    wait("scalar", "dv", vc)
                    act(lambda e, b=b, tt=tt: e.activation(ksq_sb[:, tt * 512:(tt + 1) * 512], b[:, :], AF.Square))

                def ev_v(b, tt, v):
                    wait("vector", "pe", v)
                    dve(lambda e, b=b, tt=tt: e.tensor_copy(vT_sb[:, tt * 512:(tt + 1) * 512], b[:, :]))

                proj_pass(0, ev_q)
                ckpt(3.4)
                proj_pass(1, ev_k)
                proj_pass(2, ev_v)
                wsl_free[bi] = counts.get("pe", 0)
                ckpt(3.5)

                pe_ac_barrier()
                pe_dv_barrier()
                for tt in range(T // 512):
                    v = pe(lambda e, tt=tt: e.matmul(
                        pb[5][0:2, 0:512], ones64c_sb[:], ksq_sb[:, tt * 512:(tt + 1) * 512],
                        start=True, stop=True, skip_group_check=True))
                    wait("vector", "pe", v)
                    v_ev = dve(lambda e, tt=tt: e.tensor_copy(nrm_sb[:, tt * 512:(tt + 1) * 512], pb[5][0:2, 0:512]))
                    wait("tensor", "dv", v_ev)
                wait("scalar", "dv", counts.get("dv", 0))
                v = act(lambda e: e.activation(nrm_sb[:], nrm_sb[:], AF.Sqrt))
                wait("vector", "ac", v)
                dve(lambda e: e.tensor_scalar(nrm_sb[:], nrm_sb[:], 1e-12, None, ALU.max))
                v_rn = dve(lambda e: e.reciprocal(rn_sb[:], nrm_sb[:]))
                wait("tensor", "dv", v_rn)
                for tt in range(T // 512):
                    b = pb[6 + (tt % 2)]
                    pe_dv_barrier()
                    v = pe(lambda e, b=b, tt=tt: e.matmul(
                        b[:, :], ones2T_sb[:], rn_sb[:, tt * 512:(tt + 1) * 512],
                        start=True, stop=True))
                    wait("vector", "pe", v)
                    dve(lambda e, b=b, tt=tt: e.tensor_copy(rbc_sb[:, tt * 512:(tt + 1) * 512], b[:, :]))
                dve(lambda e: e.tensor_tensor(kTn_sb[:], kTn_sb[:], rbc_sb[:], ALU.mult))
                dve(lambda e, hp=hp: e.tensor_tensor(
                    kaT_sb[0:64, :], kTn_sb[0:64, :],
                    abH_sb[0:64, 2 * hp:2 * hp + 1].to_broadcast([64, T]), ALU.mult))
                dve(lambda e, hp=hp: e.tensor_tensor(
                    kaT_sb[64:128, :], kTn_sb[64:128, :],
                    abH_sb[64:128, 2 * hp + 1:2 * hp + 2].to_broadcast([64, T]), ALU.mult))
                dve(lambda e: e.memset(M32[:], 0.0))
                v_state0 = dve(lambda e: e.memset(M16[:], 0.0))
                ckpt(3.7)

                if dbg and hp == 0:
                    wait("sync", "dv", v_state0)
                    stx("dbg", dbg_out["qT0"][r], qT_sb[:])
                    stx("dbg", dbg_out["kTn0"][r], kTn_sb[:])

                ckpt(4)
                mark_phase(f'chunks_r{r}')
                wait("tensor", "dv", v_state0)
                chst = [dict() for _ in range(NCH)]
                v_seq_st = {-1: v_state0}
                bpair = abH_sb[0:64, 16 + 2 * hp:16 + 2 * hp + 2].unsqueeze(2).to_broadcast([64, 2, C])
                bpair2d = abH_sb[0:64, 16 + 2 * hp:16 + 2 * hp + 2]
                mULp = mUL_sb[:, 2 * hp:2 * hp + 2, :, :]
                dmask2 = dmask_sb[:].unsqueeze(1).to_broadcast([64, 2, C])

                def gen1_mms(ci):
                    cs = ci * C
                    par = ci % 2
                    b0 = pb[par * 2]
                    b1 = pb[par * 2 + 1]
                    qTa = qT_sb[0:64, cs:cs + C]
                    qTb = qT_sb[64:128, cs:cs + C]
                    kTa = kTn_sb[0:64, cs:cs + C]
                    kTb = kTn_sb[64:128, cs:cs + C]
                    vTa = vT_sb[0:64, cs:cs + C]
                    vTb = vT_sb[64:128, cs:cs + C]
                    if ci >= 2:
                        wait("tensor", "dv", chst[ci - 2].get("dv_chain", 0))
                        wait("tensor", "dv", chst[ci - 2].get("dv_attnT", 0))
                        wait("tensor", "ac", chst[ci - 2].get("ac_attnT", 0))
                        if DMA_CHAIN:
                            wait("tensor", f"gl{ci % 2}", chst[ci - 2].get("gd_chain", 0))
                    MM(b1[0:64, 0:64], kTb, kTb, True, (64, 0))
                    MM(b1[0:64, 64:128], qTb, kTb, False, (64, 0))
                    MM(b1[0:64, 128:192], kTb, id_bf[64:128, 64:128], False, (64, 0))
                    MM(b1[0:64, 192:256], vTb, id_bf[64:128, 64:128], False, (64, 0))
                    MM(b0[0:64, 0:64], kTa, kTa, True, (0, 0))
                    MM(b0[0:64, 64:128], qTa, kTa, False, (0, 0))
                    MM(b0[0:64, 128:192], kTa, id_bf[0:64, 0:64], False, (0, 0))
                    v = MM(b0[0:64, 192:256], vTa, id_bf[0:64, 0:64], False, (0, 0))
                    chst[ci]["pe_gen1"] = v

                def gen1_evacs(ci):
                    par = ci % 2
                    pari = ci % 4
                    st = chst[ci]
                    bsp = pbW[par][0:64, :].rearrange("p (h c) -> p h c", h=2)
                    Gs4 = bsp[:, :, 0:C].unsqueeze(2).to_broadcast([64, 2, 2, C])
                    NL4 = chNWL[:, pari, :, C:3 * C].rearrange("p h (o c) -> p h o c", o=2)
                    Uc = chNWL[:, pari, :, 0:C]
                    VB2c = VB2[:, pari, :, :]
                    at2 = attn2[:, pari, :, :]
                    if dbg:
                        wait("vector", "dbg", counts.get("dbg", 0))
                    wait("vector", "pe", st["pe_gen1"])
                    if NO_4D:
                        dve(lambda e, bsp=bsp, pari=pari, mULp=mULp: e.tensor_tensor(
                            chNWL[:, pari, :, C:2 * C], bsp[:, :, 0:C], mULp[:, :, 0, :], ALU.mult))
                        dve(lambda e, bsp=bsp, pari=pari, mULp=mULp: e.tensor_tensor(
                            chNWL[:, pari, :, 2 * C:3 * C], bsp[:, :, 0:C], mULp[:, :, 1, :], ALU.mult))
                    else:
                        dve(lambda e, Gs4=Gs4, NL4=NL4, mULp=mULp: e.tensor_tensor(NL4, Gs4, mULp, ALU.mult))
                    dve(lambda e, bsp=bsp, Uc=Uc: e.tensor_copy(Uc, bsp[:, :, 128:192]))
                    dve(lambda e, bsp=bsp, VB2c=VB2c, bpair=bpair: e.tensor_tensor(VB2c, bsp[:, :, 192:256], bpair, ALU.mult))
                    v = dve(lambda e, bsp=bsp: e.scalar_tensor_tensor(
                        sm_sb[:], bsp[:, :, 64:128], 0.125, dmask2, ALU.mult, ALU.add))
                    wait("scalar", "dv", v)
                    act(lambda e: e.activation(sm_sb[:, 0, :], sm_sb[:, 0, :], AF.Exp,
                                               accum_out=ssum[:, 0:1]))
                    v = act(lambda e: e.activation(sm_sb[:, 1, :], sm_sb[:, 1, :], AF.Exp,
                                                   accum_out=ssum[:, 1:2]))
                    wait("vector", "ac", v)
                    dve(lambda e, bp2=bpair2d: e.tensor_tensor(ssum[:], ssum[:], bp2, ALU.mult))
                    dve(lambda e: e.reciprocal(srec[:], ssum[:]))
                    v = dve(lambda e, at2=at2: e.tensor_tensor(at2, sm_sb[:],
                                                               srec[:].unsqueeze(2).to_broadcast([64, 2, C]), ALU.mult))
                    st["dv_par"] = v

                def chain_lvl_mms(ci, li, lvl):
                    par = ci % 2
                    pari = ci % 4
                    b = pb[par * 2 + ((li + 1) % 2)]
                    st = chst[ci]
                    wait("tensor", "dv", st.get("dv_chain", st["dv_par"]))
                    wait("tensor", "ac", st.get("ac_chain", 0))
                    if DMA_CHAIN:
                        wait("tensor", f"gl{ci % 2}", st.get("gd_chain", 0))
                    UN_a = chNWL[:, pari, 0, 0:2 * C]
                    UN_b = chNWL[:, pari, 1, 0:2 * C]
                    N_a = chNWL[:, pari, 0, C:2 * C]
                    N_b = chNWL[:, pari, 1, C:2 * C]
                    L_a = chNWL[:, pari, 0, 2 * C:3 * C]
                    L_b = chNWL[:, pari, 1, 2 * C:3 * C]
                    if lvl < 16:
                        MM(b[0:64, 0:128], L_a, UN_a, True, (0, 0))
                        v = MM(b[0:64, 128:256], L_b, UN_b, False, (0, 0))
                        MM(b[0:64, 256:320], N_a, L_a, False, (0, 0))
                        v = MM(b[0:64, 320:384], N_b, L_b, False, (0, 0))
                    elif lvl == 16:
                        MM(b[0:64, 0:64], L_a, chNWL[:, pari, 0, 0:C], True, (0, 0))
                        v = MM(b[0:64, 64:128], L_b, chNWL[:, pari, 1, 0:C], False, (0, 0))
                        MM(b[0:64, 256:320], N_a, L_a, False, (0, 0))
                        v = MM(b[0:64, 320:384], N_b, L_b, False, (0, 0))
                    else:
                        MM(b[0:64, 0:64], L_a, chNWL[:, pari, 0, 0:C], True, (0, 0))
                        v = MM(b[0:64, 64:128], L_b, chNWL[:, pari, 1, 0:C], False, (0, 0))
                    st["pe_chain"] = v

                def chain_lvl_evacs(ci, li, lvl):
                    par = ci % 2
                    pari = ci % 4
                    b = pb[par * 2 + ((li + 1) % 2)]
                    st = chst[ci]
                    Uc = chNWL[:, pari, :, 0:C]
                    sgn = -1.0 if lvl == 1 else 1.0
                    if dbg:
                        wait("vector", "dbg", counts.get("dbg", 0))
                        wait("scalar", "dbg", counts.get("dbg", 0))
                    wait("vector", "pe", st["pe_chain"])
                    if lvl < 16:
                        v = dve(lambda e, b=b, sgn=sgn, Uc=Uc: e.scalar_tensor_tensor(
                            Uc, b[0:64, 0:256].rearrange("p (h c) -> p h c", h=2)[:, :, 0:C],
                            sgn, Uc, ALU.mult, ALU.add))
                        if NO_ACT_CHAIN:
                            if DMA_CHAIN:
                                wait("gpsimd", "pe", st["pe_chain"])
                                wait("gpsimd", "dv", v)
                                st["gd_chain"] = gdma(
                                    f"gl{ci % 2}", chNWL[:, pari, :, 2 * C:3 * C],
                                    b[0:64, 256:384].rearrange("p (h c) -> p h c", h=2))
                                v = dve(lambda e, b=b, pari=pari: e.tensor_copy(
                                    chNWL[:, pari, :, C:2 * C],
                                    b[0:64, 0:256].rearrange("p (h c) -> p h c", h=2)[:, :, C:2 * C]))
                            else:
                                dve(lambda e, b=b, pari=pari: e.tensor_copy(
                                    chNWL[:, pari, :, C:2 * C],
                                    b[0:64, 0:256].rearrange("p (h c) -> p h c", h=2)[:, :, C:2 * C]))
                                v = dve(lambda e, b=b, pari=pari: e.tensor_copy(
                                    chNWL[:, pari, :, 2 * C:3 * C],
                                    b[0:64, 256:384].rearrange("p (h c) -> p h c", h=2)))
                        else:
                            wait("scalar", "dv", v)
                            wait("scalar", "pe", st["pe_chain"])
                            act(lambda e, b=b, pari=pari: e.activation(
                                chNWL[:, pari, :, C:2 * C],
                                b[0:64, 0:256].rearrange("p (h c) -> p h c", h=2)[:, :, C:2 * C], AF.Copy))
                            va = act(lambda e, b=b, pari=pari: e.activation(
                                chNWL[:, pari, :, 2 * C:3 * C],
                                b[0:64, 256:384].rearrange("p (h c) -> p h c", h=2), AF.Copy))
                            st["ac_chain"] = va
                    elif lvl == 16:
                        v = dve(lambda e, b=b, Uc=Uc: e.scalar_tensor_tensor(
                            Uc, b[0:64, 0:128].rearrange("p (h c) -> p h c", h=2),
                            1.0, Uc, ALU.mult, ALU.add))
                        if NO_ACT_CHAIN:
                            if DMA_CHAIN:
                                wait("gpsimd", "pe", st["pe_chain"])
                                wait("gpsimd", "dv", v)
                                st["gd_chain"] = gdma(
                                    f"gl{ci % 2}", chNWL[:, pari, :, 2 * C:3 * C],
                                    b[0:64, 256:384].rearrange("p (h c) -> p h c", h=2))
                            else:
                                v = dve(lambda e, b=b, pari=pari: e.tensor_copy(
                                    chNWL[:, pari, :, 2 * C:3 * C],
                                    b[0:64, 256:384].rearrange("p (h c) -> p h c", h=2)))
                        else:
                            wait("scalar", "dv", v)
                            wait("scalar", "pe", st["pe_chain"])
                            va = act(lambda e, b=b, pari=pari: e.activation(
                                chNWL[:, pari, :, 2 * C:3 * C],
                                b[0:64, 256:384].rearrange("p (h c) -> p h c", h=2), AF.Copy))
                            st["ac_chain"] = va
                    else:
                        v = dve(lambda e, b=b, Uc=Uc: e.scalar_tensor_tensor(
                            Uc, b[0:64, 0:128].rearrange("p (h c) -> p h c", h=2),
                            1.0, Uc, ALU.mult, ALU.add))
                    st["dv_chain"] = v

                def attnT_mms(ci):
                    par = ci % 2
                    pari = ci % 4
                    b = pb[par * 2 + 1]
                    st = chst[ci]
                    wait("tensor", "dv", st["dv_chain"])
                    wait("tensor", "ac", st.get("ac_chain", 0))
                    MM(b[0:64, 384:448], attn2[:, pari, 0, :], id_bf[0:64, 0:64], False, (0, 0))
                    st["pe_attnT"] = MM(b[0:64, 448:512], attn2[:, pari, 1, :], id_bf[0:64, 0:64], False, (0, 0))

                def attnT_evac(ci):
                    par = ci % 2
                    pari = ci % 4
                    b = pb[par * 2 + 1]
                    st = chst[ci]
                    if NO_ACT_ATTNT:
                        wait("vector", "pe", st["pe_attnT"])
                        st["dv_attnT"] = dve(lambda e, b=b, pari=pari: e.tensor_copy(
                            attnT2[:, pari, :, :],
                            b[0:64, 384:512].rearrange("p (h c) -> p h c", h=2)))
                        st["ac_attnT"] = 0
                        return
                    if dbg:
                        wait("scalar", "dbg", counts.get("dbg", 0))
                    wait("scalar", "pe", st["pe_attnT"])
                    st["ac_attnT"] = act(lambda e, b=b, pari=pari: e.activation(
                        attnT2[:, pari, :, :],
                        b[0:64, 384:512].rearrange("p (h c) -> p h c", h=2), AF.Copy))

                def seq1_mms(ci):
                    cs = ci * C
                    pari = ci % 4
                    st = chst[ci]
                    qTa = qT_sb[0:64, cs:cs + C]
                    qTb = qT_sb[64:128, cs:cs + C]
                    kaTa = kaT_sb[0:64, cs:cs + C]
                    kaTb = kaT_sb[64:128, cs:cs + C]
                    VB2c = VB2[:, pari, :, :]
                    aT2c = attnT2[:, pari, :, :]
                    wait("tensor", "dv", v_seq_st[ci - 1])
                    wait("tensor", "dv", st.get("dv_attnT", 0))
                    wait("tensor", "ac", st["ac_attnT"])
                    MM(pb[4][0:64, 0:64], kaTa, M16[0:64, 1, :], True, (0, 0))
                    MM(pb[5][0:64, 0:64], kaTb, M16[64:128, 1, :], True, (64, 0))
                    MM(pb[4][0:64, 64:128], VB2c[:, 0, :], aT2c[:, 0, :], False, (0, 0))
                    MM(pb[4][64:128, 64:128], VB2c[:, 1, :], aT2c[:, 1, :], True, (0, 64))
                    MM(pb[4][0:64, 256:320], M16[0:64, 0, :], qTa, False, (0, 0))
                    v = MM(pb[5][64:128, 64:128], M16[64:128, 0, :], qTb, True, (64, 64))
                    st["pe_seq1"] = v

                def seq1_dve(ci):
                    cs = ci * C
                    pari = ci % 4
                    st = chst[ci]
                    VB2c = VB2[:, pari, :, :]
                    AtSp = pbW[2][0:64, :].rearrange("p (h c) -> p h c", h=2)[:, :, 0:C]
                    if dbg:
                        wait("vector", "dbg", counts.get("dbg", 0))
                    wait("vector", "pe", st["pe_seq1"])
                    v_yt = dve(lambda e, VB2c=VB2c, AtSp=AtSp: e.tensor_tensor(Yt_sb[:], VB2c, AtSp, ALU.subtract))
                    dve(lambda e: e.tensor_copy(co_f[:, :], pb[4][:, 64:128]))
                    dve(lambda e, cs=cs, hp=hp: e.scalar_tensor_tensor(
                        coT_sb[0:64, hp, cs:cs + C], pb[4][0:64, 256:320], 0.1, co_f[0:64, :], ALU.mult, ALU.add))
                    dve(lambda e, cs=cs, hp=hp: e.scalar_tensor_tensor(
                        coT_sb[64:128, hp, cs:cs + C], pb[5][64:128, 64:128], 0.1, co_f[64:128, :], ALU.mult, ALU.add))
                    st["dv_yt"] = v_yt

                def seq2_mms(ci):
                    pari = ci % 4
                    st = chst[ci]
                    U2a = chNWL[:, pari, 0, 0:C]
                    U2b = chNWL[:, pari, 1, 0:C]
                    wait("tensor", "dv", st["dv_yt"])
                    MM(pb[6][0:64, 0:64], Yt_sb[:, 0, :], U2a, True, (0, 0))
                    MM(pb[6][64:128, 0:64], Yt_sb[:, 1, :], U2b, True, (0, 64))
                    MM(pb[6][0:64, 64:128], U2a, Yt_sb[:, 0, :], False, (0, 0))
                    v = MM(pb[6][64:128, 64:128], U2b, Yt_sb[:, 1, :], False, (0, 64))
                    st["pe_seq2"] = v

                def seq2_dve(ci):
                    st = chst[ci]
                    wait("vector", "pe", st["pe_seq2"])
                    dve(lambda e: e.tensor_tensor(
                        M32[:], M32[:], pb[6][:, 0:128].rearrange("p (o c) -> p o c", o=2), ALU.add))
                    v = dve(lambda e: e.tensor_copy(M16[:], M32[:]))
                    v_seq_st[ci] = v

                pend = None
                for sc in range(NCH // 2):
                    c0, c1 = 2 * sc, 2 * sc + 1
                    gen1_mms(c0)
                    gen1_mms(c1)
                    if NO_WEAVE:
                        gen1_evacs(c0)
                        gen1_evacs(c1)
                        for li, lvl in enumerate((1, 2, 4, 8, 16, 32)):
                            chain_lvl_mms(c0, li, lvl)
                            chain_lvl_mms(c1, li, lvl)
                            chain_lvl_evacs(c0, li, lvl)
                            chain_lvl_evacs(c1, li, lvl)
                        attnT_mms(c0)
                        attnT_mms(c1)
                        attnT_evac(c0)
                        attnT_evac(c1)
                        for p in (c0, c1):
                            seq1_mms(p)
                            seq1_dve(p)
                            seq2_mms(p)
                            seq2_dve(p)
                        continue
                    if pend is None:
                        gen1_evacs(c0)
                        if dbg and hp == 0 and r == 0:
                            wait("sync", "dv", counts.get("dv", 0))
                            stx("dbg", dbg_out["probe16"][0:64, 1664:1728], chNWL[:, 0, 1, C:2 * C])
                        gen1_evacs(c1)
                    else:
                        p0, p1 = pend
                        seq1_mms(p0)
                        seq1_dve(p0)
                        seq2_mms(p0)
                        seq2_dve(p0)
                        gen1_evacs(c0)
                        if dbg and hp == 0 and r == 0 and p0 == 0:
                            wait("sync", "dv", counts.get("dv", 0))
                            wait("sync", "ac", counts.get("ac", 0))
                            stx("dbg", dbg_out["probe16"][0:64, 64:128], chNWL[:, 0, 0, 0:C])
                            stx("dbg", dbg_out["probe16"][0:64, 192:256], VB2[:, 0, 0, :])
                            stx("dbg", dbg_out["probe16"][0:64, 256:320], Yt_sb[:, 0, :])
                            stx("dbg", dbg_out["probe16"][0:64, 320:384], attn2[:, 0, 0, :])
                            stx("dbg", dbg_out["probe16"][0:64, 384:448], attnT2[:, 0, 0, :])
                            stx("dbg", dbg_out["probe32"][0:64, 0:64], M32[0:64, 0, :])
                            stx("dbg", dbg_out["probe32"][0:64, 64:128], M32[0:64, 1, :])
                            stx("dbg", dbg_out["probe16"][0:64, 1728:1792], chNWL[:, 0, 1, 0:C])
                            stx("dbg", dbg_out["probe16"][0:64, 1792:1856], VB2[:, 0, 1, :])
                            stx("dbg", dbg_out["probe16"][0:64, 1856:1920], Yt_sb[:, 1, :])
                            stx("dbg", dbg_out["probe16"][0:64, 1920:1984], attn2[:, 0, 1, :])
                        seq1_mms(p1)
                        seq1_dve(p1)
                        seq2_mms(p1)
                        seq2_dve(p1)
                        gen1_evacs(c1)
                    for li, lvl in enumerate((1, 2, 4, 8, 16, 32)):
                        chain_lvl_mms(c0, li, lvl)
                        chain_lvl_mms(c1, li, lvl)
                        chain_lvl_evacs(c0, li, lvl)
                        chain_lvl_evacs(c1, li, lvl)
                    attnT_mms(c0)
                    attnT_mms(c1)
                    attnT_evac(c0)
                    attnT_evac(c1)
                    pend = (c0, c1)
                for p in (pend or ()):
                    seq1_mms(p)
                    seq1_dve(p)
                    seq2_mms(p)
                    seq2_dve(p)
                if DUMP_QK and hp == 0 and r == 0:
                    wait("sync", "dv", counts.get("dv", 0))
                    wait("sync", "ac", counts.get("ac", 0))
                    stx("dbg", qk_dump[0], qT_sb[:])
                    stx("dbg", qk_dump[1], kTn_sb[:])
                    stx("dbg", qk_dump[2], vT_sb[:])
                    v_qkd = stx("dbg", qk_dump[3], coT_sb[:, 0, :])
                    wait("sync", "dbg", v_qkd)
                    wait("vector", "dbg", v_qkd)
                    wait("scalar", "dbg", v_qkd)
                    wait("tensor", "dbg", v_qkd)
                    wait("gpsimd", "dbg", v_qkd)
                if dbg and hp == 0:
                    wait("sync", "dv", counts.get("dv", 0))
                    stx("dbg", dbg_out["coT0"][r], coT_sb[:, 0, :])

            ckpt(5)
            mark_phase(f'wo_r{r}')
            all_engine_barrier()
            ar.reset(mk_wo)
            wo_sb = ar.al([128, 8, D], BF16)
            v_wo = gdma("wo", wo_sb[:], woT_d.rearrange("(k p) o -> p k o", p=128))
            pe_dv_barrier()
            wait("tensor", "wo", v_wo)
            for m in range(NT):
                for n in range(2):
                    b = pb[6 + ((m * 2 + n) % 2)]
                    pe_dv_barrier()
                    for kt in range(8):
                        pe(lambda e, b=b, m=m, n=n, kt=kt: e.matmul(
                            b[:, :], coT_sb[:, kt, m * 128:(m + 1) * 128],
                            wo_sb[:, kt, n * 512:(n + 1) * 512],
                            start=(kt == 0), stop=(kt == 7)))
                    v = counts.get("pe", 0)
                    wait("vector", "pe", v)
                    dve(lambda e, b=b, m=m, n=n: e.tensor_copy(s_sb[:, m, n * 512:(n + 1) * 512], b[:, :]))
            v_s_done = counts.get("dv", 0)
            if dbg:
                wait("sync", "dv", v_s_done)
                for m in range(NT):
                    stx("dbg", dbg_out["s_attn"][r, m], s_sb[:, m, :])

            ckpt(6)
            mark_phase(f'ffn_rms1_r{r}')
            all_engine_barrier()
            ar.reset(0)
            xselT_sb = ar.al([128, 8, CAP], BF16)
            hNT_sb = ar.al([128, 8, CAP], BF16)
            zT_sb = ar.al([128, 8, CAP], F32)
            h2T_sb = ar.al([128, 8, CAP], BF16)
            oT_sb = ar.al([128, 8, CAP], BF16)
            mk_ffn1 = ar.mark()
            xsel_sb = ar.al([128, NCAP, D], F32)
            fsq_sb = ar.al([128, NCAP, D], F32)
            hnat_sb = ar.al([128, NCAP, D], BF16)

            wait("sync", "sc0", counts.get("sc0", 0))
            wait("sync", "sc1", counts.get("sc1", 0))
            v_xsel = sdma("scm", xsel_sb[:], xsel_dram[r].rearrange("(n p) d -> p n d", p=128))
            if dbg:
                wait("sync", "scm", v_xsel)
                stx("dbg", dbg_out["xsel"][r].rearrange("(n p) d -> p n d", p=128), xsel_sb[:])
            wait("scalar", "scm", v_xsel)
            v = act(lambda e: e.activation(fsq_sb[:], xsel_sb[:], AF.Square))
            wait("vector", "ac", v)
            wait("vector", "scm", v_xsel)
            dve(lambda e: e.tensor_reduce(fst_sb[:, :, 0:1], fsq_sb[:], axis=AX.X, op=ALU.add))
            v = dve(lambda e: e.tensor_scalar(fst_sb[:, :, 1:2], fst_sb[:, :, 0:1], 1.0 / D, 1e-6, ALU.mult, ALU.add))
            wait("scalar", "dv", v)
            v = act(lambda e: e.activation(fst_sb[:, :, 2:3], fst_sb[:, :, 1:2], AF.Ln))
            wait("vector", "ac", v)
            v = dve(lambda e: e.tensor_scalar(fst_sb[:, :, 2:3], fst_sb[:, :, 2:3], -0.5, None, ALU.mult))
            wait("scalar", "dv", v)
            v = act(lambda e: e.activation(fst_sb[:, :, 3:4], fst_sb[:, :, 2:3], AF.Exp))
            wait("vector", "ac", v)
            v_hn = dve(lambda e: e.tensor_tensor(hnat_sb[:], xsel_sb[:],
                                                 fst_sb[:, :, 3:4].to_broadcast([128, NCAP, D]), ALU.mult))
            wait("gpsimd", "dv", v_hn)
            gdma("gh", hnb_dram[r].rearrange("(n p) d -> p n d", p=128), hnat_sb[:])
            v2 = gdma("gh", xselb_dram[r].rearrange("(n p) d -> p n d", p=128), xsel_sb[:])
            wait("scalar", "gh", v2)
            v_hT = 0
            for j in range(8):
                adma("fft", hNT_sb[:, j, :], hnb_dram[r, :, j * 128:(j + 1) * 128], transpose=True)
                v_hT = adma("fft", xselT_sb[:, j, :], xselb_dram[r, :, j * 128:(j + 1) * 128], transpose=True)

            ckpt(7)
            all_engine_barrier()
            ar.reset(mk_ffn1)
            wqkv_sb = ar.al([128, 8, 1536], BF16)
            qkvT_sb = ar.al([128, 24, CAP], BF16)
            fattnT_sb = ar.al([128, NKB, 2, CAP], BF16)

            wait("tensor", "fft", v_hT)
            for half in range(2):
                v_wqkv = gdma("wqv", wqkv_sb[:], wqkvT_d.rearrange("(k p) o -> p k o", p=128)[:, :, half * 1536:(half + 1) * 1536])
                wait("vector", "wqv", v_wqkv)
                wait("vector", "dv", v_g12)
                for kt in range(8):
                    dve(lambda e, kt=kt: e.tensor_tensor(
                        wqkv_sb[:, kt, :], wqkv_sb[:, kt, :],
                        g1c_sb[:, kt:kt + 1].to_broadcast([128, 1536]), ALU.mult))
                if half == 0:
                    dve(lambda e: e.tensor_scalar(wqkv_sb[:, :, 0:D], wqkv_sb[:, :, 0:D], 0.125, None, ALU.mult))
                v_fold = counts.get("dv", 0)
                wait("tensor", "dv", v_fold)
                for oo in range(12):
                    o = half * 12 + oo
                    b = pb[6 + (o % 2)]
                    pe_dv_barrier()
                    for kt in range(8):
                        pe(lambda e, b=b, oo=oo, kt=kt: e.matmul(
                            b[:, 0:CAP], wqkv_sb[:, kt, oo * 128:(oo + 1) * 128],
                            hNT_sb[:, kt, :], start=(kt == 0), stop=(kt == 7)))
                    v = counts.get("pe", 0)
                    wait("vector", "pe", v)
                    dve(lambda e, b=b, o=o: e.tensor_copy(qkvT_sb[:, o, :], b[:, 0:CAP]))
                wait("gpsimd", "pe", counts.get("pe", 0))
            v_qkv = counts.get("dv", 0)
            if dbg:
                wait("sync", "dv", v_qkv)
                stx("dbg", dbg_out["qkvT0"][r][:, 0:CAP], qkvT_sb[:, 0, :])
                stx("dbg", dbg_out["qkvT0"][r][:, CAP:2 * CAP], qkvT_sb[:, 8, :])
                stx("dbg", dbg_out["qkvT0"][r][:, 2 * CAP:3 * CAP], qkvT_sb[:, 16, :])

            ckpt(8)
            mark_phase(f'ffn_attn_r{r}')
            wait("tensor", "dv", v_qkv)
            for hp in range(8):
                pe_dv_barrier()
                for kb in range(NKB):
                    MM(pb[1][0:128, kb * 64:kb * 64 + 64], qkvT_sb[0:64, 16 + hp, kb * 128:(kb + 1) * 128],
                       id_bf[0:64, 0:64], kb == 0, (0, 0))
                    v_mm = MM(pb[3][0:128, kb * 64:kb * 64 + 64], qkvT_sb[64:128, 16 + hp, kb * 128:(kb + 1) * 128],
                              id_bf[64:128, 64:128], kb == 0, (64, 0))
                wait("vector", "pe", v_mm)
                dve(lambda e: e.tensor_copy(vnat_sb[:, :, 0, :],
                                            pb[1][0:128, 0:NKB * 64].rearrange("p (k c) -> p k c", k=NKB)))
                dve(lambda e: e.tensor_copy(vnat_sb[:, :, 1, :],
                                            pb[3][0:128, 0:NKB * 64].rearrange("p (k c) -> p k c", k=NKB)))
                for qs in range(NQS):
                    pe_dv_barrier()
                    MM(pb[0][0:64, 0:CAP], qkvT_sb[0:64, hp, qs * 64:(qs + 1) * 64],
                       qkvT_sb[0:64, 8 + hp, :], True, (0, 0))
                    v_sb_ = MM(pb[2][0:64, 0:CAP], qkvT_sb[64:128, hp, qs * 64:(qs + 1) * 64],
                               qkvT_sb[64:128, 8 + hp, :], True, (64, 0))
                    wait("vector", "pe", v_sb_)
                    moff = 64 * (NQS - 1 - qs)
                    dve(lambda e, qs=qs, moff=moff: e.scalar_tensor_tensor(
                        fS_sb[:, 0, :], pb[0][0:64, 0:CAP], 1.0, fmb_sb[:, moff:moff + CAP], ALU.mult, ALU.add))
                    dve(lambda e, qs=qs, moff=moff: e.scalar_tensor_tensor(
                        fS_sb[:, 1, :], pb[2][0:64, 0:CAP], 1.0, fmb_sb[:, moff:moff + CAP], ALU.mult, ALU.add))
                    wait("scalar", "dv", counts.get("dv", 0))
                    act(lambda e: e.activation(fS_sb[:, 0, :], fS_sb[:, 0, :], AF.Exp,
                                               accum_out=fsum_sb[:, 0:1]))
                    v = act(lambda e: e.activation(fS_sb[:, 1, :], fS_sb[:, 1, :], AF.Exp,
                                                   accum_out=fsum_sb[:, 1:2]))
                    wait("vector", "ac", v)
                    dve(lambda e: e.reciprocal(frec_sb[:], fsum_sb[:]))
                    v_fat = dve(lambda e: e.tensor_tensor(fattn_sb[:], fS_sb[:],
                                                          frec_sb[:].unsqueeze(2).to_broadcast([64, 2, CAP]), ALU.mult))
                    if dbg and hp == 0 and qs == 0 and r == 0:
                        wait("sync", "dv", v_fat)
                        stx("dbg", dbg_out["probe32"][0:64, 128:128 + CAP], fS_sb[:, 0, :])
                        stx("dbg", dbg_out["probe16"][0:64, 448:448 + CAP], fattn_sb[:, 0, :])
                        stx("dbg", dbg_out["probe16"][0:64, 2048:2048 + CAP], fattn_sb[:, 1, :])
                        wait("vector", "dbg", counts["dbg"])
                    wait("tensor", "dv", v_fat)
                    for kb in range(NKB):
                        MM(pb[1][0:128, 256 + kb * 64:256 + kb * 64 + 64],
                           fattn_sb[:, 0, kb * 128:(kb + 1) * 128], id_bf[0:64, 0:64], kb == 0, (0, 0))
                        v_mm = MM(pb[3][0:128, 256 + kb * 64:256 + kb * 64 + 64],
                                  fattn_sb[:, 1, kb * 128:(kb + 1) * 128], id_bf[0:64, 0:64], kb == 0, (0, 0))
                    wait("vector", "pe", v_mm)
                    dve(lambda e, qs=qs: e.tensor_copy(
                        fattnT_sb[:, :, 0, qs * 64:(qs + 1) * 64],
                        pb[1][0:128, 256:256 + NKB * 64].rearrange("p (k c) -> p k c", k=NKB)))
                    dve(lambda e, qs=qs: e.tensor_copy(
                        fattnT_sb[:, :, 1, qs * 64:(qs + 1) * 64],
                        pb[3][0:128, 256:256 + NKB * 64].rearrange("p (k c) -> p k c", k=NKB)))
                v_fT = counts.get("dv", 0)
                wait("tensor", "dv", v_fT)
                for kb in range(NKB):
                    MM(pb[0][0:64, 0:CAP], vnat_sb[:, kb, 0, :], fattnT_sb[:, kb, 0, :], kb == 0, (0, 0))
                    v_mm = MM(pb[4][64:128, 0:CAP], vnat_sb[:, kb, 1, :], fattnT_sb[:, kb, 1, :], kb == 0, (0, 64))
                wait("vector", "pe", v_mm)
                dve(lambda e, hp=hp: e.tensor_copy(oT_sb[0:64, hp, :], pb[0][0:64, 0:CAP]))
                dve(lambda e, hp=hp: e.tensor_copy(oT_sb[64:128, hp, :], pb[4][64:128, 0:CAP]))
                if dbg and hp == 0 and r == 0:
                    wait("sync", "dv", counts["dv"])
                    stx("dbg", dbg_out["probe16"][:, 1024:1024 + CAP], fattnT_sb[:, 0, 0, :])
                    stx("dbg", dbg_out["probe16"][:, 1536:1536 + 64], vnat_sb[:, 0, 0, :])
                    wait("vector", "dbg", counts["dbg"])
            v_oT = counts.get("dv", 0)
            if dbg:
                wait("sync", "dv", v_oT)
                stx("dbg", dbg_out["oT0"][r], oT_sb[:, 0, :])

            ckpt(9)
            mark_phase(f'ffn_wout_r{r}')
            all_engine_barrier()
            ar.reset(mk_ffn1)
            wout_sb = ar.al([128, 8, D], BF16)
            v_wout = gdma("wou", wout_sb[:], woutT_d.rearrange("(k p) o -> p k o", p=128))
            wait("tensor", "dv", v_oT)
            wait("tensor", "wou", v_wout)
            for o in range(8):
                b = pb[6 + (o % 2)]
                pe_dv_barrier()
                for kt in range(8):
                    pe(lambda e, b=b, o=o, kt=kt: e.matmul(
                        b[:, 0:CAP], wout_sb[:, kt, o * 128:(o + 1) * 128],
                        oT_sb[:, kt, :], start=(kt == 0), stop=(kt == 7)))
                v = counts.get("pe", 0)
                wait("vector", "pe", v)
                dve(lambda e, b=b, o=o: e.scalar_tensor_tensor(
                    zT_sb[:, o, :], b[:, 0:CAP], 1.0, xselT_sb[:, o, :], ALU.mult, ALU.add))
            v_z = counts.get("dv", 0)
            if dbg:
                wait("sync", "dv", v_z)
                stx("dbg", dbg_out["zT0"][r], zT_sb[:, 0, :])

            v_mm_prev = [0, 0]
            for kt in range(8):
                zb = zsq_sb[kt % 2]
                wait("scalar", "dv", v_z)
                wait("scalar", "pe", v_mm_prev[kt % 2])
                v = act(lambda e, kt=kt, zb=zb: e.activation(zb[:], zT_sb[:, kt, :], AF.Square))
                wait("tensor", "ac", v)
                v_mm_prev[kt % 2] = pe(lambda e, kt=kt, zb=zb: e.matmul(
                    pb[5][0:1, 0:CAP], ones128_sb[:], zb[:],
                    start=(kt == 0), stop=(kt == 7)))
            v = v_mm_prev[1]
            wait("vector", "pe", v)
            v = dve(lambda e: e.tensor_scalar(ssq2_sb[:], pb[5][0:1, 0:CAP], 1.0 / D, 1e-6, ALU.mult, ALU.add))
            wait("scalar", "dv", v)
            v = act(lambda e: e.activation(ssq2_sb[:], ssq2_sb[:], AF.Ln))
            wait("vector", "ac", v)
            v = dve(lambda e: e.tensor_scalar(ssq2_sb[:], ssq2_sb[:], -0.5, None, ALU.mult))
            wait("scalar", "dv", v)
            v = act(lambda e: e.activation(rstd2_sb[:], ssq2_sb[:], AF.Exp))
            wait("tensor", "ac", v)
            pe_dv_barrier()
            v = pe(lambda e: e.matmul(pb[5][:, 0:CAP], onesrow_sb[0:1, :], rstd2_sb[0:1, :],
                                      start=True, stop=True, skip_group_check=True))
            wait("vector", "pe", v)
            dve(lambda e: e.tensor_copy(rbc2_sb[:], pb[5][:, 0:CAP]))
            for kt in range(8):
                dve(lambda e, kt=kt: e.tensor_tensor(h2T_sb[:, kt, :], zT_sb[:, kt, :], rbc2_sb[:], ALU.mult))
            v_h2 = counts.get("dv", 0)

            ckpt(10)
            mark_phase(f'ffn_mlp_r{r}')
            all_engine_barrier()
            ar.reset(mk_ffn1)
            xpT_sb = ar.al([128, 8, CAP], F32)
            duT_sb = ar.al([128, 8, CAP], F32)
            a1_sb = ar.al([128, CAP], F32)
            mk_mlp = ar.mark()
            gT_sb = ar.al([128, KFF, CAP], BF16)
            w12s = [ar.al([128, 8, 256], BF16) for _ in range(2)]
            w3s = [ar.al([128, KFF, 128], BF16) for _ in range(2)]

            w12_free = {0: 0, 1: 0}
            wait("vector", "dv", v_h2)
            for o in range(KFF):
                bi = o % 2
                wait("gpsimd", "pe", w12_free[bi])
                gdma(f"wf{bi}", w12s[bi][:, :, 0:128], w1T_d.rearrange("(k p) o -> p k o", p=128)[:, :, o * 128:(o + 1) * 128])
                v_w = gdma(f"wf{bi}", w12s[bi][:, :, 128:256], w2T_d.rearrange("(k p) o -> p k o", p=128)[:, :, o * 128:(o + 1) * 128])
                wait("vector", f"wf{bi}", v_w)
                for kt in range(8):
                    dve(lambda e, kt=kt, bi=bi: e.tensor_tensor(
                        w12s[bi][:, kt, :], w12s[bi][:, kt, :],
                        g2c_sb[:, kt:kt + 1].to_broadcast([128, 256]), ALU.mult))
                v_wf = counts.get("dv", 0)
                wait("tensor", "dv", v_wf)
                for kt in range(8):
                    pe(lambda e, o=o, kt=kt, bi=bi: e.matmul(
                        pb[6][:, 0:CAP], w12s[bi][:, kt, 0:128],
                        h2T_sb[:, kt, :], start=(kt == 0), stop=(kt == 7)))
                v_u = counts.get("pe", 0)
                for kt in range(8):
                    pe(lambda e, o=o, kt=kt, bi=bi: e.matmul(
                        pb[7][:, 0:CAP], w12s[bi][:, kt, 128:256],
                        h2T_sb[:, kt, :], start=(kt == 0), stop=(kt == 7)))
                v_v2 = counts.get("pe", 0)
                w12_free[bi] = v_v2
                wait("scalar", "pe", v_u)
                wait("scalar", "dv", counts.get("dv", 0) if o > 0 else 0)
                v_su = act(lambda e: e.activation(su_sb[:], pb[6][:, 0:CAP], AF.Silu))
                wait("vector", "pe", v_v2)
                dve(lambda e: e.tensor_copy(vv_sb[:], pb[7][:, 0:CAP]))
                wait("vector", "ac", v_su)
                v_g = dve(lambda e, o=o: e.tensor_tensor(gT_sb[:, o, :], su_sb[:], vv_sb[:], ALU.mult))
                wait("tensor", "dv", v_g)
                pe_ac_barrier()
            v_gT = counts.get("dv", 0)

            w3_free = {0: 0, 1: 0}
            wait("tensor", "dv", v_gT)
            for o in range(8):
                bi = o % 2
                b = pb[6 + bi]
                wait("gpsimd", "pe", w3_free[bi])
                v_w = gdma(f"w3{bi}", w3s[bi][:], w3T_d.rearrange("(k p) o -> p k o", p=128)[:, :, o * 128:(o + 1) * 128])
                wait("tensor", f"w3{bi}", v_w)
                pe_dv_barrier()
                for kt in range(KFF):
                    pe(lambda e, b=b, o=o, kt=kt, bi=bi: e.matmul(
                        b[:, 0:CAP], w3s[bi][:, kt, :],
                        gT_sb[:, kt, :], start=(kt == 0), stop=(kt == KFF - 1)))
                v = counts.get("pe", 0)
                w3_free[bi] = v
                wait("vector", "pe", v)
                dve(lambda e, b=b, o=o: e.scalar_tensor_tensor(
                    xpT_sb[:, o, :], b[:, 0:CAP], 1.0, zT_sb[:, o, :], ALU.mult, ALU.add))
            v_xp = counts.get("dv", 0)

            ckpt(11)
            wait("scalar", "dv", v_xp)
            v_du_prev = 0
            for o in range(8):
                wait("scalar", "dv", v_du_prev)
                act(lambda e, o=o: e.activation(duT_sb[:, o, :], xselT_sb[:, o, :], AF.Silu))
                v_a = act(lambda e, o=o: e.activation(a1_sb[:], xpT_sb[:, o, :], AF.Silu))
                wait("vector", "ac", v_a)
                v_du_prev = dve(lambda e, o=o: e.tensor_tensor(duT_sb[:, o, :], a1_sb[:], duT_sb[:, o, :], ALU.subtract))
            v_duT = counts.get("dv", 0)

            all_engine_barrier()
            ar.reset(mk_mlp)
            dun_sb = ar.al([128, NCAP, D], F32)
            wait("tensor", "dv", v_duT)
            for o in range(8):
                for qb in range(NCAP):
                    b = pb[6 + ((o * NCAP + qb) % 2)]
                    pe_dv_barrier()
                    v = pe(lambda e, b=b, o=o, qb=qb: e.matmul(
                        b[:, 0:128], duT_sb[:, o, qb * 128:(qb + 1) * 128], id_f32[:, :],
                        start=True, stop=True))
                    wait("vector", "pe", v)
                    dve(lambda e, b=b, o=o, qb=qb: e.tensor_copy(
                        dun_sb[:, qb, o * 128:(o + 1) * 128], b[:, 0:128]))
            v_dun = counts.get("dv", 0)
            wait("sync", "dv", v_dun)
            v_dp = sdma("scm", dp_dram[r].rearrange("(n p) d -> p n d", p=128), dun_sb[:])
            if dbg:
                stx("dbg", dbg_out["dp"][r].rearrange("(n p) d -> p n d", p=128), dun_sb[:])

            ckpt(12)
            mark_phase(f'epilogue_r{r}')
            all_engine_barrier()
            ar.reset(0)
            g_t = [ar.al([128, D], F32) for _ in range(2)]
            xe_t = [ar.al([128, D], F32) for _ in range(2)]
            out_t = [ar.al([128, D], F32) for _ in range(2)]
            ep_free = {0: (0, 0), 1: (0, 0)}
            for j in range(NT):
                bi = j % 2
                dv0, st0 = ep_free[bi]
                wait("sync", "dv", dv0)
                wait("sync", f"ey{bi}", st0)
                v_xnl = sdma(f"eg{bi}", g_t[bi][:], xn_dram[r, j * 128:(j + 1) * 128, :])
                wait("scalar", f"eg{bi}", v_xnl)
                v_si = act(lambda e, bi=bi: e.activation(g_t[bi][:], g_t[bi][:], AF.Silu))
                wait("gpsimd", "ac", v_si)
                wait("gpsimd", "scm", v_dp)
                v_ga = _mk("gpsimd", f"ga{bi}", 16, lambda e, bi=bi, j=j, r=r: e.indirect_dma_start(
                    out=g_t[bi][:],
                    out_offset=None,
                    in_=dp_dram[r],
                    in_offset=IndirectOffsetOnAxis(rki32[:, j:j + 1], 0),
                    bounds_check=bc_cell[0],
                    oob_is_err=False,
                    compute_op=ALU.add))
                v_xl = sdma(f"ex{bi}", xe_t[bi][:], x_in[r, j * 128:(j + 1) * 128, :])
                wait("vector", f"ga{bi}", v_ga)
                wait("vector", f"ex{bi}", v_xl)
                dve(lambda e, bi=bi, j=j: e.tensor_tensor(sg_t[:], s_sb[:, j, :], g_t[bi][:], ALU.mult))
                v_o = dve(lambda e, bi=bi: e.tensor_tensor(out_t[bi][:], xe_t[bi][:], sg_t[:], ALU.add))
                wait("sync", "dv", v_o)
                v_y = stx(f"ey{bi}", y_out[r, j * 128:(j + 1) * 128, :], out_t[bi][:])
                ep_free[bi] = (v_o, v_y)
        except StopEmission:
            pass
        mark_phase('end')
        for _nm in list(counts):
            wait("sync", _nm, counts[_nm])

        block = ctx.enter_context(nc.Block())
        for name in ("sync", "vector", "scalar", "gpsimd", "tensor"):
            fns = ops[name]
            def mk(fns):
                def body(eng):
                    for f in fns:
                        f(eng)
                return body
            getattr(block, name)(mk(fns))

    return nc, dbg_out


def make_consts(T, CAP):
    import ml_dtypes
    C = 64
    NQS = CAP // 64
    ident_bf = np.eye(128).astype(ml_dtypes.bfloat16)
    ident_f32 = np.eye(128, dtype=np.float32)
    onesrow = np.ones((1, 128), np.float32)
    ones128 = np.ones((128, 1), np.float32)
    ones64c = np.zeros((128, 2), np.float32)
    ones64c[0:64, 0] = 1.0
    ones64c[64:128, 1] = 1.0
    ones2T = np.ascontiguousarray(ones64c.T)
    t_i = np.arange(C)
    causU = (t_i[None, :] > t_i[:, None]).astype(np.float32)
    causL = (t_i[None, :] < t_i[:, None]).astype(np.float32)
    dmask = np.where(t_i[None, :] > t_i[:, None], NEG, 0.0).astype(np.float32)
    p_i = np.arange(64)[:, None]
    j_i = np.arange(CAP + 64 * NQS)[None, :]
    fmbase = np.where(j_i - 64 * (NQS - 1) <= p_i, 0.0, NEG).astype(np.float32)
    return dict(ident_bf=ident_bf, ident_f32=ident_f32, onesrow=onesrow,
                ones128=ones128, ones64c=ones64c, ones2T=ones2T,
                causU=causU, causL=causL, dmask=dmask, fmbase=fmbase)


def make_inputs_map(cfg, xshard, Wq, Wk, Wv, Wo, alpha_raw, beta_raw, w_router,
                    W_qkv, W_out, g1, g2, W1, W2, W3):
    T, R, DFFP = cfg["T"], cfg["R"], cfg["DFFP"]
    CAP = T // 4
    D = 1024
    tr = lambda w: np.ascontiguousarray(np.asarray(w, np.float32).T)
    DFF = W1.shape[0]
    W1p = np.zeros((DFFP, D), np.float32); W1p[:DFF] = W1
    W2p = np.zeros((DFFP, D), np.float32); W2p[:DFF] = W2
    W3p = np.zeros((D, DFFP), np.float32); W3p[:, :DFF] = W3
    g12 = np.concatenate([np.asarray(g1, np.float32).reshape(8, 128),
                          np.asarray(g2, np.float32).reshape(8, 128)], axis=0)
    ab = np.concatenate([np.asarray(alpha_raw).reshape(-1),
                         np.asarray(beta_raw).reshape(-1)]).reshape(1, 32).astype(np.float32)
    m = dict(x=np.ascontiguousarray(np.asarray(xshard, np.float32)),
             wqT=tr(Wq), wkT=tr(Wk), wvT=tr(Wv), woT=tr(Wo),
             ab_raw=ab, wrouter=np.asarray(w_router, np.float32).reshape(1, D),
             wqkvT=tr(W_qkv), woutT=tr(W_out), g12=g12,
             w1T=tr(W1p), w2T=tr(W2p), w3T=np.ascontiguousarray(W3p.T))
    m.update(make_consts(T, CAP))
    return m


_CFG = dict(T=2048, R=2, DFFP=2816, debug=False, strict=True, no_act_evac=True)
_N_CORES = 8
_BUILT = None


def _get_program():
    global _BUILT
    if _BUILT is None:
        _BUILT = build_program(_CFG)
    return _BUILT


def kernel(L0_attn_x, L0_attn_Wq, L0_attn_Wk, L0_attn_Wv, L0_attn_Wo,
           L0_attn_alpha_raw, L0_attn_beta_raw, L0_ffn_w_router, L0_ffn_W_qkv,
           L0_ffn_W_out, L0_ffn_g1, L0_ffn_g2, L0_ffn_W1, L0_ffn_W2, L0_ffn_W3):
    x = np.asarray(L0_attn_x, np.float32)
    nc, _ = _get_program()
    base = make_inputs_map(_CFG, x[0:2], L0_attn_Wq, L0_attn_Wk, L0_attn_Wv,
                           L0_attn_Wo, L0_attn_alpha_raw, L0_attn_beta_raw,
                           L0_ffn_w_router, L0_ffn_W_qkv, L0_ffn_W_out,
                           L0_ffn_g1, L0_ffn_g2, L0_ffn_W1, L0_ffn_W2, L0_ffn_W3)
    in_maps = []
    for c in range(_N_CORES):
        m = dict(base)
        m["x"] = np.ascontiguousarray(x[2 * c:2 * c + 2])
        in_maps.append(m)
    res = run_bass_kernel_spmd(nc, in_maps, core_ids=list(range(_N_CORES)))
    return np.concatenate([np.asarray(r["y"], np.float32) for r in res.results], axis=0)


# version 18 (final)
# speedup vs baseline: 1.0139x; 1.0139x over previous
from contextlib import ExitStack
import numpy as np
import concourse.bass as bass
import concourse.mybir as mybir
from concourse.bass_utils import run_bass_kernel_spmd
from concourse.bass_core_rust import IndirectOffsetOnAxis

F32 = mybir.dt.float32
BF16 = mybir.dt.bfloat16
I32 = mybir.dt.int32
U8 = mybir.dt.uint8
AF = mybir.ActivationFunctionType
ALU = mybir.AluOpType
AX = mybir.AxisListType

NEG = -1e30
DTSZ = {F32: 4, BF16: 2, I32: 4}


class Arena:
    def __init__(self, ap, nbytes):
        self.ap = ap
        self.nbytes = nbytes
        self.off = 0

    def mark(self):
        return self.off

    def reset(self, mark=0):
        self.off = mark

    def al(self, shape, dtype):
        sz = DTSZ[dtype]
        free = 1
        for d in shape[1:]:
            free *= d
        nb = free * sz
        off = (self.off + 127) & ~127
        assert off + nb <= self.nbytes, f"arena overflow: need {nb} at {off}, cap {self.nbytes}"
        self.off = off + nb
        v = self.ap[:, off:off + nb].bitcast(dtype)
        if len(shape) == 3:
            v = v.rearrange("p (a b) -> p a b", a=shape[1])
        elif len(shape) == 4:
            v = v.rearrange("p (a b c) -> p a b c", a=shape[1], b=shape[2])
        if shape[0] < 128:
            v = v[0:shape[0]]
        return v


def build_program(cfg):
    T = cfg["T"]
    R = cfg["R"]
    D = 1024
    H = 16
    DH = 64
    C = 64
    NCH = T // C
    NT = T // 128
    CAP = T // 4
    NCAP = CAP // 128
    DFFP = cfg["DFFP"]
    KFF = DFFP // 128
    NQS = CAP // 64
    NKB = CAP // 128
    dbg = cfg.get("debug", False)
    strict = cfg.get("strict", False)
    PH = cfg.get("phases", 99)
    NO_WEAVE = cfg.get("no_weave", False)
    NO_ACT_EVAC = cfg.get("no_act_evac", False)
    NO_ACT_CHAIN = cfg.get("no_act_chain", False) or NO_ACT_EVAC
    NO_ACT_ATTNT = cfg.get("no_act_attnT", False) or NO_ACT_EVAC
    NO_4D = cfg.get("no_4d", False)
    DMA_CHAIN = cfg.get("dma_chain", False)
    assert NCAP >= 1 and NKB >= 1 and T % 512 == 0

    class StopEmission(Exception):
        pass

    nc = bass.Bass("TRN2", target_bir_lowering=False,
                   detect_race_conditions=strict)

    x_in = nc.dram_tensor("x", [R, T, D], F32, kind="ExternalInput").ap()
    wqT_d = nc.dram_tensor("wqT", [D, D], F32, kind="ExternalInput").ap()
    wkT_d = nc.dram_tensor("wkT", [D, D], F32, kind="ExternalInput").ap()
    wvT_d = nc.dram_tensor("wvT", [D, D], F32, kind="ExternalInput").ap()
    woT_d = nc.dram_tensor("woT", [D, D], F32, kind="ExternalInput").ap()
    ab_raw = nc.dram_tensor("ab_raw", [1, 2 * H], F32, kind="ExternalInput").ap()
    wrouter_d = nc.dram_tensor("wrouter", [1, D], F32, kind="ExternalInput").ap()
    wqkvT_d = nc.dram_tensor("wqkvT", [D, 3 * D], F32, kind="ExternalInput").ap()
    woutT_d = nc.dram_tensor("woutT", [D, D], F32, kind="ExternalInput").ap()
    g12_d = nc.dram_tensor("g12", [16, 128], F32, kind="ExternalInput").ap()
    w1T_d = nc.dram_tensor("w1T", [D, DFFP], F32, kind="ExternalInput").ap()
    w2T_d = nc.dram_tensor("w2T", [D, DFFP], F32, kind="ExternalInput").ap()
    w3T_d = nc.dram_tensor("w3T", [DFFP, D], F32, kind="ExternalInput").ap()
    ident_bf_d = nc.dram_tensor("ident_bf", [128, 128], BF16, kind="ExternalInput").ap()
    ident_f32_d = nc.dram_tensor("ident_f32", [128, 128], F32, kind="ExternalInput").ap()
    onesrow_d = nc.dram_tensor("onesrow", [1, 128], F32, kind="ExternalInput").ap()
    ones128_d = nc.dram_tensor("ones128", [128, 1], F32, kind="ExternalInput").ap()
    ones64c_d = nc.dram_tensor("ones64c", [128, 2], F32, kind="ExternalInput").ap()
    ones2T_d = nc.dram_tensor("ones2T", [2, 128], F32, kind="ExternalInput").ap()
    causU_d = nc.dram_tensor("causU", [64, C], F32, kind="ExternalInput").ap()
    causL_d = nc.dram_tensor("causL", [64, C], F32, kind="ExternalInput").ap()
    dmask_d = nc.dram_tensor("dmask", [64, C], F32, kind="ExternalInput").ap()
    fmb_d = nc.dram_tensor("fmbase", [64, CAP + 64 * NQS], F32, kind="ExternalInput").ap()
    y_out = nc.dram_tensor("y", [R, T, D], F32, kind="ExternalOutput").ap()
    DUMP_QK = cfg.get("dump_qk", False)
    if DUMP_QK:
        qk_dump = nc.dram_tensor("qk_dump", [4, 128, T], BF16, kind="ExternalOutput").ap()

    xn_dram = nc.dram_tensor("xn_dram", [R, T, D], F32, kind="Internal").ap()
    xnb_dram = nc.dram_tensor("xnb_dram", [R, T, D], BF16, kind="Internal").ap()
    sc_dram = nc.dram_tensor("sc_dram", [R, NT, 128], F32, kind="Internal").ap()
    xsel_dram = [nc.dram_tensor(f"xsel_dram{i}", [CAP, D], F32, kind="Internal").ap() for i in range(R)]
    xselb_dram = nc.dram_tensor("xselb_dram", [R, CAP, D], BF16, kind="Internal").ap()
    hnb_dram = nc.dram_tensor("hnb_dram", [R, CAP, D], BF16, kind="Internal").ap()
    dp_dram = [nc.dram_tensor(f"dp_dram{i}", [CAP, D], F32, kind="Internal").ap() for i in range(R)]

    dbg_out = {}
    if dbg:
        def dbg_t(name, shape, dt=F32):
            dbg_out[name] = nc.dram_tensor("dbg_" + name, shape, dt, kind="ExternalOutput").ap()
        dbg_t("xn", [R, T, D])
        dbg_t("ranks", [R, 128, NT])
        dbg_t("xsel", [R, CAP, D])
        dbg_t("qT0", [R, 128, T], BF16)
        dbg_t("kTn0", [R, 128, T], BF16)
        dbg_t("coT0", [R, 128, T], BF16)
        dbg_t("s_attn", [R, NT, 128, D], BF16)
        dbg_t("qkvT0", [R, 128, 3 * CAP], BF16)
        dbg_t("oT0", [R, 128, CAP], BF16)
        dbg_t("zT0", [R, 128, CAP])
        dbg_t("dp", [R, CAP, D])
        dbg_t("probe16", [128, 8192], BF16)
        dbg_t("probe32", [128, 8192])

    ctx = ExitStack()
    sbf = lambda n, s, d=F32: ctx.enter_context(nc.sbuf_tensor(n, s, d))

    with ctx:
        id_bf = sbf("id_bf", [128, 128], BF16)
        id_f32 = sbf("id_f32", [128, 128], F32)
        onesrow_sb = sbf("onesrow_sb", [1, 128])
        ones128_sb = sbf("ones128_sb", [128, 1])
        ones64c_sb = sbf("ones64c_sb", [128, 2])
        ones2T_sb = sbf("ones2T_sb", [2, 128])
        causU_sb = sbf("causU_sb", [64, C])
        causL_sb = sbf("causL_sb", [64, C])
        dmask_sb = sbf("dmask_sb", [64, C])
        fmb_sb = sbf("fmb_sb", [64, CAP + 64 * NQS])
        wr_row = sbf("wr_row", [1, D])
        wrb_sb = sbf("wrb_sb", [128, D])
        ab_sb = sbf("ab_sb", [1, 2 * H])
        abH_sb = sbf("abH_sb", [128, 2 * H])
        mUL_sb = sbf("mUL_sb", [64, H, 2, C], BF16)
        g1c_sb = sbf("g1c_sb", [128, 8])
        g2c_sb = sbf("g2c_sb", [128, 8])
        gr_sb = sbf("gr_sb", [16, 128])
        stat = sbf("stat", [128, 8])
        sccol = sbf("sccol", [128, NT])
        rkcol = sbf("rkcol", [128, NT])
        rki32 = sbf("rki32", [128, NT], I32)
        scT_sb = sbf("scT_sb", [16, 128])
        Yt_sb = sbf("Yt_sb", [64, 2, C], BF16)
        M32 = sbf("M32", [128, 2, C])
        M16 = sbf("M16", [128, 2, C], BF16)
        co_f = sbf("co_f", [128, C])
        sm_sb = sbf("sm_sb", [64, 2, C])
        ssum = sbf("ssum", [64, 2])
        srec = sbf("srec", [64, 2])
        chNWL = sbf("chNWL", [64, 4, 2, 3 * C], BF16)
        VB2 = sbf("VB2", [64, 4, 2, C], BF16)
        attn2 = sbf("attn2", [64, 4, 2, C], BF16)
        attnT2 = sbf("attnT2", [64, 4, 2, C], BF16)
        fS_sb = sbf("fS_sb", [64, 2, CAP])
        fsum_sb = sbf("fsum_sb", [64, 2])
        frec_sb = sbf("frec_sb", [64, 2])
        fattn_sb = sbf("fattn_sb", [64, 2, CAP], BF16)
        vnat_sb = sbf("vnat_sb", [128, NKB, 2, DH], BF16)
        fst_sb = sbf("fst_sb", [128, NCAP, 4])
        ssq2_sb = sbf("ssq2_sb", [1, CAP])
        rstd2_sb = sbf("rstd2_sb", [1, CAP])
        rbc2_sb = sbf("rbc2_sb", [128, CAP])
        su_sb = sbf("su_sb", [128, CAP], BF16)
        vv_sb = sbf("vv_sb", [128, CAP], BF16)
        zsq_sb = [sbf(f"zsq_sb{i}", [128, CAP]) for i in range(2)]
        sg_t = sbf("sg_t", [128, D])
        dbgt = sbf("dbgt", [64, 640]) if dbg else None
        s_sb = sbf("s_sb", [128, NT, D], BF16)

        ARENA_KB = cfg.get("arena_kb", 123)
        AR = sbf("AR", [128, ARENA_KB * 1024], U8)
        ar = Arena(AR[:], ARENA_KB * 1024)

        pbW = [ctx.enter_context(nc.psum_tensor(f"pbW{i}", [128, 1024], F32)) for i in range(4)]
        pb = [pbW[i // 2][:, (i % 2) * 512:(i % 2) * 512 + 512] for i in range(8)]

        sems = {}
        counts = {}
        ops = {e: [] for e in ("sync", "vector", "scalar", "gpsimd", "tensor")}
        lastw = {e: {} for e in ops}

        def _sem(name):
            if name not in sems:
                sems[name] = ctx.enter_context(nc.semaphore("s_" + name))
                counts[name] = 0
            return sems[name]

        def emit(engine, fn):
            ops[engine].append(fn)

        def wait(engine, sem_name, val):
            if val is None or val <= 0:
                return
            s = _sem(sem_name)
            if lastw[engine].get(sem_name, 0) >= val:
                return
            lastw[engine][sem_name] = val
            emit(engine, lambda e, s=s, v=val: e.wait_ge(s, v))

        def _mk(engine, sem_name, by, fn):
            s = _sem(sem_name)
            counts[sem_name] += by
            emit(engine, lambda e, f=fn, s=s, b=by: f(e).then_inc(s, b))
            return counts[sem_name]

        def sdma(st, out, in_, **kw):
            return _mk("sync", st, 16, lambda e: e.dma_start(out=out, in_=in_, **kw))

        def adma(st, out, in_, **kw):
            return _mk("scalar", st, 16, lambda e: e.dma_start(out=out, in_=in_, **kw))

        def gdma(st, out, in_, **kw):
            return _mk("gpsimd", st, 16, lambda e: e.dma_start(out=out, in_=in_, **kw))

        def pe(fn):
            return _mk("tensor", "pe", 1, fn)

        def dve(fn):
            if strict:
                wait("vector", "dv", counts.get("dv", 0))
            return _mk("vector", "dv", 1, fn)

        def act(fn):
            if strict:
                wait("scalar", "ac", counts.get("ac", 0))
            return _mk("scalar", "ac", 1, fn)

        def stx(st, out, in_):
            return _mk("sync", st, 16, lambda e: e.dma_start(out=out, in_=in_))

        def MM(bank_ap, lhsT, rhs, start, tile=None):
            return pe(lambda e: e.matmul(bank_ap, lhsT, rhs, start=start, stop=True,
                                         skip_group_check=not start,
                                         tile_position=tile))

        def pe_dv_barrier():
            wait("tensor", "dv", counts.get("dv", 0))

        def pe_ac_barrier():
            wait("tensor", "ac", counts.get("ac", 0))

        phase_marks = []
        dbg_out["phase_marks"] = phase_marks

        def mark_phase(name):
            phase_marks.append((name, {e: len(ops[e]) for e in ops}))

        def all_engine_barrier():
            snap = dict(counts)
            for e in ops:
                for name, val in snap.items():
                    wait(e, name, val)

        bc_cell = [None]

        def _init_bc(e):
            bc_cell[0] = e.alloc_register("bc_reg")
            return e.reg_mov(bc_cell[0], CAP - 1)

        emit("gpsimd", lambda e: _init_bc(e))

        sdma("cs", id_bf[:], ident_bf_d[:, :])
        sdma("cs", id_f32[:], ident_f32_d[:, :])
        sdma("cs", onesrow_sb[:], onesrow_d[:, :])
        sdma("cs", ones128_sb[:], ones128_d[:, :])
        sdma("cs", ones64c_sb[:], ones64c_d[:, :])
        sdma("cs", ones2T_sb[:], ones2T_d[:, :])
        sdma("cs", causU_sb[:], causU_d[:, :])
        sdma("cs", causL_sb[:], causL_d[:, :])
        sdma("cs", dmask_sb[:], dmask_d[:, :])
        sdma("cs", fmb_sb[:], fmb_d[:, :])
        sdma("cs", gr_sb[:], g12_d[:, :])
        sdma("cs", wr_row[:], wrouter_d[:, :])
        v_c = sdma("cs", ab_sb[:], ab_raw[:, :])

        wait("tensor", "cs", v_c)
        for h in range(D // 512):
            b = pb[6 + (h % 2)]
            v = pe(lambda e, h=h, b=b: e.matmul(b[:, :], onesrow_sb[0:1, :],
                                                wr_row[0:1, h * 512:(h + 1) * 512],
                                                start=True, stop=True))
            wait("vector", "pe", v)
            dve(lambda e, h=h, b=b: e.tensor_copy(wrb_sb[:, h * 512:(h + 1) * 512], b[:, :]))
        pe_dv_barrier()
        v = pe(lambda e: e.matmul(pb[6][0:128, 0:16], gr_sb[:], id_f32[0:16, 0:16],
                                  start=True, stop=True))
        wait("vector", "pe", v)
        dve(lambda e: e.tensor_copy(g1c_sb[:], pb[6][:, 0:8]))
        v_g12 = dve(lambda e: e.tensor_copy(g2c_sb[:], pb[6][:, 8:16]))

        wait("scalar", "cs", v_c)
        v = act(lambda e: e.activation(ab_sb[:], ab_sb[:], AF.Sigmoid))
        wait("vector", "ac", v)
        dve(lambda e: e.tensor_scalar(ab_sb[:, 0:H], ab_sb[:, 0:H], 0.5, None, ALU.mult))
        v_ab = dve(lambda e: e.tensor_scalar(ab_sb[:, H:2 * H], ab_sb[:, H:2 * H], 2.0, None, ALU.mult))
        wait("tensor", "dv", v_ab)
        pe_dv_barrier()
        v = pe(lambda e: e.matmul(pb[6][0:128, 0:32], onesrow_sb[0:1, 0:128],
                                  ab_sb[0:1, 0:32], start=True, stop=True))
        wait("vector", "pe", v)
        dve(lambda e: e.tensor_copy(abH_sb[:], pb[6][0:128, 0:32]))
        for h in range(H):
            dve(lambda e, h=h: e.tensor_tensor(
                mUL_sb[:, h, 0, :], causU_sb[:], abH_sb[0:64, h:h + 1].to_broadcast([64, C]), ALU.mult))
            dve(lambda e, h=h: e.tensor_tensor(
                mUL_sb[:, h, 1, :], causL_sb[:], abH_sb[0:64, h:h + 1].to_broadcast([64, C]), ALU.mult))
        v_consts_dv = counts.get("dv", 0)

        def ckpt(n):
            if PH < n:
                raise StopEmission()
        try:
          for r in range(R):
            ckpt(1)
            mark_phase(f'ln_r{r}')
            ar.reset(0)
            coT_sb = ar.al([128, 8, T], BF16)
            mk_wo = ar.mark()
            qT_sb = ar.al([128, T], BF16)
            kTn_sb = ar.al([128, T], BF16)
            vT_sb = ar.al([128, T], BF16)
            kraw_sb = kTn_sb
            wsl = [[ar.al([128, 8, 128], BF16) for _ in range(2)] for _ in range(3)]
            nrm_sb = ar.al([2, T], F32)
            rn_sb = ar.al([2, T], F32)
            ln_bytes = (2 * 4 * D + 2 * 4 * D + 2 * 2 * D + 4 * D
                        + 4 * max(T, D) + 4 * T + 4 * T + 1024)
            if ar.off < ln_bytes:
                ar.reset(ln_bytes)
            xnT = ar.al([128, 8, T], BF16)
            mk_kn = ar.mark()
            ksq_sb = ar.al([128, T], F32)
            rbc_sb = ar.al([128, T], F32)
            ar.reset(mk_kn)
            kaT_sb = ar.al([128, T], BF16)
            ar.reset(0)
            x_t = [ar.al([128, D], F32) for _ in range(2)]
            xn_t = [ar.al([128, D], F32) for _ in range(2)]
            xnb_t = [ar.al([128, D], BF16) for _ in range(2)]
            xsq_t = ar.al([128, D], F32)
            trash = ar.al([128, max(T, D)], F32)
            sbcast = ar.al([128, T], F32)
            srow = ar.al([1, T], F32)

            all_engine_barrier()
            v_sqred_prev = 0
            v_xn_st = {}
            xt_free = {0: (0, 0), 1: (0, 0)}
            xn_free = {0: (0, 0, 0), 1: (0, 0, 0)}
            for j in range(NT):
                bi = j % 2
                dv0, ac0 = xt_free[bi]
                wait("sync", "dv", dv0)
                wait("sync", "ac", ac0)
                v_x = sdma(f"x{bi}", x_t[bi][:], x_in[r, j * 128:(j + 1) * 128, :])
                wait("scalar", f"x{bi}", v_x)
                wait("scalar", "dv", v_sqred_prev)
                v_sq = act(lambda e, bi=bi: e.activation(xsq_t[:], x_t[bi][:], AF.Square,
                                                         accum_out=stat[:, 1:2]))
                wait("vector", f"x{bi}", v_x)
                ld0, gp0, db0 = xn_free[bi]
                wait("vector", f"xs{bi}", ld0)
                wait("vector", f"xb{bi}", gp0)
                wait("vector", f"db{bi}", db0)
                dve(lambda e, bi=bi: e.reduce_sum(stat[:, 0:1], x_t[bi][:], axis=AX.X))
                wait("vector", "ac", v_sq)
                dve(lambda e: e.tensor_scalar(stat[:, 2:3], stat[:, 0:1], 1.0 / D, None, ALU.mult))
                v_sqred_prev = dve(lambda e: e.tensor_scalar(stat[:, 3:4], stat[:, 1:2], 1.0 / D, None, ALU.mult))
                dve(lambda e: e.tensor_tensor(stat[:, 4:5], stat[:, 2:3], stat[:, 2:3], ALU.mult))
                dve(lambda e: e.tensor_tensor(stat[:, 5:6], stat[:, 3:4], stat[:, 4:5], ALU.subtract))
                v_var = dve(lambda e: e.tensor_scalar(stat[:, 5:6], stat[:, 5:6], 1e-5, None, ALU.add))
                wait("scalar", "dv", v_var)
                v_std = act(lambda e: e.activation(stat[:, 6:7], stat[:, 5:6], AF.Sqrt))
                wait("vector", "ac", v_std)
                dve(lambda e: e.reciprocal(stat[:, 7:8], stat[:, 6:7]))
                v_xn = dve(lambda e, bi=bi: e.tensor_scalar(
                    xn_t[bi][:], x_t[bi][:], stat[:, 2:3], stat[:, 7:8], ALU.subtract, ALU.mult))
                dve(lambda e, bi=bi: e.tensor_tensor(trash[:, 0:D], xn_t[bi][:], wrb_sb[:], ALU.mult))
                dve(lambda e, j=j: e.reduce_sum(sccol[:, j:j + 1], trash[:, 0:D], axis=AX.X))
                v_cast = dve(lambda e, bi=bi: e.tensor_copy(xnb_t[bi][:], xn_t[bi][:]))
                wait("sync", "dv", v_xn)
                v_xn_st[j] = sdma(f"xs{bi}", xn_dram[r, j * 128:(j + 1) * 128, :], xn_t[bi][:])
                wait("gpsimd", "dv", v_cast)
                v_xnb = gdma(f"xb{bi}", xnb_dram[r, j * 128:(j + 1) * 128, :], xnb_t[bi][:])
                v_dbg = 0
                if dbg:
                    v_dbg = stx(f"db{bi}", dbg_out["xn"][r, j * 128:(j + 1) * 128, :], xn_t[bi][:])
                xt_free[bi] = (v_cast, v_sq)
                xn_free[bi] = (v_xn_st[j], v_xnb, v_dbg)
            v_scores = counts.get("dv", 0)

            wait("scalar", "xb0", counts.get("xb0", 0))
            wait("scalar", "xb1", counts.get("xb1", 0))
            v_xnT = 0
            for j in range(8):
                v_xnT = adma("xnt", xnT[:, j, :], xnb_dram[r, :, j * 128:(j + 1) * 128], transpose=True)

            ckpt(2)
            mark_phase(f'ranks_r{r}')
            wait("tensor", "dv", v_scores)
            v = pe(lambda e: e.matmul(pb[6][0:NT, 0:128], sccol[:], id_f32[:, 0:128],
                                      start=True, stop=True))
            wait("vector", "pe", v)
            v = dve(lambda e: e.tensor_copy(scT_sb[0:NT, :], pb[6][0:NT, 0:128]))
            wait("sync", "dv", v)
            v = sdma("scm", sc_dram[r], scT_sb[0:NT, :])
            wait("sync", "scm", v)
            v_srow = sdma("scm", srow[:, :], sc_dram[r].rearrange("j p -> (j p)").unsqueeze(0))
            wait("tensor", "scm", v_srow)
            for h in range(T // 512):
                b = pb[6 + (h % 2)]
                pe_dv_barrier()
                v = pe(lambda e, h=h, b=b: e.matmul(b[:, :], onesrow_sb[0:1, :],
                                                    srow[0:1, h * 512:(h + 1) * 512],
                                                    start=True, stop=True))
                wait("vector", "pe", v)
                dve(lambda e, h=h, b=b: e.tensor_copy(sbcast[:, h * 512:(h + 1) * 512], b[:, :]))
            wait("vector", "dv", counts.get("dv", 0))
            for j in range(NT):
                dve(lambda e, j=j: e.tensor_scalar(
                    trash[:, 0:T], sbcast[:, 0:T], sccol[:, j:j + 1], 0.0,
                    ALU.is_gt, ALU.add, accum_out=rkcol[:, j:j + 1]))
            v_rk = dve(lambda e: e.tensor_copy(rki32[:], rkcol[:]))
            if dbg:
                wait("sync", "dv", v_rk)
                stx("dbg", dbg_out["ranks"][r], rkcol[:])
            wait("gpsimd", "dv", v_rk)
            wait("gpsimd", "xs0", counts.get("xs0", 0))
            wait("gpsimd", "xs1", counts.get("xs1", 0))
            wait("sync", "xs0", counts.get("xs0", 0))
            wait("sync", "xs1", counts.get("xs1", 0))
            wait("sync", "xb0", counts.get("xb0", 0))
            wait("sync", "xb1", counts.get("xb1", 0))
            wait("sync", "db0", counts.get("db0", 0))
            wait("sync", "db1", counts.get("db1", 0))
            v_scat = {0: 0, 1: 0}
            for j in range(NT):
                bi = j % 2
                wait("sync", f"sc{bi}", v_scat[bi])
                v_stage = sdma(f"x{bi}", xn_t[bi][:], xn_dram[r, j * 128:(j + 1) * 128, :])
                wait("gpsimd", f"x{bi}", v_stage)
                v_scat[bi] = _mk("gpsimd", f"sc{bi}", 16, lambda e, j=j, bi=bi, r=r: e.indirect_dma_start(
                    out=xsel_dram[r],
                    out_offset=IndirectOffsetOnAxis(rki32[:, j:j + 1], 0),
                    in_=xn_t[bi][:],
                    in_offset=None,
                    bounds_check=bc_cell[0],
                    oob_is_err=False))

            ckpt(3)
            mark_phase(f'dn_setup_r{r}')
            all_engine_barrier()
            ckpt(3.05)

            wait("tensor", "xnt", v_xnT)
            wait("tensor", "dv", v_consts_dv)
            wsl_free = {}
            for hp in range(8):
                bi = hp % 2
                if dbg:
                    wait("vector", "dbg", counts.get("dbg", 0))
                wait("gpsimd", "pe", wsl_free.get(bi, 0))
                gdma(f"wa{bi}", wsl[0][bi][:], wqT_d.rearrange("(k p) o -> p k o", p=128)[:, :, hp * 128:(hp + 1) * 128])
                gdma(f"wa{bi}", wsl[1][bi][:], wkT_d.rearrange("(k p) o -> p k o", p=128)[:, :, hp * 128:(hp + 1) * 128])
                v_w = gdma(f"wa{bi}", wsl[2][bi][:], wvT_d.rearrange("(k p) o -> p k o", p=128)[:, :, hp * 128:(hp + 1) * 128])
                wait("tensor", f"wa{bi}", v_w)
                ckpt(3.3)

                def proj_pass(wi, evac):
                    for tt in range(T // 512):
                        b = pb[6 + (tt % 2)]
                        pe_dv_barrier()
                        pe_ac_barrier()
                        ckpt(3.31)
                        for kt in range(8):
                            pe(lambda e, b=b, kt=kt, tt=tt, wi=wi, bi=bi: e.matmul(
                                b[:, :], wsl[wi][bi][:, kt, :],
                                xnT[:, kt, tt * 512:(tt + 1) * 512],
                                start=(kt == 0), stop=(kt == 7)))
                            ckpt(3.32 + kt / 100.0)
                        ckpt(3.33)
                        evac(b, tt, counts.get("pe", 0))
                        ckpt(3.35)

                def ev_q(b, tt, v):
                    wait("vector", "pe", v)
                    dve(lambda e, b=b, tt=tt: e.tensor_copy(qT_sb[:, tt * 512:(tt + 1) * 512], b[:, :]))

                def ev_k(b, tt, v):
                    wait("vector", "pe", v)
                    vc = dve(lambda e, b=b, tt=tt: e.tensor_copy(kraw_sb[:, tt * 512:(tt + 1) * 512], b[:, :]))
                    wait("scalar", "dv", vc)
                    act(lambda e, b=b, tt=tt: e.activation(ksq_sb[:, tt * 512:(tt + 1) * 512], b[:, :], AF.Square))

                def ev_v(b, tt, v):
                    wait("vector", "pe", v)
                    dve(lambda e, b=b, tt=tt: e.tensor_copy(vT_sb[:, tt * 512:(tt + 1) * 512], b[:, :]))

                proj_pass(0, ev_q)
                ckpt(3.4)
                proj_pass(1, ev_k)
                proj_pass(2, ev_v)
                wsl_free[bi] = counts.get("pe", 0)
                ckpt(3.5)

                pe_ac_barrier()
                pe_dv_barrier()
                for tt in range(T // 512):
                    v = pe(lambda e, tt=tt: e.matmul(
                        pb[5][0:2, 0:512], ones64c_sb[:], ksq_sb[:, tt * 512:(tt + 1) * 512],
                        start=True, stop=True, skip_group_check=True))
                    wait("vector", "pe", v)
                    v_ev = dve(lambda e, tt=tt: e.tensor_copy(nrm_sb[:, tt * 512:(tt + 1) * 512], pb[5][0:2, 0:512]))
                    wait("tensor", "dv", v_ev)
                wait("scalar", "dv", counts.get("dv", 0))
                v = act(lambda e: e.activation(nrm_sb[:], nrm_sb[:], AF.Sqrt))
                wait("vector", "ac", v)
                dve(lambda e: e.tensor_scalar(nrm_sb[:], nrm_sb[:], 1e-12, None, ALU.max))
                v_rn = dve(lambda e: e.reciprocal(rn_sb[:], nrm_sb[:]))
                wait("tensor", "dv", v_rn)
                for tt in range(T // 512):
                    b = pb[6 + (tt % 2)]
                    pe_dv_barrier()
                    v = pe(lambda e, b=b, tt=tt: e.matmul(
                        b[:, :], ones2T_sb[:], rn_sb[:, tt * 512:(tt + 1) * 512],
                        start=True, stop=True))
                    wait("vector", "pe", v)
                    dve(lambda e, b=b, tt=tt: e.tensor_copy(rbc_sb[:, tt * 512:(tt + 1) * 512], b[:, :]))
                dve(lambda e: e.tensor_tensor(kTn_sb[:], kTn_sb[:], rbc_sb[:], ALU.mult))
                dve(lambda e, hp=hp: e.tensor_tensor(
                    kaT_sb[0:64, :], kTn_sb[0:64, :],
                    abH_sb[0:64, 2 * hp:2 * hp + 1].to_broadcast([64, T]), ALU.mult))
                dve(lambda e, hp=hp: e.tensor_tensor(
                    kaT_sb[64:128, :], kTn_sb[64:128, :],
                    abH_sb[64:128, 2 * hp + 1:2 * hp + 2].to_broadcast([64, T]), ALU.mult))
                dve(lambda e: e.memset(M32[:], 0.0))
                v_state0 = dve(lambda e: e.memset(M16[:], 0.0))
                ckpt(3.7)

                if dbg and hp == 0:
                    wait("sync", "dv", v_state0)
                    stx("dbg", dbg_out["qT0"][r], qT_sb[:])
                    stx("dbg", dbg_out["kTn0"][r], kTn_sb[:])

                ckpt(4)
                mark_phase(f'chunks_r{r}')
                wait("tensor", "dv", v_state0)
                chst = [dict() for _ in range(NCH)]
                v_seq_st = {-1: v_state0}
                bpair = abH_sb[0:64, 16 + 2 * hp:16 + 2 * hp + 2].unsqueeze(2).to_broadcast([64, 2, C])
                bpair2d = abH_sb[0:64, 16 + 2 * hp:16 + 2 * hp + 2]
                mULp = mUL_sb[:, 2 * hp:2 * hp + 2, :, :]
                dmask2 = dmask_sb[:].unsqueeze(1).to_broadcast([64, 2, C])

                def gen1_mms(ci):
                    cs = ci * C
                    par = ci % 2
                    b0 = pb[par * 2]
                    b1 = pb[par * 2 + 1]
                    qTa = qT_sb[0:64, cs:cs + C]
                    qTb = qT_sb[64:128, cs:cs + C]
                    kTa = kTn_sb[0:64, cs:cs + C]
                    kTb = kTn_sb[64:128, cs:cs + C]
                    vTa = vT_sb[0:64, cs:cs + C]
                    vTb = vT_sb[64:128, cs:cs + C]
                    if ci >= 2:
                        wait("tensor", "dv", chst[ci - 2].get("dv_chain", 0))
                        wait("tensor", "dv", chst[ci - 2].get("dv_attnT", 0))
                        wait("tensor", "ac", chst[ci - 2].get("ac_attnT", 0))
                        if DMA_CHAIN:
                            wait("tensor", f"gl{ci % 2}", chst[ci - 2].get("gd_chain", 0))
                    MM(b1[0:64, 0:64], kTb, kTb, True, (64, 0))
                    MM(b1[0:64, 64:128], qTb, kTb, False, (64, 0))
                    MM(b1[0:64, 128:192], kTb, id_bf[64:128, 64:128], False, (64, 0))
                    MM(b1[0:64, 192:256], vTb, id_bf[64:128, 64:128], False, (64, 0))
                    MM(b0[0:64, 0:64], kTa, kTa, True, (0, 0))
                    MM(b0[0:64, 64:128], qTa, kTa, False, (0, 0))
                    MM(b0[0:64, 128:192], kTa, id_bf[0:64, 0:64], False, (0, 0))
                    v = MM(b0[0:64, 192:256], vTa, id_bf[0:64, 0:64], False, (0, 0))
                    chst[ci]["pe_gen1"] = v

                def gen1_evacs(ci):
                    par = ci % 2
                    pari = ci % 4
                    st = chst[ci]
                    bsp = pbW[par][0:64, :].rearrange("p (h c) -> p h c", h=2)
                    Gs4 = bsp[:, :, 0:C].unsqueeze(2).to_broadcast([64, 2, 2, C])
                    NL4 = chNWL[:, pari, :, C:3 * C].rearrange("p h (o c) -> p h o c", o=2)
                    Uc = chNWL[:, pari, :, 0:C]
                    VB2c = VB2[:, pari, :, :]
                    at2 = attn2[:, pari, :, :]
                    if dbg:
                        wait("vector", "dbg", counts.get("dbg", 0))
                    wait("vector", "pe", st["pe_gen1"])
                    if NO_4D:
                        dve(lambda e, bsp=bsp, pari=pari, mULp=mULp: e.tensor_tensor(
                            chNWL[:, pari, :, C:2 * C], bsp[:, :, 0:C], mULp[:, :, 0, :], ALU.mult))
                        dve(lambda e, bsp=bsp, pari=pari, mULp=mULp: e.tensor_tensor(
                            chNWL[:, pari, :, 2 * C:3 * C], bsp[:, :, 0:C], mULp[:, :, 1, :], ALU.mult))
                    else:
                        dve(lambda e, Gs4=Gs4, NL4=NL4, mULp=mULp: e.tensor_tensor(NL4, Gs4, mULp, ALU.mult))
                    dve(lambda e, bsp=bsp, Uc=Uc: e.tensor_copy(Uc, bsp[:, :, 128:192]))
                    dve(lambda e, bsp=bsp, VB2c=VB2c, bpair=bpair: e.tensor_tensor(VB2c, bsp[:, :, 192:256], bpair, ALU.mult))
                    v = dve(lambda e, bsp=bsp: e.scalar_tensor_tensor(
                        sm_sb[:], bsp[:, :, 64:128], 0.125, dmask2, ALU.mult, ALU.add))
                    wait("scalar", "dv", v)
                    act(lambda e: e.activation(sm_sb[:, 0, :], sm_sb[:, 0, :], AF.Exp,
                                               accum_out=ssum[:, 0:1]))
                    v = act(lambda e: e.activation(sm_sb[:, 1, :], sm_sb[:, 1, :], AF.Exp,
                                                   accum_out=ssum[:, 1:2]))
                    wait("vector", "ac", v)
                    dve(lambda e, bp2=bpair2d: e.tensor_tensor(ssum[:], ssum[:], bp2, ALU.mult))
                    dve(lambda e: e.reciprocal(srec[:], ssum[:]))
                    v = dve(lambda e, at2=at2: e.tensor_tensor(at2, sm_sb[:],
                                                               srec[:].unsqueeze(2).to_broadcast([64, 2, C]), ALU.mult))
                    st["dv_par"] = v

                def chain_lvl_mms(ci, li, lvl):
                    par = ci % 2
                    pari = ci % 4
                    b = pb[par * 2 + ((li + 1) % 2)]
                    st = chst[ci]
                    wait("tensor", "dv", st.get("dv_chain", st["dv_par"]))
                    wait("tensor", "ac", st.get("ac_chain", 0))
                    if DMA_CHAIN:
                        wait("tensor", f"gl{ci % 2}", st.get("gd_chain", 0))
                    UN_a = chNWL[:, pari, 0, 0:2 * C]
                    UN_b = chNWL[:, pari, 1, 0:2 * C]
                    N_a = chNWL[:, pari, 0, C:2 * C]
                    N_b = chNWL[:, pari, 1, C:2 * C]
                    L_a = chNWL[:, pari, 0, 2 * C:3 * C]
                    L_b = chNWL[:, pari, 1, 2 * C:3 * C]
                    if lvl < 16:
                        MM(b[0:64, 0:128], L_a, UN_a, True, (0, 0))
                        v = MM(b[0:64, 128:256], L_b, UN_b, False, (0, 0))
                        MM(b[0:64, 256:320], N_a, L_a, False, (0, 0))
                        v = MM(b[0:64, 320:384], N_b, L_b, False, (0, 0))
                    elif lvl == 16:
                        MM(b[0:64, 0:64], L_a, chNWL[:, pari, 0, 0:C], True, (0, 0))
                        v = MM(b[0:64, 64:128], L_b, chNWL[:, pari, 1, 0:C], False, (0, 0))
                        MM(b[0:64, 256:320], N_a, L_a, False, (0, 0))
                        v = MM(b[0:64, 320:384], N_b, L_b, False, (0, 0))
                    else:
                        MM(b[0:64, 0:64], L_a, chNWL[:, pari, 0, 0:C], True, (0, 0))
                        v = MM(b[0:64, 64:128], L_b, chNWL[:, pari, 1, 0:C], False, (0, 0))
                    st["pe_chain"] = v

                def chain_lvl_evacs(ci, li, lvl):
                    par = ci % 2
                    pari = ci % 4
                    b = pb[par * 2 + ((li + 1) % 2)]
                    st = chst[ci]
                    Uc = chNWL[:, pari, :, 0:C]
                    sgn = -1.0 if lvl == 1 else 1.0
                    if dbg:
                        wait("vector", "dbg", counts.get("dbg", 0))
                        wait("scalar", "dbg", counts.get("dbg", 0))
                    wait("vector", "pe", st["pe_chain"])
                    if lvl < 16:
                        v = dve(lambda e, b=b, sgn=sgn, Uc=Uc: e.scalar_tensor_tensor(
                            Uc, b[0:64, 0:256].rearrange("p (h c) -> p h c", h=2)[:, :, 0:C],
                            sgn, Uc, ALU.mult, ALU.add))
                        if NO_ACT_CHAIN:
                            if DMA_CHAIN:
                                wait("gpsimd", "pe", st["pe_chain"])
                                wait("gpsimd", "dv", v)
                                st["gd_chain"] = gdma(
                                    f"gl{ci % 2}", chNWL[:, pari, :, 2 * C:3 * C],
                                    b[0:64, 256:384].rearrange("p (h c) -> p h c", h=2))
                                v = dve(lambda e, b=b, pari=pari: e.tensor_copy(
                                    chNWL[:, pari, :, C:2 * C],
                                    b[0:64, 0:256].rearrange("p (h c) -> p h c", h=2)[:, :, C:2 * C]))
                            else:
                                dve(lambda e, b=b, pari=pari: e.tensor_copy(
                                    chNWL[:, pari, :, C:2 * C],
                                    b[0:64, 0:256].rearrange("p (h c) -> p h c", h=2)[:, :, C:2 * C]))
                                v = dve(lambda e, b=b, pari=pari: e.tensor_copy(
                                    chNWL[:, pari, :, 2 * C:3 * C],
                                    b[0:64, 256:384].rearrange("p (h c) -> p h c", h=2)))
                        else:
                            wait("scalar", "dv", v)
                            wait("scalar", "pe", st["pe_chain"])
                            act(lambda e, b=b, pari=pari: e.activation(
                                chNWL[:, pari, :, C:2 * C],
                                b[0:64, 0:256].rearrange("p (h c) -> p h c", h=2)[:, :, C:2 * C], AF.Copy))
                            va = act(lambda e, b=b, pari=pari: e.activation(
                                chNWL[:, pari, :, 2 * C:3 * C],
                                b[0:64, 256:384].rearrange("p (h c) -> p h c", h=2), AF.Copy))
                            st["ac_chain"] = va
                    elif lvl == 16:
                        v = dve(lambda e, b=b, Uc=Uc: e.scalar_tensor_tensor(
                            Uc, b[0:64, 0:128].rearrange("p (h c) -> p h c", h=2),
                            1.0, Uc, ALU.mult, ALU.add))
                        if NO_ACT_CHAIN:
                            if DMA_CHAIN:
                                wait("gpsimd", "pe", st["pe_chain"])
                                wait("gpsimd", "dv", v)
                                st["gd_chain"] = gdma(
                                    f"gl{ci % 2}", chNWL[:, pari, :, 2 * C:3 * C],
                                    b[0:64, 256:384].rearrange("p (h c) -> p h c", h=2))
                            else:
                                v = dve(lambda e, b=b, pari=pari: e.tensor_copy(
                                    chNWL[:, pari, :, 2 * C:3 * C],
                                    b[0:64, 256:384].rearrange("p (h c) -> p h c", h=2)))
                        else:
                            wait("scalar", "dv", v)
                            wait("scalar", "pe", st["pe_chain"])
                            va = act(lambda e, b=b, pari=pari: e.activation(
                                chNWL[:, pari, :, 2 * C:3 * C],
                                b[0:64, 256:384].rearrange("p (h c) -> p h c", h=2), AF.Copy))
                            st["ac_chain"] = va
                    else:
                        v = dve(lambda e, b=b, Uc=Uc: e.scalar_tensor_tensor(
                            Uc, b[0:64, 0:128].rearrange("p (h c) -> p h c", h=2),
                            1.0, Uc, ALU.mult, ALU.add))
                    st["dv_chain"] = v

                def attnT_mms(ci):
                    par = ci % 2
                    pari = ci % 4
                    b = pb[par * 2 + 1]
                    st = chst[ci]
                    wait("tensor", "dv", st["dv_chain"])
                    wait("tensor", "ac", st.get("ac_chain", 0))
                    MM(b[0:64, 384:448], attn2[:, pari, 0, :], id_bf[0:64, 0:64], False, (0, 0))
                    st["pe_attnT"] = MM(b[0:64, 448:512], attn2[:, pari, 1, :], id_bf[0:64, 0:64], False, (0, 0))

                def attnT_evac(ci):
                    par = ci % 2
                    pari = ci % 4
                    b = pb[par * 2 + 1]
                    st = chst[ci]
                    if NO_ACT_ATTNT:
                        wait("vector", "pe", st["pe_attnT"])
                        st["dv_attnT"] = dve(lambda e, b=b, pari=pari: e.tensor_copy(
                            attnT2[:, pari, :, :],
                            b[0:64, 384:512].rearrange("p (h c) -> p h c", h=2)))
                        st["ac_attnT"] = 0
                        return
                    if dbg:
                        wait("scalar", "dbg", counts.get("dbg", 0))
                    wait("scalar", "pe", st["pe_attnT"])
                    st["ac_attnT"] = act(lambda e, b=b, pari=pari: e.activation(
                        attnT2[:, pari, :, :],
                        b[0:64, 384:512].rearrange("p (h c) -> p h c", h=2), AF.Copy))

                def seq1_mms(ci):
                    cs = ci * C
                    pari = ci % 4
                    st = chst[ci]
                    qTa = qT_sb[0:64, cs:cs + C]
                    qTb = qT_sb[64:128, cs:cs + C]
                    kaTa = kaT_sb[0:64, cs:cs + C]
                    kaTb = kaT_sb[64:128, cs:cs + C]
                    VB2c = VB2[:, pari, :, :]
                    aT2c = attnT2[:, pari, :, :]
                    wait("tensor", "dv", v_seq_st[ci - 1])
                    wait("tensor", "dv", st.get("dv_attnT", 0))
                    wait("tensor", "ac", st["ac_attnT"])
                    MM(pb[4][0:64, 0:64], kaTa, M16[0:64, 1, :], True, (0, 0))
                    MM(pb[5][0:64, 0:64], kaTb, M16[64:128, 1, :], True, (64, 0))
                    MM(pb[4][0:64, 64:128], VB2c[:, 0, :], aT2c[:, 0, :], False, (0, 0))
                    MM(pb[4][64:128, 64:128], VB2c[:, 1, :], aT2c[:, 1, :], True, (0, 64))
                    MM(pb[4][0:64, 256:320], M16[0:64, 0, :], qTa, False, (0, 0))
                    v = MM(pb[5][64:128, 64:128], M16[64:128, 0, :], qTb, True, (64, 64))
                    st["pe_seq1"] = v

                def seq1_dve(ci):
                    cs = ci * C
                    pari = ci % 4
                    st = chst[ci]
                    VB2c = VB2[:, pari, :, :]
                    AtSp = pbW[2][0:64, :].rearrange("p (h c) -> p h c", h=2)[:, :, 0:C]
                    if dbg:
                        wait("vector", "dbg", counts.get("dbg", 0))
                    wait("vector", "pe", st["pe_seq1"])
                    v_yt = dve(lambda e, VB2c=VB2c, AtSp=AtSp: e.tensor_tensor(Yt_sb[:], VB2c, AtSp, ALU.subtract))
                    dve(lambda e: e.tensor_copy(co_f[:, :], pb[4][:, 64:128]))
                    dve(lambda e, cs=cs, hp=hp: e.scalar_tensor_tensor(
                        coT_sb[0:64, hp, cs:cs + C], pb[4][0:64, 256:320], 0.1, co_f[0:64, :], ALU.mult, ALU.add))
                    dve(lambda e, cs=cs, hp=hp: e.scalar_tensor_tensor(
                        coT_sb[64:128, hp, cs:cs + C], pb[5][64:128, 64:128], 0.1, co_f[64:128, :], ALU.mult, ALU.add))
                    st["dv_yt"] = v_yt

                def seq2_mms(ci):
                    pari = ci % 4
                    st = chst[ci]
                    U2a = chNWL[:, pari, 0, 0:C]
                    U2b = chNWL[:, pari, 1, 0:C]
                    wait("tensor", "dv", st["dv_yt"])
                    MM(pb[6][0:64, 0:64], Yt_sb[:, 0, :], U2a, True, (0, 0))
                    MM(pb[6][64:128, 0:64], Yt_sb[:, 1, :], U2b, True, (0, 64))
                    MM(pb[6][0:64, 64:128], U2a, Yt_sb[:, 0, :], False, (0, 0))
                    v = MM(pb[6][64:128, 64:128], U2b, Yt_sb[:, 1, :], False, (0, 64))
                    st["pe_seq2"] = v

                def seq2_dve(ci):
                    st = chst[ci]
                    wait("vector", "pe", st["pe_seq2"])
                    dve(lambda e: e.tensor_tensor(
                        M32[:], M32[:], pb[6][:, 0:128].rearrange("p (o c) -> p o c", o=2), ALU.add))
                    v = dve(lambda e: e.tensor_copy(M16[:], M32[:]))
                    v_seq_st[ci] = v

                pend = None
                for sc in range(NCH // 2):
                    c0, c1 = 2 * sc, 2 * sc + 1
                    gen1_mms(c0)
                    gen1_mms(c1)
                    if NO_WEAVE:
                        gen1_evacs(c0)
                        gen1_evacs(c1)
                        for li, lvl in enumerate((1, 2, 4, 8, 16, 32)):
                            chain_lvl_mms(c0, li, lvl)
                            chain_lvl_mms(c1, li, lvl)
                            chain_lvl_evacs(c0, li, lvl)
                            chain_lvl_evacs(c1, li, lvl)
                        attnT_mms(c0)
                        attnT_mms(c1)
                        attnT_evac(c0)
                        attnT_evac(c1)
                        for p in (c0, c1):
                            seq1_mms(p)
                            seq1_dve(p)
                            seq2_mms(p)
                            seq2_dve(p)
                        continue
                    if pend is None:
                        gen1_evacs(c0)
                        if dbg and hp == 0 and r == 0:
                            wait("sync", "dv", counts.get("dv", 0))
                            stx("dbg", dbg_out["probe16"][0:64, 1664:1728], chNWL[:, 0, 1, C:2 * C])
                        gen1_evacs(c1)
                    else:
                        p0, p1 = pend
                        seq1_mms(p0)
                        seq1_dve(p0)
                        seq2_mms(p0)
                        seq2_dve(p0)
                        gen1_evacs(c0)
                        if dbg and hp == 0 and r == 0 and p0 == 0:
                            wait("sync", "dv", counts.get("dv", 0))
                            wait("sync", "ac", counts.get("ac", 0))
                            stx("dbg", dbg_out["probe16"][0:64, 64:128], chNWL[:, 0, 0, 0:C])
                            stx("dbg", dbg_out["probe16"][0:64, 192:256], VB2[:, 0, 0, :])
                            stx("dbg", dbg_out["probe16"][0:64, 256:320], Yt_sb[:, 0, :])
                            stx("dbg", dbg_out["probe16"][0:64, 320:384], attn2[:, 0, 0, :])
                            stx("dbg", dbg_out["probe16"][0:64, 384:448], attnT2[:, 0, 0, :])
                            stx("dbg", dbg_out["probe32"][0:64, 0:64], M32[0:64, 0, :])
                            stx("dbg", dbg_out["probe32"][0:64, 64:128], M32[0:64, 1, :])
                            stx("dbg", dbg_out["probe16"][0:64, 1728:1792], chNWL[:, 0, 1, 0:C])
                            stx("dbg", dbg_out["probe16"][0:64, 1792:1856], VB2[:, 0, 1, :])
                            stx("dbg", dbg_out["probe16"][0:64, 1856:1920], Yt_sb[:, 1, :])
                            stx("dbg", dbg_out["probe16"][0:64, 1920:1984], attn2[:, 0, 1, :])
                        seq1_mms(p1)
                        seq1_dve(p1)
                        seq2_mms(p1)
                        seq2_dve(p1)
                        gen1_evacs(c1)
                    for li, lvl in enumerate((1, 2, 4, 8, 16, 32)):
                        chain_lvl_mms(c0, li, lvl)
                        chain_lvl_mms(c1, li, lvl)
                        chain_lvl_evacs(c0, li, lvl)
                        chain_lvl_evacs(c1, li, lvl)
                    attnT_mms(c0)
                    attnT_mms(c1)
                    attnT_evac(c0)
                    attnT_evac(c1)
                    pend = (c0, c1)
                for p in (pend or ()):
                    seq1_mms(p)
                    seq1_dve(p)
                    seq2_mms(p)
                    seq2_dve(p)
                if DUMP_QK and hp == 0 and r == 0:
                    wait("sync", "dv", counts.get("dv", 0))
                    wait("sync", "ac", counts.get("ac", 0))
                    stx("dbg", qk_dump[0], qT_sb[:])
                    stx("dbg", qk_dump[1], kTn_sb[:])
                    stx("dbg", qk_dump[2], vT_sb[:])
                    v_qkd = stx("dbg", qk_dump[3], coT_sb[:, 0, :])
                    wait("sync", "dbg", v_qkd)
                    wait("vector", "dbg", v_qkd)
                    wait("scalar", "dbg", v_qkd)
                    wait("tensor", "dbg", v_qkd)
                    wait("gpsimd", "dbg", v_qkd)
                if dbg and hp == 0:
                    wait("sync", "dv", counts.get("dv", 0))
                    stx("dbg", dbg_out["coT0"][r], coT_sb[:, 0, :])

            ckpt(5)
            mark_phase(f'wo_r{r}')
            all_engine_barrier()
            ar.reset(mk_wo)
            wo_sb = ar.al([128, 8, D], BF16)
            v_wo = gdma("wo", wo_sb[:], woT_d.rearrange("(k p) o -> p k o", p=128))
            pe_dv_barrier()
            wait("tensor", "wo", v_wo)
            for m in range(NT):
                for n in range(2):
                    b = pb[6 + ((m * 2 + n) % 2)]
                    pe_dv_barrier()
                    for kt in range(8):
                        pe(lambda e, b=b, m=m, n=n, kt=kt: e.matmul(
                            b[:, :], coT_sb[:, kt, m * 128:(m + 1) * 128],
                            wo_sb[:, kt, n * 512:(n + 1) * 512],
                            start=(kt == 0), stop=(kt == 7)))
                    v = counts.get("pe", 0)
                    wait("vector", "pe", v)
                    dve(lambda e, b=b, m=m, n=n: e.tensor_copy(s_sb[:, m, n * 512:(n + 1) * 512], b[:, :]))
            v_s_done = counts.get("dv", 0)
            if dbg:
                wait("sync", "dv", v_s_done)
                for m in range(NT):
                    stx("dbg", dbg_out["s_attn"][r, m], s_sb[:, m, :])

            ckpt(6)
            mark_phase(f'ffn_rms1_r{r}')
            all_engine_barrier()
            ar.reset(0)
            xselT_sb = ar.al([128, 8, CAP], BF16)
            hNT_sb = ar.al([128, 8, CAP], BF16)
            zT_sb = ar.al([128, 8, CAP], F32)
            h2T_sb = ar.al([128, 8, CAP], BF16)
            oT_sb = ar.al([128, 8, CAP], BF16)
            mk_ffn1 = ar.mark()
            xsel_sb = ar.al([128, NCAP, D], F32)
            fsq_sb = ar.al([128, NCAP, D], F32)
            hnat_sb = ar.al([128, NCAP, D], BF16)

            wait("sync", "sc0", counts.get("sc0", 0))
            wait("sync", "sc1", counts.get("sc1", 0))
            v_xsel = sdma("scm", xsel_sb[:], xsel_dram[r].rearrange("(n p) d -> p n d", p=128))
            if dbg:
                wait("sync", "scm", v_xsel)
                stx("dbg", dbg_out["xsel"][r].rearrange("(n p) d -> p n d", p=128), xsel_sb[:])
            wait("scalar", "scm", v_xsel)
            v = act(lambda e: e.activation(fsq_sb[:], xsel_sb[:], AF.Square))
            wait("vector", "ac", v)
            wait("vector", "scm", v_xsel)
            dve(lambda e: e.tensor_reduce(fst_sb[:, :, 0:1], fsq_sb[:], axis=AX.X, op=ALU.add))
            v = dve(lambda e: e.tensor_scalar(fst_sb[:, :, 1:2], fst_sb[:, :, 0:1], 1.0 / D, 1e-6, ALU.mult, ALU.add))
            wait("scalar", "dv", v)
            v = act(lambda e: e.activation(fst_sb[:, :, 2:3], fst_sb[:, :, 1:2], AF.Ln))
            wait("vector", "ac", v)
            v = dve(lambda e: e.tensor_scalar(fst_sb[:, :, 2:3], fst_sb[:, :, 2:3], -0.5, None, ALU.mult))
            wait("scalar", "dv", v)
            v = act(lambda e: e.activation(fst_sb[:, :, 3:4], fst_sb[:, :, 2:3], AF.Exp))
            wait("vector", "ac", v)
            v_hn = dve(lambda e: e.tensor_tensor(hnat_sb[:], xsel_sb[:],
                                                 fst_sb[:, :, 3:4].to_broadcast([128, NCAP, D]), ALU.mult))
            wait("gpsimd", "dv", v_hn)
            gdma("gh", hnb_dram[r].rearrange("(n p) d -> p n d", p=128), hnat_sb[:])
            v2 = gdma("gh", xselb_dram[r].rearrange("(n p) d -> p n d", p=128), xsel_sb[:])
            wait("scalar", "gh", v2)
            v_hT = 0
            for j in range(8):
                adma("fft", hNT_sb[:, j, :], hnb_dram[r, :, j * 128:(j + 1) * 128], transpose=True)
                v_hT = adma("fft", xselT_sb[:, j, :], xselb_dram[r, :, j * 128:(j + 1) * 128], transpose=True)

            ckpt(7)
            all_engine_barrier()
            ar.reset(mk_ffn1)
            wqkv_sb = ar.al([128, 8, 1536], BF16)
            qkvT_sb = ar.al([128, 24, CAP], BF16)
            fattnT_sb = ar.al([128, NKB, 2, CAP], BF16)

            wait("tensor", "fft", v_hT)
            for half in range(2):
                v_wqkv = gdma("wqv", wqkv_sb[:], wqkvT_d.rearrange("(k p) o -> p k o", p=128)[:, :, half * 1536:(half + 1) * 1536])
                wait("vector", "wqv", v_wqkv)
                wait("vector", "dv", v_g12)
                for kt in range(8):
                    dve(lambda e, kt=kt: e.tensor_tensor(
                        wqkv_sb[:, kt, :], wqkv_sb[:, kt, :],
                        g1c_sb[:, kt:kt + 1].to_broadcast([128, 1536]), ALU.mult))
                if half == 0:
                    dve(lambda e: e.tensor_scalar(wqkv_sb[:, :, 0:D], wqkv_sb[:, :, 0:D], 0.125, None, ALU.mult))
                v_fold = counts.get("dv", 0)
                wait("tensor", "dv", v_fold)
                for oo in range(12):
                    o = half * 12 + oo
                    b = pb[6 + (o % 2)]
                    pe_dv_barrier()
                    for kt in range(8):
                        pe(lambda e, b=b, oo=oo, kt=kt: e.matmul(
                            b[:, 0:CAP], wqkv_sb[:, kt, oo * 128:(oo + 1) * 128],
                            hNT_sb[:, kt, :], start=(kt == 0), stop=(kt == 7)))
                    v = counts.get("pe", 0)
                    wait("vector", "pe", v)
                    dve(lambda e, b=b, o=o: e.tensor_copy(qkvT_sb[:, o, :], b[:, 0:CAP]))
                wait("gpsimd", "pe", counts.get("pe", 0))
            v_qkv = counts.get("dv", 0)
            if dbg:
                wait("sync", "dv", v_qkv)
                stx("dbg", dbg_out["qkvT0"][r][:, 0:CAP], qkvT_sb[:, 0, :])
                stx("dbg", dbg_out["qkvT0"][r][:, CAP:2 * CAP], qkvT_sb[:, 8, :])
                stx("dbg", dbg_out["qkvT0"][r][:, 2 * CAP:3 * CAP], qkvT_sb[:, 16, :])

            ckpt(8)
            mark_phase(f'ffn_attn_r{r}')
            wait("tensor", "dv", v_qkv)
            for hp in range(8):
                pe_dv_barrier()
                for kb in range(NKB):
                    MM(pb[1][0:128, kb * 64:kb * 64 + 64], qkvT_sb[0:64, 16 + hp, kb * 128:(kb + 1) * 128],
                       id_bf[0:64, 0:64], kb == 0, (0, 0))
                    v_mm = MM(pb[3][0:128, kb * 64:kb * 64 + 64], qkvT_sb[64:128, 16 + hp, kb * 128:(kb + 1) * 128],
                              id_bf[64:128, 64:128], kb == 0, (64, 0))
                wait("vector", "pe", v_mm)
                dve(lambda e: e.tensor_copy(vnat_sb[:, :, 0, :],
                                            pb[1][0:128, 0:NKB * 64].rearrange("p (k c) -> p k c", k=NKB)))
                dve(lambda e: e.tensor_copy(vnat_sb[:, :, 1, :],
                                            pb[3][0:128, 0:NKB * 64].rearrange("p (k c) -> p k c", k=NKB)))
                v_stt_prev = counts.get("dv", 0)
                for qs in range(NQS):
                    wait("tensor", "dv", v_stt_prev)
                    MM(pb[0][0:64, 0:CAP], qkvT_sb[0:64, hp, qs * 64:(qs + 1) * 64],
                       qkvT_sb[0:64, 8 + hp, :], True, (0, 0))
                    v_sb_ = MM(pb[2][0:64, 0:CAP], qkvT_sb[64:128, hp, qs * 64:(qs + 1) * 64],
                               qkvT_sb[64:128, 8 + hp, :], True, (64, 0))
                    wait("vector", "pe", v_sb_)
                    moff = 64 * (NQS - 1 - qs)
                    dve(lambda e, qs=qs, moff=moff: e.scalar_tensor_tensor(
                        fS_sb[:, 0, :], pb[0][0:64, 0:CAP], 1.0, fmb_sb[:, moff:moff + CAP], ALU.mult, ALU.add))
                    v_stt_prev = dve(lambda e, qs=qs, moff=moff: e.scalar_tensor_tensor(
                        fS_sb[:, 1, :], pb[2][0:64, 0:CAP], 1.0, fmb_sb[:, moff:moff + CAP], ALU.mult, ALU.add))
                    wait("scalar", "dv", counts.get("dv", 0))
                    act(lambda e: e.activation(fS_sb[:, 0, :], fS_sb[:, 0, :], AF.Exp,
                                               accum_out=fsum_sb[:, 0:1]))
                    v = act(lambda e: e.activation(fS_sb[:, 1, :], fS_sb[:, 1, :], AF.Exp,
                                                   accum_out=fsum_sb[:, 1:2]))
                    wait("vector", "ac", v)
                    dve(lambda e: e.reciprocal(frec_sb[:], fsum_sb[:]))
                    v_fat = dve(lambda e: e.tensor_tensor(fattn_sb[:], fS_sb[:],
                                                          frec_sb[:].unsqueeze(2).to_broadcast([64, 2, CAP]), ALU.mult))
                    if dbg and hp == 0 and qs == 0 and r == 0:
                        wait("sync", "dv", v_fat)
                        stx("dbg", dbg_out["probe32"][0:64, 128:128 + CAP], fS_sb[:, 0, :])
                        stx("dbg", dbg_out["probe16"][0:64, 448:448 + CAP], fattn_sb[:, 0, :])
                        stx("dbg", dbg_out["probe16"][0:64, 2048:2048 + CAP], fattn_sb[:, 1, :])
                        wait("vector", "dbg", counts["dbg"])
                    wait("tensor", "dv", v_fat)
                    for kb in range(NKB):
                        MM(pb[1][0:128, 256 + kb * 64:256 + kb * 64 + 64],
                           fattn_sb[:, 0, kb * 128:(kb + 1) * 128], id_bf[0:64, 0:64], kb == 0, (0, 0))
                        v_mm = MM(pb[3][0:128, 256 + kb * 64:256 + kb * 64 + 64],
                                  fattn_sb[:, 1, kb * 128:(kb + 1) * 128], id_bf[0:64, 0:64], kb == 0, (0, 0))
                    wait("vector", "pe", v_mm)
                    dve(lambda e, qs=qs: e.tensor_copy(
                        fattnT_sb[:, :, 0, qs * 64:(qs + 1) * 64],
                        pb[1][0:128, 256:256 + NKB * 64].rearrange("p (k c) -> p k c", k=NKB)))
                    dve(lambda e, qs=qs: e.tensor_copy(
                        fattnT_sb[:, :, 1, qs * 64:(qs + 1) * 64],
                        pb[3][0:128, 256:256 + NKB * 64].rearrange("p (k c) -> p k c", k=NKB)))
                v_fT = counts.get("dv", 0)
                wait("tensor", "dv", v_fT)
                for kb in range(NKB):
                    MM(pb[0][0:64, 0:CAP], vnat_sb[:, kb, 0, :], fattnT_sb[:, kb, 0, :], kb == 0, (0, 0))
                    v_mm = MM(pb[4][64:128, 0:CAP], vnat_sb[:, kb, 1, :], fattnT_sb[:, kb, 1, :], kb == 0, (0, 64))
                wait("vector", "pe", v_mm)
                dve(lambda e, hp=hp: e.tensor_copy(oT_sb[0:64, hp, :], pb[0][0:64, 0:CAP]))
                dve(lambda e, hp=hp: e.tensor_copy(oT_sb[64:128, hp, :], pb[4][64:128, 0:CAP]))
                if dbg and hp == 0 and r == 0:
                    wait("sync", "dv", counts["dv"])
                    stx("dbg", dbg_out["probe16"][:, 1024:1024 + CAP], fattnT_sb[:, 0, 0, :])
                    stx("dbg", dbg_out["probe16"][:, 1536:1536 + 64], vnat_sb[:, 0, 0, :])
                    wait("vector", "dbg", counts["dbg"])
            v_oT = counts.get("dv", 0)
            if dbg:
                wait("sync", "dv", v_oT)
                stx("dbg", dbg_out["oT0"][r], oT_sb[:, 0, :])

            ckpt(9)
            mark_phase(f'ffn_wout_r{r}')
            all_engine_barrier()
            ar.reset(mk_ffn1)
            wout_sb = ar.al([128, 8, D], BF16)
            v_wout = gdma("wou", wout_sb[:], woutT_d.rearrange("(k p) o -> p k o", p=128))
            wait("tensor", "dv", v_oT)
            wait("tensor", "wou", v_wout)
            for o in range(8):
                b = pb[6 + (o % 2)]
                pe_dv_barrier()
                for kt in range(8):
                    pe(lambda e, b=b, o=o, kt=kt: e.matmul(
                        b[:, 0:CAP], wout_sb[:, kt, o * 128:(o + 1) * 128],
                        oT_sb[:, kt, :], start=(kt == 0), stop=(kt == 7)))
                v = counts.get("pe", 0)
                wait("vector", "pe", v)
                dve(lambda e, b=b, o=o: e.scalar_tensor_tensor(
                    zT_sb[:, o, :], b[:, 0:CAP], 1.0, xselT_sb[:, o, :], ALU.mult, ALU.add))
            v_z = counts.get("dv", 0)
            if dbg:
                wait("sync", "dv", v_z)
                stx("dbg", dbg_out["zT0"][r], zT_sb[:, 0, :])

            v_mm_prev = [0, 0]
            for kt in range(8):
                zb = zsq_sb[kt % 2]
                wait("scalar", "dv", v_z)
                wait("scalar", "pe", v_mm_prev[kt % 2])
                v = act(lambda e, kt=kt, zb=zb: e.activation(zb[:], zT_sb[:, kt, :], AF.Square))
                wait("tensor", "ac", v)
                v_mm_prev[kt % 2] = pe(lambda e, kt=kt, zb=zb: e.matmul(
                    pb[5][0:1, 0:CAP], ones128_sb[:], zb[:],
                    start=(kt == 0), stop=(kt == 7)))
            v = v_mm_prev[1]
            wait("vector", "pe", v)
            v = dve(lambda e: e.tensor_scalar(ssq2_sb[:], pb[5][0:1, 0:CAP], 1.0 / D, 1e-6, ALU.mult, ALU.add))
            wait("scalar", "dv", v)
            v = act(lambda e: e.activation(ssq2_sb[:], ssq2_sb[:], AF.Ln))
            wait("vector", "ac", v)
            v = dve(lambda e: e.tensor_scalar(ssq2_sb[:], ssq2_sb[:], -0.5, None, ALU.mult))
            wait("scalar", "dv", v)
            v = act(lambda e: e.activation(rstd2_sb[:], ssq2_sb[:], AF.Exp))
            wait("tensor", "ac", v)
            pe_dv_barrier()
            v = pe(lambda e: e.matmul(pb[5][:, 0:CAP], onesrow_sb[0:1, :], rstd2_sb[0:1, :],
                                      start=True, stop=True, skip_group_check=True))
            wait("vector", "pe", v)
            dve(lambda e: e.tensor_copy(rbc2_sb[:], pb[5][:, 0:CAP]))
            for kt in range(8):
                dve(lambda e, kt=kt: e.tensor_tensor(h2T_sb[:, kt, :], zT_sb[:, kt, :], rbc2_sb[:], ALU.mult))
            v_h2 = counts.get("dv", 0)

            ckpt(10)
            mark_phase(f'ffn_mlp_r{r}')
            all_engine_barrier()
            ar.reset(mk_ffn1)
            xpT_sb = ar.al([128, 8, CAP], F32)
            duT_sb = ar.al([128, 8, CAP], F32)
            a1_sb = ar.al([128, CAP], F32)
            mk_mlp = ar.mark()
            gT_sb = ar.al([128, KFF, CAP], BF16)
            w12s = [ar.al([128, 8, 256], BF16) for _ in range(2)]
            w3s = [ar.al([128, KFF, 128], BF16) for _ in range(2)]

            w12_free = {0: 0, 1: 0}
            wait("vector", "dv", v_h2)
            for o in range(KFF):
                bi = o % 2
                wait("gpsimd", "pe", w12_free[bi])
                gdma(f"wf{bi}", w12s[bi][:, :, 0:128], w1T_d.rearrange("(k p) o -> p k o", p=128)[:, :, o * 128:(o + 1) * 128])
                v_w = gdma(f"wf{bi}", w12s[bi][:, :, 128:256], w2T_d.rearrange("(k p) o -> p k o", p=128)[:, :, o * 128:(o + 1) * 128])
                wait("vector", f"wf{bi}", v_w)
                for kt in range(8):
                    dve(lambda e, kt=kt, bi=bi: e.tensor_tensor(
                        w12s[bi][:, kt, :], w12s[bi][:, kt, :],
                        g2c_sb[:, kt:kt + 1].to_broadcast([128, 256]), ALU.mult))
                v_wf = counts.get("dv", 0)
                wait("tensor", "dv", v_wf)
                for kt in range(8):
                    pe(lambda e, o=o, kt=kt, bi=bi: e.matmul(
                        pb[6][:, 0:CAP], w12s[bi][:, kt, 0:128],
                        h2T_sb[:, kt, :], start=(kt == 0), stop=(kt == 7)))
                v_u = counts.get("pe", 0)
                for kt in range(8):
                    pe(lambda e, o=o, kt=kt, bi=bi: e.matmul(
                        pb[7][:, 0:CAP], w12s[bi][:, kt, 128:256],
                        h2T_sb[:, kt, :], start=(kt == 0), stop=(kt == 7)))
                v_v2 = counts.get("pe", 0)
                w12_free[bi] = v_v2
                wait("scalar", "pe", v_u)
                wait("scalar", "dv", counts.get("dv", 0) if o > 0 else 0)
                v_su = act(lambda e: e.activation(su_sb[:], pb[6][:, 0:CAP], AF.Silu))
                wait("vector", "pe", v_v2)
                dve(lambda e: e.tensor_copy(vv_sb[:], pb[7][:, 0:CAP]))
                wait("vector", "ac", v_su)
                v_g = dve(lambda e, o=o: e.tensor_tensor(gT_sb[:, o, :], su_sb[:], vv_sb[:], ALU.mult))
                wait("tensor", "dv", v_g)
                pe_ac_barrier()
            v_gT = counts.get("dv", 0)

            w3_free = {0: 0, 1: 0}
            wait("tensor", "dv", v_gT)
            for o in range(8):
                bi = o % 2
                b = pb[6 + bi]
                wait("gpsimd", "pe", w3_free[bi])
                v_w = gdma(f"w3{bi}", w3s[bi][:], w3T_d.rearrange("(k p) o -> p k o", p=128)[:, :, o * 128:(o + 1) * 128])
                wait("tensor", f"w3{bi}", v_w)
                pe_dv_barrier()
                for kt in range(KFF):
                    pe(lambda e, b=b, o=o, kt=kt, bi=bi: e.matmul(
                        b[:, 0:CAP], w3s[bi][:, kt, :],
                        gT_sb[:, kt, :], start=(kt == 0), stop=(kt == KFF - 1)))
                v = counts.get("pe", 0)
                w3_free[bi] = v
                wait("vector", "pe", v)
                dve(lambda e, b=b, o=o: e.scalar_tensor_tensor(
                    xpT_sb[:, o, :], b[:, 0:CAP], 1.0, zT_sb[:, o, :], ALU.mult, ALU.add))
            v_xp = counts.get("dv", 0)

            ckpt(11)
            wait("scalar", "dv", v_xp)
            v_du_prev = 0
            for o in range(8):
                wait("scalar", "dv", v_du_prev)
                act(lambda e, o=o: e.activation(duT_sb[:, o, :], xselT_sb[:, o, :], AF.Silu))
                v_a = act(lambda e, o=o: e.activation(a1_sb[:], xpT_sb[:, o, :], AF.Silu))
                wait("vector", "ac", v_a)
                v_du_prev = dve(lambda e, o=o: e.tensor_tensor(duT_sb[:, o, :], a1_sb[:], duT_sb[:, o, :], ALU.subtract))
            v_duT = counts.get("dv", 0)

            all_engine_barrier()
            ar.reset(mk_mlp)
            dun_sb = ar.al([128, NCAP, D], F32)
            wait("tensor", "dv", v_duT)
            for o in range(8):
                for qb in range(NCAP):
                    b = pb[6 + ((o * NCAP + qb) % 2)]
                    pe_dv_barrier()
                    v = pe(lambda e, b=b, o=o, qb=qb: e.matmul(
                        b[:, 0:128], duT_sb[:, o, qb * 128:(qb + 1) * 128], id_f32[:, :],
                        start=True, stop=True))
                    wait("vector", "pe", v)
                    dve(lambda e, b=b, o=o, qb=qb: e.tensor_copy(
                        dun_sb[:, qb, o * 128:(o + 1) * 128], b[:, 0:128]))
            v_dun = counts.get("dv", 0)
            wait("sync", "dv", v_dun)
            v_dp = sdma("scm", dp_dram[r].rearrange("(n p) d -> p n d", p=128), dun_sb[:])
            if dbg:
                stx("dbg", dbg_out["dp"][r].rearrange("(n p) d -> p n d", p=128), dun_sb[:])

            ckpt(12)
            mark_phase(f'epilogue_r{r}')
            all_engine_barrier()
            ar.reset(0)
            g_t = [ar.al([128, D], F32) for _ in range(2)]
            xe_t = [ar.al([128, D], F32) for _ in range(2)]
            out_t = [ar.al([128, D], F32) for _ in range(2)]
            ep_free = {0: (0, 0), 1: (0, 0)}
            for j in range(NT):
                bi = j % 2
                dv0, st0 = ep_free[bi]
                wait("sync", "dv", dv0)
                wait("sync", f"ey{bi}", st0)
                v_xnl = sdma(f"eg{bi}", g_t[bi][:], xn_dram[r, j * 128:(j + 1) * 128, :])
                wait("scalar", f"eg{bi}", v_xnl)
                v_si = act(lambda e, bi=bi: e.activation(g_t[bi][:], g_t[bi][:], AF.Silu))
                wait("gpsimd", "ac", v_si)
                wait("gpsimd", "scm", v_dp)
                v_ga = _mk("gpsimd", f"ga{bi}", 16, lambda e, bi=bi, j=j, r=r: e.indirect_dma_start(
                    out=g_t[bi][:],
                    out_offset=None,
                    in_=dp_dram[r],
                    in_offset=IndirectOffsetOnAxis(rki32[:, j:j + 1], 0),
                    bounds_check=bc_cell[0],
                    oob_is_err=False,
                    compute_op=ALU.add))
                v_xl = sdma(f"ex{bi}", xe_t[bi][:], x_in[r, j * 128:(j + 1) * 128, :])
                wait("vector", f"ga{bi}", v_ga)
                wait("vector", f"ex{bi}", v_xl)
                dve(lambda e, bi=bi, j=j: e.tensor_tensor(sg_t[:], s_sb[:, j, :], g_t[bi][:], ALU.mult))
                v_o = dve(lambda e, bi=bi: e.tensor_tensor(out_t[bi][:], xe_t[bi][:], sg_t[:], ALU.add))
                wait("sync", "dv", v_o)
                v_y = stx(f"ey{bi}", y_out[r, j * 128:(j + 1) * 128, :], out_t[bi][:])
                ep_free[bi] = (v_o, v_y)
        except StopEmission:
            pass
        mark_phase('end')
        for _nm in list(counts):
            wait("sync", _nm, counts[_nm])

        block = ctx.enter_context(nc.Block())
        for name in ("sync", "vector", "scalar", "gpsimd", "tensor"):
            fns = ops[name]
            def mk(fns):
                def body(eng):
                    for f in fns:
                        f(eng)
                return body
            getattr(block, name)(mk(fns))

    return nc, dbg_out


def make_consts(T, CAP):
    import ml_dtypes
    C = 64
    NQS = CAP // 64
    ident_bf = np.eye(128).astype(ml_dtypes.bfloat16)
    ident_f32 = np.eye(128, dtype=np.float32)
    onesrow = np.ones((1, 128), np.float32)
    ones128 = np.ones((128, 1), np.float32)
    ones64c = np.zeros((128, 2), np.float32)
    ones64c[0:64, 0] = 1.0
    ones64c[64:128, 1] = 1.0
    ones2T = np.ascontiguousarray(ones64c.T)
    t_i = np.arange(C)
    causU = (t_i[None, :] > t_i[:, None]).astype(np.float32)
    causL = (t_i[None, :] < t_i[:, None]).astype(np.float32)
    dmask = np.where(t_i[None, :] > t_i[:, None], NEG, 0.0).astype(np.float32)
    p_i = np.arange(64)[:, None]
    j_i = np.arange(CAP + 64 * NQS)[None, :]
    fmbase = np.where(j_i - 64 * (NQS - 1) <= p_i, 0.0, NEG).astype(np.float32)
    return dict(ident_bf=ident_bf, ident_f32=ident_f32, onesrow=onesrow,
                ones128=ones128, ones64c=ones64c, ones2T=ones2T,
                causU=causU, causL=causL, dmask=dmask, fmbase=fmbase)


def make_inputs_map(cfg, xshard, Wq, Wk, Wv, Wo, alpha_raw, beta_raw, w_router,
                    W_qkv, W_out, g1, g2, W1, W2, W3):
    T, R, DFFP = cfg["T"], cfg["R"], cfg["DFFP"]
    CAP = T // 4
    D = 1024
    tr = lambda w: np.ascontiguousarray(np.asarray(w, np.float32).T)
    DFF = W1.shape[0]
    W1p = np.zeros((DFFP, D), np.float32); W1p[:DFF] = W1
    W2p = np.zeros((DFFP, D), np.float32); W2p[:DFF] = W2
    W3p = np.zeros((D, DFFP), np.float32); W3p[:, :DFF] = W3
    g12 = np.concatenate([np.asarray(g1, np.float32).reshape(8, 128),
                          np.asarray(g2, np.float32).reshape(8, 128)], axis=0)
    ab = np.concatenate([np.asarray(alpha_raw).reshape(-1),
                         np.asarray(beta_raw).reshape(-1)]).reshape(1, 32).astype(np.float32)
    m = dict(x=np.ascontiguousarray(np.asarray(xshard, np.float32)),
             wqT=tr(Wq), wkT=tr(Wk), wvT=tr(Wv), woT=tr(Wo),
             ab_raw=ab, wrouter=np.asarray(w_router, np.float32).reshape(1, D),
             wqkvT=tr(W_qkv), woutT=tr(W_out), g12=g12,
             w1T=tr(W1p), w2T=tr(W2p), w3T=np.ascontiguousarray(W3p.T))
    m.update(make_consts(T, CAP))
    return m


_CFG = dict(T=2048, R=2, DFFP=2816, debug=False, strict=True, no_act_evac=True, no_weave=True)
_N_CORES = 8
_BUILT = None


def _get_program():
    global _BUILT
    if _BUILT is None:
        _BUILT = build_program(_CFG)
    return _BUILT


def kernel(L0_attn_x, L0_attn_Wq, L0_attn_Wk, L0_attn_Wv, L0_attn_Wo,
           L0_attn_alpha_raw, L0_attn_beta_raw, L0_ffn_w_router, L0_ffn_W_qkv,
           L0_ffn_W_out, L0_ffn_g1, L0_ffn_g2, L0_ffn_W1, L0_ffn_W2, L0_ffn_W3):
    x = np.asarray(L0_attn_x, np.float32)
    nc, _ = _get_program()
    base = make_inputs_map(_CFG, x[0:2], L0_attn_Wq, L0_attn_Wk, L0_attn_Wv,
                           L0_attn_Wo, L0_attn_alpha_raw, L0_attn_beta_raw,
                           L0_ffn_w_router, L0_ffn_W_qkv, L0_ffn_W_out,
                           L0_ffn_g1, L0_ffn_g2, L0_ffn_W1, L0_ffn_W2, L0_ffn_W3)
    in_maps = []
    for c in range(_N_CORES):
        m = dict(base)
        m["x"] = np.ascontiguousarray(x[2 * c:2 * c + 2])
        in_maps.append(m)
    res = run_bass_kernel_spmd(nc, in_maps, core_ids=list(range(_N_CORES)))
    return np.concatenate([np.asarray(r["y"], np.float32) for r in res.results], axis=0)


# version 19
# speedup vs baseline: 1.0364x; 1.0222x over previous
from contextlib import ExitStack
import numpy as np
import concourse.bass as bass
import concourse.mybir as mybir
from concourse.bass_utils import run_bass_kernel_spmd
from concourse.bass_core_rust import IndirectOffsetOnAxis

F32 = mybir.dt.float32
BF16 = mybir.dt.bfloat16
I32 = mybir.dt.int32
U8 = mybir.dt.uint8
AF = mybir.ActivationFunctionType
ALU = mybir.AluOpType
AX = mybir.AxisListType

NEG = -1e30
DTSZ = {F32: 4, BF16: 2, I32: 4}


class Arena:
    def __init__(self, ap, nbytes):
        self.ap = ap
        self.nbytes = nbytes
        self.off = 0

    def mark(self):
        return self.off

    def reset(self, mark=0):
        self.off = mark

    def al(self, shape, dtype):
        sz = DTSZ[dtype]
        free = 1
        for d in shape[1:]:
            free *= d
        nb = free * sz
        off = (self.off + 127) & ~127
        assert off + nb <= self.nbytes, f"arena overflow: need {nb} at {off}, cap {self.nbytes}"
        self.off = off + nb
        v = self.ap[:, off:off + nb].bitcast(dtype)
        if len(shape) == 3:
            v = v.rearrange("p (a b) -> p a b", a=shape[1])
        elif len(shape) == 4:
            v = v.rearrange("p (a b c) -> p a b c", a=shape[1], b=shape[2])
        if shape[0] < 128:
            v = v[0:shape[0]]
        return v


def build_program(cfg):
    T = cfg["T"]
    R = cfg["R"]
    D = 1024
    H = 16
    DH = 64
    C = 64
    NCH = T // C
    NT = T // 128
    CAP = T // 4
    NCAP = CAP // 128
    DFFP = cfg["DFFP"]
    KFF = DFFP // 128
    NQS = CAP // 64
    NKB = CAP // 128
    dbg = cfg.get("debug", False)
    strict = cfg.get("strict", False)
    PH = cfg.get("phases", 99)
    NO_WEAVE = cfg.get("no_weave", False)
    NO_ACT_EVAC = cfg.get("no_act_evac", False)
    NO_ACT_CHAIN = cfg.get("no_act_chain", False) or NO_ACT_EVAC
    NO_ACT_ATTNT = cfg.get("no_act_attnT", False) or NO_ACT_EVAC
    NO_4D = cfg.get("no_4d", False)
    DMA_CHAIN = cfg.get("dma_chain", False)
    assert NCAP >= 1 and NKB >= 1 and T % 512 == 0

    class StopEmission(Exception):
        pass

    nc = bass.Bass("TRN2", target_bir_lowering=False,
                   detect_race_conditions=strict)

    x_in = nc.dram_tensor("x", [R, T, D], F32, kind="ExternalInput").ap()
    wqT_d = nc.dram_tensor("wqT", [D, D], F32, kind="ExternalInput").ap()
    wkT_d = nc.dram_tensor("wkT", [D, D], F32, kind="ExternalInput").ap()
    wvT_d = nc.dram_tensor("wvT", [D, D], F32, kind="ExternalInput").ap()
    woT_d = nc.dram_tensor("woT", [D, D], F32, kind="ExternalInput").ap()
    ab_raw = nc.dram_tensor("ab_raw", [1, 2 * H], F32, kind="ExternalInput").ap()
    wrouter_d = nc.dram_tensor("wrouter", [1, D], F32, kind="ExternalInput").ap()
    wqkvT_d = nc.dram_tensor("wqkvT", [D, 3 * D], F32, kind="ExternalInput").ap()
    woutT_d = nc.dram_tensor("woutT", [D, D], F32, kind="ExternalInput").ap()
    g12_d = nc.dram_tensor("g12", [16, 128], F32, kind="ExternalInput").ap()
    w1T_d = nc.dram_tensor("w1T", [D, DFFP], F32, kind="ExternalInput").ap()
    w2T_d = nc.dram_tensor("w2T", [D, DFFP], F32, kind="ExternalInput").ap()
    w3T_d = nc.dram_tensor("w3T", [DFFP, D], F32, kind="ExternalInput").ap()
    ident_bf_d = nc.dram_tensor("ident_bf", [128, 128], BF16, kind="ExternalInput").ap()
    ident_f32_d = nc.dram_tensor("ident_f32", [128, 128], F32, kind="ExternalInput").ap()
    onesrow_d = nc.dram_tensor("onesrow", [1, 128], F32, kind="ExternalInput").ap()
    ones128_d = nc.dram_tensor("ones128", [128, 1], F32, kind="ExternalInput").ap()
    ones64c_d = nc.dram_tensor("ones64c", [128, 2], F32, kind="ExternalInput").ap()
    ones2T_d = nc.dram_tensor("ones2T", [2, 128], F32, kind="ExternalInput").ap()
    causU_d = nc.dram_tensor("causU", [64, C], F32, kind="ExternalInput").ap()
    causL_d = nc.dram_tensor("causL", [64, C], F32, kind="ExternalInput").ap()
    dmask_d = nc.dram_tensor("dmask", [64, C], F32, kind="ExternalInput").ap()
    fmb_d = nc.dram_tensor("fmbase", [64, CAP + 64 * NQS], F32, kind="ExternalInput").ap()
    y_out = nc.dram_tensor("y", [R, T, D], F32, kind="ExternalOutput").ap()
    DUMP_QK = cfg.get("dump_qk", False)
    if DUMP_QK:
        qk_dump = nc.dram_tensor("qk_dump", [4, 128, T], BF16, kind="ExternalOutput").ap()

    xn_dram = nc.dram_tensor("xn_dram", [R, T, D], F32, kind="Internal").ap()
    xnb_dram = nc.dram_tensor("xnb_dram", [R, T, D], BF16, kind="Internal").ap()
    sc_dram = nc.dram_tensor("sc_dram", [R, NT, 128], F32, kind="Internal").ap()
    xsel_dram = [nc.dram_tensor(f"xsel_dram{i}", [CAP, D], F32, kind="Internal").ap() for i in range(R)]
    xselb_dram = nc.dram_tensor("xselb_dram", [R, CAP, D], BF16, kind="Internal").ap()
    hnb_dram = nc.dram_tensor("hnb_dram", [R, CAP, D], BF16, kind="Internal").ap()
    dp_dram = [nc.dram_tensor(f"dp_dram{i}", [CAP, D], F32, kind="Internal").ap() for i in range(R)]

    dbg_out = {}
    if dbg:
        def dbg_t(name, shape, dt=F32):
            dbg_out[name] = nc.dram_tensor("dbg_" + name, shape, dt, kind="ExternalOutput").ap()
        dbg_t("xn", [R, T, D])
        dbg_t("ranks", [R, 128, NT])
        dbg_t("xsel", [R, CAP, D])
        dbg_t("qT0", [R, 128, T], BF16)
        dbg_t("kTn0", [R, 128, T], BF16)
        dbg_t("coT0", [R, 128, T], BF16)
        dbg_t("s_attn", [R, NT, 128, D], BF16)
        dbg_t("qkvT0", [R, 128, 3 * CAP], BF16)
        dbg_t("oT0", [R, 128, CAP], BF16)
        dbg_t("zT0", [R, 128, CAP])
        dbg_t("dp", [R, CAP, D])
        dbg_t("probe16", [128, 8192], BF16)
        dbg_t("probe32", [128, 8192])

    ctx = ExitStack()
    sbf = lambda n, s, d=F32: ctx.enter_context(nc.sbuf_tensor(n, s, d))

    with ctx:
        id_bf = sbf("id_bf", [128, 128], BF16)
        id_f32 = sbf("id_f32", [128, 128], F32)
        onesrow_sb = sbf("onesrow_sb", [1, 128])
        ones128_sb = sbf("ones128_sb", [128, 1])
        ones64c_sb = sbf("ones64c_sb", [128, 2])
        ones2T_sb = sbf("ones2T_sb", [2, 128])
        causU_sb = sbf("causU_sb", [64, C])
        causL_sb = sbf("causL_sb", [64, C])
        dmask_sb = sbf("dmask_sb", [64, C])
        fmb_sb = sbf("fmb_sb", [64, CAP + 64 * NQS])
        wr_row = sbf("wr_row", [1, D])
        wrb_sb = sbf("wrb_sb", [128, D])
        ab_sb = sbf("ab_sb", [1, 2 * H])
        abH_sb = sbf("abH_sb", [128, 2 * H])
        mUL_sb = sbf("mUL_sb", [64, H, 2, C], BF16)
        g1c_sb = sbf("g1c_sb", [128, 8])
        g2c_sb = sbf("g2c_sb", [128, 8])
        gr_sb = sbf("gr_sb", [16, 128])
        stat = sbf("stat", [128, 8])
        sccol = sbf("sccol", [128, NT])
        rkcol = sbf("rkcol", [128, NT])
        rki32 = sbf("rki32", [128, NT], I32)
        scT_sb = sbf("scT_sb", [16, 128])
        Yt_sb = sbf("Yt_sb", [64, 2, C], BF16)
        M32 = sbf("M32", [128, 2, C])
        M16 = sbf("M16", [128, 2, C], BF16)
        co_f = sbf("co_f", [128, C])
        sm_sb = sbf("sm_sb", [64, 2, C])
        ssum = sbf("ssum", [64, 2])
        srec = sbf("srec", [64, 2])
        chNWL = sbf("chNWL", [64, 4, 2, 3 * C], BF16)
        VB2 = sbf("VB2", [64, 4, 2, C], BF16)
        attn2 = sbf("attn2", [64, 4, 2, C], BF16)
        attnT2 = sbf("attnT2", [64, 4, 2, C], BF16)
        fS_sb = sbf("fS_sb", [64, 2, CAP])
        fsum_sb = sbf("fsum_sb", [64, 2])
        frec_sb = sbf("frec_sb", [64, 2])
        fattn_sb = sbf("fattn_sb", [64, 2, CAP], BF16)
        vnat_sb = sbf("vnat_sb", [128, NKB, 2, DH], BF16)
        fst_sb = sbf("fst_sb", [128, NCAP, 4])
        ssq2_sb = sbf("ssq2_sb", [1, CAP])
        rstd2_sb = sbf("rstd2_sb", [1, CAP])
        rbc2_sb = sbf("rbc2_sb", [128, CAP])
        su_sb = sbf("su_sb", [128, CAP], BF16)
        vv_sb = sbf("vv_sb", [128, CAP], BF16)
        zsq_sb = [sbf(f"zsq_sb{i}", [128, CAP]) for i in range(2)]
        sg_t = sbf("sg_t", [128, D])
        dbgt = sbf("dbgt", [64, 640]) if dbg else None
        s_sb = sbf("s_sb", [128, NT, D], BF16)

        ARENA_KB = cfg.get("arena_kb", 123)
        AR = sbf("AR", [128, ARENA_KB * 1024], U8)
        ar = Arena(AR[:], ARENA_KB * 1024)

        pbW = [ctx.enter_context(nc.psum_tensor(f"pbW{i}", [128, 1024], F32)) for i in range(4)]
        pb = [pbW[i // 2][:, (i % 2) * 512:(i % 2) * 512 + 512] for i in range(8)]

        sems = {}
        counts = {}
        ops = {e: [] for e in ("sync", "vector", "scalar", "gpsimd", "tensor")}
        lastw = {e: {} for e in ops}

        def _sem(name):
            if name not in sems:
                sems[name] = ctx.enter_context(nc.semaphore("s_" + name))
                counts[name] = 0
            return sems[name]

        def emit(engine, fn):
            ops[engine].append(fn)

        def wait(engine, sem_name, val):
            if val is None or val <= 0:
                return
            s = _sem(sem_name)
            if lastw[engine].get(sem_name, 0) >= val:
                return
            lastw[engine][sem_name] = val
            emit(engine, lambda e, s=s, v=val: e.wait_ge(s, v))

        def _mk(engine, sem_name, by, fn):
            s = _sem(sem_name)
            counts[sem_name] += by
            emit(engine, lambda e, f=fn, s=s, b=by: f(e).then_inc(s, b))
            return counts[sem_name]

        def sdma(st, out, in_, **kw):
            return _mk("sync", st, 16, lambda e: e.dma_start(out=out, in_=in_, **kw))

        def adma(st, out, in_, **kw):
            return _mk("scalar", st, 16, lambda e: e.dma_start(out=out, in_=in_, **kw))

        def gdma(st, out, in_, **kw):
            return _mk("gpsimd", st, 16, lambda e: e.dma_start(out=out, in_=in_, **kw))

        def pe(fn):
            return _mk("tensor", "pe", 1, fn)

        def dve(fn):
            if strict:
                wait("vector", "dv", counts.get("dv", 0))
            return _mk("vector", "dv", 1, fn)

        def act(fn):
            if strict:
                wait("scalar", "ac", counts.get("ac", 0))
            return _mk("scalar", "ac", 1, fn)

        def stx(st, out, in_):
            return _mk("sync", st, 16, lambda e: e.dma_start(out=out, in_=in_))

        def MM(bank_ap, lhsT, rhs, start, tile=None):
            return pe(lambda e: e.matmul(bank_ap, lhsT, rhs, start=start, stop=True,
                                         skip_group_check=not start,
                                         tile_position=tile))

        def pe_dv_barrier():
            wait("tensor", "dv", counts.get("dv", 0))

        def pe_ac_barrier():
            wait("tensor", "ac", counts.get("ac", 0))

        phase_marks = []
        dbg_out["phase_marks"] = phase_marks

        def mark_phase(name):
            phase_marks.append((name, {e: len(ops[e]) for e in ops}))

        def all_engine_barrier():
            snap = dict(counts)
            for e in ops:
                for name, val in snap.items():
                    wait(e, name, val)

        bc_cell = [None]

        def _init_bc(e):
            bc_cell[0] = e.alloc_register("bc_reg")
            return e.reg_mov(bc_cell[0], CAP - 1)

        emit("gpsimd", lambda e: _init_bc(e))

        sdma("cs", id_bf[:], ident_bf_d[:, :])
        sdma("cs", id_f32[:], ident_f32_d[:, :])
        sdma("cs", onesrow_sb[:], onesrow_d[:, :])
        sdma("cs", ones128_sb[:], ones128_d[:, :])
        sdma("cs", ones64c_sb[:], ones64c_d[:, :])
        sdma("cs", ones2T_sb[:], ones2T_d[:, :])
        sdma("cs", causU_sb[:], causU_d[:, :])
        sdma("cs", causL_sb[:], causL_d[:, :])
        sdma("cs", dmask_sb[:], dmask_d[:, :])
        sdma("cs", fmb_sb[:], fmb_d[:, :])
        sdma("cs", gr_sb[:], g12_d[:, :])
        sdma("cs", wr_row[:], wrouter_d[:, :])
        v_c = sdma("cs", ab_sb[:], ab_raw[:, :])

        wait("tensor", "cs", v_c)
        for h in range(D // 512):
            b = pb[6 + (h % 2)]
            v = pe(lambda e, h=h, b=b: e.matmul(b[:, :], onesrow_sb[0:1, :],
                                                wr_row[0:1, h * 512:(h + 1) * 512],
                                                start=True, stop=True))
            wait("vector", "pe", v)
            dve(lambda e, h=h, b=b: e.tensor_copy(wrb_sb[:, h * 512:(h + 1) * 512], b[:, :]))
        pe_dv_barrier()
        v = pe(lambda e: e.matmul(pb[6][0:128, 0:16], gr_sb[:], id_f32[0:16, 0:16],
                                  start=True, stop=True))
        wait("vector", "pe", v)
        dve(lambda e: e.tensor_copy(g1c_sb[:], pb[6][:, 0:8]))
        v_g12 = dve(lambda e: e.tensor_copy(g2c_sb[:], pb[6][:, 8:16]))

        wait("scalar", "cs", v_c)
        v = act(lambda e: e.activation(ab_sb[:], ab_sb[:], AF.Sigmoid))
        wait("vector", "ac", v)
        dve(lambda e: e.tensor_scalar(ab_sb[:, 0:H], ab_sb[:, 0:H], 0.5, None, ALU.mult))
        v_ab = dve(lambda e: e.tensor_scalar(ab_sb[:, H:2 * H], ab_sb[:, H:2 * H], 2.0, None, ALU.mult))
        wait("tensor", "dv", v_ab)
        pe_dv_barrier()
        v = pe(lambda e: e.matmul(pb[6][0:128, 0:32], onesrow_sb[0:1, 0:128],
                                  ab_sb[0:1, 0:32], start=True, stop=True))
        wait("vector", "pe", v)
        dve(lambda e: e.tensor_copy(abH_sb[:], pb[6][0:128, 0:32]))
        for h in range(H):
            dve(lambda e, h=h: e.tensor_tensor(
                mUL_sb[:, h, 0, :], causU_sb[:], abH_sb[0:64, h:h + 1].to_broadcast([64, C]), ALU.mult))
            dve(lambda e, h=h: e.tensor_tensor(
                mUL_sb[:, h, 1, :], causL_sb[:], abH_sb[0:64, h:h + 1].to_broadcast([64, C]), ALU.mult))
        v_consts_dv = counts.get("dv", 0)

        def ckpt(n):
            if PH < n:
                raise StopEmission()
        try:
          for r in range(R):
            ckpt(1)
            mark_phase(f'ln_r{r}')
            ar.reset(0)
            coT_sb = ar.al([128, 8, T], BF16)
            mk_wo = ar.mark()
            qT_sb = ar.al([128, T], BF16)
            kTn_sb = ar.al([128, T], BF16)
            vT_sb = ar.al([128, T], BF16)
            kraw_sb = kTn_sb
            wsl = [[ar.al([128, 8, 128], BF16) for _ in range(2)] for _ in range(3)]
            nrm_sb = ar.al([2, T], F32)
            rn_sb = ar.al([2, T], F32)
            ln_bytes = (2 * 4 * D + 2 * 4 * D + 2 * 2 * D + 4 * D
                        + 4 * max(T, D) + 4 * T + 4 * T + 1024)
            if ar.off < ln_bytes:
                ar.reset(ln_bytes)
            xnT = ar.al([128, 8, T], BF16)
            mk_kn = ar.mark()
            ksq_sb = ar.al([128, T], F32)
            rbc_sb = ar.al([128, T], F32)
            ar.reset(mk_kn)
            kaT_sb = ar.al([128, T], BF16)
            ar.reset(0)
            x_t = [ar.al([128, D], F32) for _ in range(2)]
            xn_t = [ar.al([128, D], F32) for _ in range(2)]
            xnb_t = [ar.al([128, D], BF16) for _ in range(2)]
            xsq_t = ar.al([128, D], F32)
            trash = ar.al([128, max(T, D)], F32)
            sbcast = ar.al([128, T], F32)
            srow = ar.al([1, T], F32)

            all_engine_barrier()
            v_sqred_prev = 0
            v_xn_st = {}
            xt_free = {0: (0, 0), 1: (0, 0)}
            xn_free = {0: (0, 0, 0), 1: (0, 0, 0)}
            for j in range(NT):
                bi = j % 2
                dv0, ac0 = xt_free[bi]
                wait("sync", "dv", dv0)
                wait("sync", "ac", ac0)
                v_x = sdma(f"x{bi}", x_t[bi][:], x_in[r, j * 128:(j + 1) * 128, :])
                wait("scalar", f"x{bi}", v_x)
                wait("scalar", "dv", v_sqred_prev)
                v_sq = act(lambda e, bi=bi: e.activation(xsq_t[:], x_t[bi][:], AF.Square,
                                                         accum_out=stat[:, 1:2]))
                wait("vector", f"x{bi}", v_x)
                ld0, gp0, db0 = xn_free[bi]
                wait("vector", f"xs{bi}", ld0)
                wait("vector", f"xb{bi}", gp0)
                wait("vector", f"db{bi}", db0)
                dve(lambda e, bi=bi: e.reduce_sum(stat[:, 0:1], x_t[bi][:], axis=AX.X))
                wait("vector", "ac", v_sq)
                dve(lambda e: e.tensor_scalar(stat[:, 2:3], stat[:, 0:1], 1.0 / D, None, ALU.mult))
                v_sqred_prev = dve(lambda e: e.tensor_scalar(stat[:, 3:4], stat[:, 1:2], 1.0 / D, None, ALU.mult))
                dve(lambda e: e.tensor_tensor(stat[:, 4:5], stat[:, 2:3], stat[:, 2:3], ALU.mult))
                dve(lambda e: e.tensor_tensor(stat[:, 5:6], stat[:, 3:4], stat[:, 4:5], ALU.subtract))
                v_var = dve(lambda e: e.tensor_scalar(stat[:, 5:6], stat[:, 5:6], 1e-5, None, ALU.add))
                wait("scalar", "dv", v_var)
                v_std = act(lambda e: e.activation(stat[:, 6:7], stat[:, 5:6], AF.Sqrt))
                wait("vector", "ac", v_std)
                dve(lambda e: e.reciprocal(stat[:, 7:8], stat[:, 6:7]))
                v_xn = dve(lambda e, bi=bi: e.tensor_scalar(
                    xn_t[bi][:], x_t[bi][:], stat[:, 2:3], stat[:, 7:8], ALU.subtract, ALU.mult))
                dve(lambda e, bi=bi: e.tensor_tensor(trash[:, 0:D], xn_t[bi][:], wrb_sb[:], ALU.mult))
                dve(lambda e, j=j: e.reduce_sum(sccol[:, j:j + 1], trash[:, 0:D], axis=AX.X))
                v_cast = dve(lambda e, bi=bi: e.tensor_copy(xnb_t[bi][:], xn_t[bi][:]))
                wait("sync", "dv", v_xn)
                v_xn_st[j] = sdma(f"xs{bi}", xn_dram[r, j * 128:(j + 1) * 128, :], xn_t[bi][:])
                wait("gpsimd", "dv", v_cast)
                v_xnb = gdma(f"xb{bi}", xnb_dram[r, j * 128:(j + 1) * 128, :], xnb_t[bi][:])
                v_dbg = 0
                if dbg:
                    v_dbg = stx(f"db{bi}", dbg_out["xn"][r, j * 128:(j + 1) * 128, :], xn_t[bi][:])
                xt_free[bi] = (v_cast, v_sq)
                xn_free[bi] = (v_xn_st[j], v_xnb, v_dbg)
            v_scores = counts.get("dv", 0)

            wait("scalar", "xb0", counts.get("xb0", 0))
            wait("scalar", "xb1", counts.get("xb1", 0))
            v_xnT = 0
            for j in range(8):
                v_xnT = adma("xnt", xnT[:, j, :], xnb_dram[r, :, j * 128:(j + 1) * 128], transpose=True)

            ckpt(2)
            mark_phase(f'ranks_r{r}')
            wait("tensor", "dv", v_scores)
            v = pe(lambda e: e.matmul(pb[6][0:NT, 0:128], sccol[:], id_f32[:, 0:128],
                                      start=True, stop=True))
            wait("vector", "pe", v)
            v = dve(lambda e: e.tensor_copy(scT_sb[0:NT, :], pb[6][0:NT, 0:128]))
            wait("sync", "dv", v)
            v = sdma("scm", sc_dram[r], scT_sb[0:NT, :])
            wait("sync", "scm", v)
            v_srow = sdma("scm", srow[:, :], sc_dram[r].rearrange("j p -> (j p)").unsqueeze(0))
            wait("tensor", "scm", v_srow)
            for h in range(T // 512):
                b = pb[6 + (h % 2)]
                pe_dv_barrier()
                v = pe(lambda e, h=h, b=b: e.matmul(b[:, :], onesrow_sb[0:1, :],
                                                    srow[0:1, h * 512:(h + 1) * 512],
                                                    start=True, stop=True))
                wait("vector", "pe", v)
                dve(lambda e, h=h, b=b: e.tensor_copy(sbcast[:, h * 512:(h + 1) * 512], b[:, :]))
            wait("vector", "dv", counts.get("dv", 0))
            for j in range(NT):
                dve(lambda e, j=j: e.tensor_scalar(
                    trash[:, 0:T], sbcast[:, 0:T], sccol[:, j:j + 1], 0.0,
                    ALU.is_gt, ALU.add, accum_out=rkcol[:, j:j + 1]))
            v_rk = dve(lambda e: e.tensor_copy(rki32[:], rkcol[:]))
            if dbg:
                wait("sync", "dv", v_rk)
                stx("dbg", dbg_out["ranks"][r], rkcol[:])
            wait("gpsimd", "dv", v_rk)
            wait("gpsimd", "xs0", counts.get("xs0", 0))
            wait("gpsimd", "xs1", counts.get("xs1", 0))
            wait("sync", "xs0", counts.get("xs0", 0))
            wait("sync", "xs1", counts.get("xs1", 0))
            wait("sync", "xb0", counts.get("xb0", 0))
            wait("sync", "xb1", counts.get("xb1", 0))
            wait("sync", "db0", counts.get("db0", 0))
            wait("sync", "db1", counts.get("db1", 0))
            v_scat = {0: 0, 1: 0}
            for j in range(NT):
                bi = j % 2
                wait("sync", f"sc{bi}", v_scat[bi])
                v_stage = sdma(f"x{bi}", xn_t[bi][:], xn_dram[r, j * 128:(j + 1) * 128, :])
                wait("gpsimd", f"x{bi}", v_stage)
                v_scat[bi] = _mk("gpsimd", f"sc{bi}", 16, lambda e, j=j, bi=bi, r=r: e.indirect_dma_start(
                    out=xsel_dram[r],
                    out_offset=IndirectOffsetOnAxis(rki32[:, j:j + 1], 0),
                    in_=xn_t[bi][:],
                    in_offset=None,
                    bounds_check=bc_cell[0],
                    oob_is_err=False))

            ckpt(3)
            mark_phase(f'dn_setup_r{r}')
            all_engine_barrier()
            ckpt(3.05)

            wait("tensor", "xnt", v_xnT)
            wait("tensor", "dv", v_consts_dv)
            wsl_free = {}
            for hp in range(8):
                bi = hp % 2
                if dbg:
                    wait("vector", "dbg", counts.get("dbg", 0))
                wait("gpsimd", "pe", wsl_free.get(bi, 0))
                gdma(f"wa{bi}", wsl[0][bi][:], wqT_d.rearrange("(k p) o -> p k o", p=128)[:, :, hp * 128:(hp + 1) * 128])
                gdma(f"wa{bi}", wsl[1][bi][:], wkT_d.rearrange("(k p) o -> p k o", p=128)[:, :, hp * 128:(hp + 1) * 128])
                v_w = gdma(f"wa{bi}", wsl[2][bi][:], wvT_d.rearrange("(k p) o -> p k o", p=128)[:, :, hp * 128:(hp + 1) * 128])
                wait("tensor", f"wa{bi}", v_w)
                ckpt(3.3)

                bank_tok = {0: (counts.get("dv", 0), counts.get("ac", 0)),
                            1: (counts.get("dv", 0), counts.get("ac", 0))}

                def proj_pass(wi, evac):
                    for tt in range(T // 512):
                        b = pb[6 + (tt % 2)]
                        tdv, tac = bank_tok[tt % 2]
                        wait("tensor", "dv", tdv)
                        wait("tensor", "ac", tac)
                        ckpt(3.31)
                        for kt in range(8):
                            pe(lambda e, b=b, kt=kt, tt=tt, wi=wi, bi=bi: e.matmul(
                                b[:, :], wsl[wi][bi][:, kt, :],
                                xnT[:, kt, tt * 512:(tt + 1) * 512],
                                start=(kt == 0), stop=(kt == 7)))
                            ckpt(3.32 + kt / 100.0)
                        ckpt(3.33)
                        evac(b, tt, counts.get("pe", 0))
                        bank_tok[tt % 2] = (counts.get("dv", 0), counts.get("ac", 0))
                        ckpt(3.35)

                def ev_q(b, tt, v):
                    wait("vector", "pe", v)
                    dve(lambda e, b=b, tt=tt: e.tensor_copy(qT_sb[:, tt * 512:(tt + 1) * 512], b[:, :]))

                def ev_k(b, tt, v):
                    wait("vector", "pe", v)
                    vc = dve(lambda e, b=b, tt=tt: e.tensor_copy(kraw_sb[:, tt * 512:(tt + 1) * 512], b[:, :]))
                    wait("scalar", "dv", vc)
                    act(lambda e, b=b, tt=tt: e.activation(ksq_sb[:, tt * 512:(tt + 1) * 512], b[:, :], AF.Square))

                def ev_v(b, tt, v):
                    wait("vector", "pe", v)
                    dve(lambda e, b=b, tt=tt: e.tensor_copy(vT_sb[:, tt * 512:(tt + 1) * 512], b[:, :]))

                proj_pass(0, ev_q)
                ckpt(3.4)
                proj_pass(1, ev_k)
                proj_pass(2, ev_v)
                wsl_free[bi] = counts.get("pe", 0)
                ckpt(3.5)

                pe_ac_barrier()
                pe_dv_barrier()
                for tt in range(T // 512):
                    v = pe(lambda e, tt=tt: e.matmul(
                        pb[5][0:2, 0:512], ones64c_sb[:], ksq_sb[:, tt * 512:(tt + 1) * 512],
                        start=True, stop=True, skip_group_check=True))
                    wait("vector", "pe", v)
                    v_ev = dve(lambda e, tt=tt: e.tensor_copy(nrm_sb[:, tt * 512:(tt + 1) * 512], pb[5][0:2, 0:512]))
                    wait("tensor", "dv", v_ev)
                wait("scalar", "dv", counts.get("dv", 0))
                v = act(lambda e: e.activation(nrm_sb[:], nrm_sb[:], AF.Sqrt))
                wait("vector", "ac", v)
                dve(lambda e: e.tensor_scalar(nrm_sb[:], nrm_sb[:], 1e-12, None, ALU.max))
                v_rn = dve(lambda e: e.reciprocal(rn_sb[:], nrm_sb[:]))
                wait("tensor", "dv", v_rn)
                for tt in range(T // 512):
                    b = pb[6 + (tt % 2)]
                    pe_dv_barrier()
                    v = pe(lambda e, b=b, tt=tt: e.matmul(
                        b[:, :], ones2T_sb[:], rn_sb[:, tt * 512:(tt + 1) * 512],
                        start=True, stop=True))
                    wait("vector", "pe", v)
                    dve(lambda e, b=b, tt=tt: e.tensor_copy(rbc_sb[:, tt * 512:(tt + 1) * 512], b[:, :]))
                dve(lambda e: e.tensor_tensor(kTn_sb[:], kTn_sb[:], rbc_sb[:], ALU.mult))
                dve(lambda e, hp=hp: e.tensor_tensor(
                    kaT_sb[0:64, :], kTn_sb[0:64, :],
                    abH_sb[0:64, 2 * hp:2 * hp + 1].to_broadcast([64, T]), ALU.mult))
                dve(lambda e, hp=hp: e.tensor_tensor(
                    kaT_sb[64:128, :], kTn_sb[64:128, :],
                    abH_sb[64:128, 2 * hp + 1:2 * hp + 2].to_broadcast([64, T]), ALU.mult))
                dve(lambda e: e.memset(M32[:], 0.0))
                v_state0 = dve(lambda e: e.memset(M16[:], 0.0))
                ckpt(3.7)

                if dbg and hp == 0:
                    wait("sync", "dv", v_state0)
                    stx("dbg", dbg_out["qT0"][r], qT_sb[:])
                    stx("dbg", dbg_out["kTn0"][r], kTn_sb[:])

                ckpt(4)
                mark_phase(f'chunks_r{r}')
                wait("tensor", "dv", v_state0)
                chst = [dict() for _ in range(NCH)]
                v_seq_st = {-1: v_state0}
                bpair = abH_sb[0:64, 16 + 2 * hp:16 + 2 * hp + 2].unsqueeze(2).to_broadcast([64, 2, C])
                bpair2d = abH_sb[0:64, 16 + 2 * hp:16 + 2 * hp + 2]
                mULp = mUL_sb[:, 2 * hp:2 * hp + 2, :, :]
                dmask2 = dmask_sb[:].unsqueeze(1).to_broadcast([64, 2, C])

                def gen1_mms(ci):
                    cs = ci * C
                    par = ci % 2
                    b0 = pb[par * 2]
                    b1 = pb[par * 2 + 1]
                    qTa = qT_sb[0:64, cs:cs + C]
                    qTb = qT_sb[64:128, cs:cs + C]
                    kTa = kTn_sb[0:64, cs:cs + C]
                    kTb = kTn_sb[64:128, cs:cs + C]
                    vTa = vT_sb[0:64, cs:cs + C]
                    vTb = vT_sb[64:128, cs:cs + C]
                    if ci >= 2:
                        wait("tensor", "dv", chst[ci - 2].get("dv_chain", 0))
                        wait("tensor", "dv", chst[ci - 2].get("dv_attnT", 0))
                        wait("tensor", "ac", chst[ci - 2].get("ac_attnT", 0))
                        if DMA_CHAIN:
                            wait("tensor", f"gl{ci % 2}", chst[ci - 2].get("gd_chain", 0))
                    MM(b1[0:64, 0:64], kTb, kTb, True, (64, 0))
                    MM(b1[0:64, 64:128], qTb, kTb, False, (64, 0))
                    MM(b1[0:64, 128:192], kTb, id_bf[64:128, 64:128], False, (64, 0))
                    MM(b1[0:64, 192:256], vTb, id_bf[64:128, 64:128], False, (64, 0))
                    MM(b0[0:64, 0:64], kTa, kTa, True, (0, 0))
                    MM(b0[0:64, 64:128], qTa, kTa, False, (0, 0))
                    MM(b0[0:64, 128:192], kTa, id_bf[0:64, 0:64], False, (0, 0))
                    v = MM(b0[0:64, 192:256], vTa, id_bf[0:64, 0:64], False, (0, 0))
                    chst[ci]["pe_gen1"] = v

                def gen1_evacs(ci):
                    par = ci % 2
                    pari = ci % 4
                    st = chst[ci]
                    bsp = pbW[par][0:64, :].rearrange("p (h c) -> p h c", h=2)
                    Gs4 = bsp[:, :, 0:C].unsqueeze(2).to_broadcast([64, 2, 2, C])
                    NL4 = chNWL[:, pari, :, C:3 * C].rearrange("p h (o c) -> p h o c", o=2)
                    Uc = chNWL[:, pari, :, 0:C]
                    VB2c = VB2[:, pari, :, :]
                    at2 = attn2[:, pari, :, :]
                    if dbg:
                        wait("vector", "dbg", counts.get("dbg", 0))
                    wait("vector", "pe", st["pe_gen1"])
                    if NO_4D:
                        dve(lambda e, bsp=bsp, pari=pari, mULp=mULp: e.tensor_tensor(
                            chNWL[:, pari, :, C:2 * C], bsp[:, :, 0:C], mULp[:, :, 0, :], ALU.mult))
                        dve(lambda e, bsp=bsp, pari=pari, mULp=mULp: e.tensor_tensor(
                            chNWL[:, pari, :, 2 * C:3 * C], bsp[:, :, 0:C], mULp[:, :, 1, :], ALU.mult))
                    else:
                        dve(lambda e, Gs4=Gs4, NL4=NL4, mULp=mULp: e.tensor_tensor(NL4, Gs4, mULp, ALU.mult))
                    dve(lambda e, bsp=bsp, Uc=Uc: e.tensor_copy(Uc, bsp[:, :, 128:192]))
                    dve(lambda e, bsp=bsp, VB2c=VB2c, bpair=bpair: e.tensor_tensor(VB2c, bsp[:, :, 192:256], bpair, ALU.mult))
                    v = dve(lambda e, bsp=bsp: e.scalar_tensor_tensor(
                        sm_sb[:], bsp[:, :, 64:128], 0.125, dmask2, ALU.mult, ALU.add))
                    wait("scalar", "dv", v)
                    act(lambda e: e.activation(sm_sb[:, 0, :], sm_sb[:, 0, :], AF.Exp,
                                               accum_out=ssum[:, 0:1]))
                    v = act(lambda e: e.activation(sm_sb[:, 1, :], sm_sb[:, 1, :], AF.Exp,
                                                   accum_out=ssum[:, 1:2]))
                    wait("vector", "ac", v)
                    dve(lambda e, bp2=bpair2d: e.tensor_tensor(ssum[:], ssum[:], bp2, ALU.mult))
                    dve(lambda e: e.reciprocal(srec[:], ssum[:]))
                    v = dve(lambda e, at2=at2: e.tensor_tensor(at2, sm_sb[:],
                                                               srec[:].unsqueeze(2).to_broadcast([64, 2, C]), ALU.mult))
                    st["dv_par"] = v

                def chain_lvl_mms(ci, li, lvl):
                    par = ci % 2
                    pari = ci % 4
                    b = pb[par * 2 + ((li + 1) % 2)]
                    st = chst[ci]
                    wait("tensor", "dv", st.get("dv_chain", st["dv_par"]))
                    wait("tensor", "ac", st.get("ac_chain", 0))
                    if DMA_CHAIN:
                        wait("tensor", f"gl{ci % 2}", st.get("gd_chain", 0))
                    UN_a = chNWL[:, pari, 0, 0:2 * C]
                    UN_b = chNWL[:, pari, 1, 0:2 * C]
                    N_a = chNWL[:, pari, 0, C:2 * C]
                    N_b = chNWL[:, pari, 1, C:2 * C]
                    L_a = chNWL[:, pari, 0, 2 * C:3 * C]
                    L_b = chNWL[:, pari, 1, 2 * C:3 * C]
                    if lvl < 16:
                        MM(b[0:64, 0:128], L_a, UN_a, True, (0, 0))
                        v = MM(b[0:64, 128:256], L_b, UN_b, False, (0, 0))
                        MM(b[0:64, 256:320], N_a, L_a, False, (0, 0))
                        v = MM(b[0:64, 320:384], N_b, L_b, False, (0, 0))
                    elif lvl == 16:
                        MM(b[0:64, 0:64], L_a, chNWL[:, pari, 0, 0:C], True, (0, 0))
                        v = MM(b[0:64, 64:128], L_b, chNWL[:, pari, 1, 0:C], False, (0, 0))
                        MM(b[0:64, 256:320], N_a, L_a, False, (0, 0))
                        v = MM(b[0:64, 320:384], N_b, L_b, False, (0, 0))
                    else:
                        MM(b[0:64, 0:64], L_a, chNWL[:, pari, 0, 0:C], True, (0, 0))
                        v = MM(b[0:64, 64:128], L_b, chNWL[:, pari, 1, 0:C], False, (0, 0))
                    st["pe_chain"] = v

                def chain_lvl_evacs(ci, li, lvl):
                    par = ci % 2
                    pari = ci % 4
                    b = pb[par * 2 + ((li + 1) % 2)]
                    st = chst[ci]
                    Uc = chNWL[:, pari, :, 0:C]
                    sgn = -1.0 if lvl == 1 else 1.0
                    if dbg:
                        wait("vector", "dbg", counts.get("dbg", 0))
                        wait("scalar", "dbg", counts.get("dbg", 0))
                    wait("vector", "pe", st["pe_chain"])
                    if lvl < 16:
                        v = dve(lambda e, b=b, sgn=sgn, Uc=Uc: e.scalar_tensor_tensor(
                            Uc, b[0:64, 0:256].rearrange("p (h c) -> p h c", h=2)[:, :, 0:C],
                            sgn, Uc, ALU.mult, ALU.add))
                        if NO_ACT_CHAIN:
                            if DMA_CHAIN:
                                wait("gpsimd", "pe", st["pe_chain"])
                                wait("gpsimd", "dv", v)
                                st["gd_chain"] = gdma(
                                    f"gl{ci % 2}", chNWL[:, pari, :, 2 * C:3 * C],
                                    b[0:64, 256:384].rearrange("p (h c) -> p h c", h=2))
                                v = dve(lambda e, b=b, pari=pari: e.tensor_copy(
                                    chNWL[:, pari, :, C:2 * C],
                                    b[0:64, 0:256].rearrange("p (h c) -> p h c", h=2)[:, :, C:2 * C]))
                            else:
                                dve(lambda e, b=b, pari=pari: e.tensor_copy(
                                    chNWL[:, pari, :, C:2 * C],
                                    b[0:64, 0:256].rearrange("p (h c) -> p h c", h=2)[:, :, C:2 * C]))
                                v = dve(lambda e, b=b, pari=pari: e.tensor_copy(
                                    chNWL[:, pari, :, 2 * C:3 * C],
                                    b[0:64, 256:384].rearrange("p (h c) -> p h c", h=2)))
                        else:
                            wait("scalar", "dv", v)
                            wait("scalar", "pe", st["pe_chain"])
                            act(lambda e, b=b, pari=pari: e.activation(
                                chNWL[:, pari, :, C:2 * C],
                                b[0:64, 0:256].rearrange("p (h c) -> p h c", h=2)[:, :, C:2 * C], AF.Copy))
                            va = act(lambda e, b=b, pari=pari: e.activation(
                                chNWL[:, pari, :, 2 * C:3 * C],
                                b[0:64, 256:384].rearrange("p (h c) -> p h c", h=2), AF.Copy))
                            st["ac_chain"] = va
                    elif lvl == 16:
                        v = dve(lambda e, b=b, Uc=Uc: e.scalar_tensor_tensor(
                            Uc, b[0:64, 0:128].rearrange("p (h c) -> p h c", h=2),
                            1.0, Uc, ALU.mult, ALU.add))
                        if NO_ACT_CHAIN:
                            if DMA_CHAIN:
                                wait("gpsimd", "pe", st["pe_chain"])
                                wait("gpsimd", "dv", v)
                                st["gd_chain"] = gdma(
                                    f"gl{ci % 2}", chNWL[:, pari, :, 2 * C:3 * C],
                                    b[0:64, 256:384].rearrange("p (h c) -> p h c", h=2))
                            else:
                                v = dve(lambda e, b=b, pari=pari: e.tensor_copy(
                                    chNWL[:, pari, :, 2 * C:3 * C],
                                    b[0:64, 256:384].rearrange("p (h c) -> p h c", h=2)))
                        else:
                            wait("scalar", "dv", v)
                            wait("scalar", "pe", st["pe_chain"])
                            va = act(lambda e, b=b, pari=pari: e.activation(
                                chNWL[:, pari, :, 2 * C:3 * C],
                                b[0:64, 256:384].rearrange("p (h c) -> p h c", h=2), AF.Copy))
                            st["ac_chain"] = va
                    else:
                        v = dve(lambda e, b=b, Uc=Uc: e.scalar_tensor_tensor(
                            Uc, b[0:64, 0:128].rearrange("p (h c) -> p h c", h=2),
                            1.0, Uc, ALU.mult, ALU.add))
                    st["dv_chain"] = v

                def attnT_mms(ci):
                    par = ci % 2
                    pari = ci % 4
                    b = pb[par * 2 + 1]
                    st = chst[ci]
                    wait("tensor", "dv", st["dv_chain"])
                    wait("tensor", "ac", st.get("ac_chain", 0))
                    MM(b[0:64, 384:448], attn2[:, pari, 0, :], id_bf[0:64, 0:64], False, (0, 0))
                    st["pe_attnT"] = MM(b[0:64, 448:512], attn2[:, pari, 1, :], id_bf[0:64, 0:64], False, (0, 0))

                def attnT_evac(ci):
                    par = ci % 2
                    pari = ci % 4
                    b = pb[par * 2 + 1]
                    st = chst[ci]
                    if NO_ACT_ATTNT:
                        wait("vector", "pe", st["pe_attnT"])
                        st["dv_attnT"] = dve(lambda e, b=b, pari=pari: e.tensor_copy(
                            attnT2[:, pari, :, :],
                            b[0:64, 384:512].rearrange("p (h c) -> p h c", h=2)))
                        st["ac_attnT"] = 0
                        return
                    if dbg:
                        wait("scalar", "dbg", counts.get("dbg", 0))
                    wait("scalar", "pe", st["pe_attnT"])
                    st["ac_attnT"] = act(lambda e, b=b, pari=pari: e.activation(
                        attnT2[:, pari, :, :],
                        b[0:64, 384:512].rearrange("p (h c) -> p h c", h=2), AF.Copy))

                def seq1_mms(ci):
                    cs = ci * C
                    pari = ci % 4
                    st = chst[ci]
                    qTa = qT_sb[0:64, cs:cs + C]
                    qTb = qT_sb[64:128, cs:cs + C]
                    kaTa = kaT_sb[0:64, cs:cs + C]
                    kaTb = kaT_sb[64:128, cs:cs + C]
                    VB2c = VB2[:, pari, :, :]
                    aT2c = attnT2[:, pari, :, :]
                    wait("tensor", "dv", v_seq_st[ci - 1])
                    wait("tensor", "dv", st.get("dv_attnT", 0))
                    wait("tensor", "ac", st["ac_attnT"])
                    MM(pb[4][0:64, 0:64], kaTa, M16[0:64, 1, :], True, (0, 0))
                    MM(pb[5][0:64, 0:64], kaTb, M16[64:128, 1, :], True, (64, 0))
                    MM(pb[4][0:64, 64:128], VB2c[:, 0, :], aT2c[:, 0, :], False, (0, 0))
                    MM(pb[4][64:128, 64:128], VB2c[:, 1, :], aT2c[:, 1, :], True, (0, 64))
                    MM(pb[4][0:64, 256:320], M16[0:64, 0, :], qTa, False, (0, 0))
                    v = MM(pb[5][64:128, 64:128], M16[64:128, 0, :], qTb, True, (64, 64))
                    st["pe_seq1"] = v

                def seq1_dve(ci):
                    cs = ci * C
                    pari = ci % 4
                    st = chst[ci]
                    VB2c = VB2[:, pari, :, :]
                    AtSp = pbW[2][0:64, :].rearrange("p (h c) -> p h c", h=2)[:, :, 0:C]
                    if dbg:
                        wait("vector", "dbg", counts.get("dbg", 0))
                    wait("vector", "pe", st["pe_seq1"])
                    v_yt = dve(lambda e, VB2c=VB2c, AtSp=AtSp: e.tensor_tensor(Yt_sb[:], VB2c, AtSp, ALU.subtract))
                    dve(lambda e: e.tensor_copy(co_f[:, :], pb[4][:, 64:128]))
                    dve(lambda e, cs=cs, hp=hp: e.scalar_tensor_tensor(
                        coT_sb[0:64, hp, cs:cs + C], pb[4][0:64, 256:320], 0.1, co_f[0:64, :], ALU.mult, ALU.add))
                    dve(lambda e, cs=cs, hp=hp: e.scalar_tensor_tensor(
                        coT_sb[64:128, hp, cs:cs + C], pb[5][64:128, 64:128], 0.1, co_f[64:128, :], ALU.mult, ALU.add))
                    st["dv_yt"] = v_yt

                def seq2_mms(ci):
                    pari = ci % 4
                    st = chst[ci]
                    U2a = chNWL[:, pari, 0, 0:C]
                    U2b = chNWL[:, pari, 1, 0:C]
                    wait("tensor", "dv", st["dv_yt"])
                    MM(pb[6][0:64, 0:64], Yt_sb[:, 0, :], U2a, True, (0, 0))
                    MM(pb[6][64:128, 0:64], Yt_sb[:, 1, :], U2b, True, (0, 64))
                    MM(pb[6][0:64, 64:128], U2a, Yt_sb[:, 0, :], False, (0, 0))
                    v = MM(pb[6][64:128, 64:128], U2b, Yt_sb[:, 1, :], False, (0, 64))
                    st["pe_seq2"] = v

                def seq2_dve(ci):
                    st = chst[ci]
                    wait("vector", "pe", st["pe_seq2"])
                    dve(lambda e: e.tensor_tensor(
                        M32[:], M32[:], pb[6][:, 0:128].rearrange("p (o c) -> p o c", o=2), ALU.add))
                    v = dve(lambda e: e.tensor_copy(M16[:], M32[:]))
                    v_seq_st[ci] = v

                pend = None
                for sc in range(NCH // 2):
                    c0, c1 = 2 * sc, 2 * sc + 1
                    gen1_mms(c0)
                    gen1_mms(c1)
                    if NO_WEAVE:
                        gen1_evacs(c0)
                        gen1_evacs(c1)
                        for li, lvl in enumerate((1, 2, 4, 8, 16, 32)):
                            chain_lvl_mms(c0, li, lvl)
                            chain_lvl_mms(c1, li, lvl)
                            chain_lvl_evacs(c0, li, lvl)
                            chain_lvl_evacs(c1, li, lvl)
                        attnT_mms(c0)
                        attnT_mms(c1)
                        attnT_evac(c0)
                        attnT_evac(c1)
                        for p in (c0, c1):
                            seq1_mms(p)
                            seq1_dve(p)
                            seq2_mms(p)
                            seq2_dve(p)
                        continue
                    if pend is None:
                        gen1_evacs(c0)
                        if dbg and hp == 0 and r == 0:
                            wait("sync", "dv", counts.get("dv", 0))
                            stx("dbg", dbg_out["probe16"][0:64, 1664:1728], chNWL[:, 0, 1, C:2 * C])
                        gen1_evacs(c1)
                    else:
                        p0, p1 = pend
                        seq1_mms(p0)
                        seq1_dve(p0)
                        seq2_mms(p0)
                        seq2_dve(p0)
                        gen1_evacs(c0)
                        if dbg and hp == 0 and r == 0 and p0 == 0:
                            wait("sync", "dv", counts.get("dv", 0))
                            wait("sync", "ac", counts.get("ac", 0))
                            stx("dbg", dbg_out["probe16"][0:64, 64:128], chNWL[:, 0, 0, 0:C])
                            stx("dbg", dbg_out["probe16"][0:64, 192:256], VB2[:, 0, 0, :])
                            stx("dbg", dbg_out["probe16"][0:64, 256:320], Yt_sb[:, 0, :])
                            stx("dbg", dbg_out["probe16"][0:64, 320:384], attn2[:, 0, 0, :])
                            stx("dbg", dbg_out["probe16"][0:64, 384:448], attnT2[:, 0, 0, :])
                            stx("dbg", dbg_out["probe32"][0:64, 0:64], M32[0:64, 0, :])
                            stx("dbg", dbg_out["probe32"][0:64, 64:128], M32[0:64, 1, :])
                            stx("dbg", dbg_out["probe16"][0:64, 1728:1792], chNWL[:, 0, 1, 0:C])
                            stx("dbg", dbg_out["probe16"][0:64, 1792:1856], VB2[:, 0, 1, :])
                            stx("dbg", dbg_out["probe16"][0:64, 1856:1920], Yt_sb[:, 1, :])
                            stx("dbg", dbg_out["probe16"][0:64, 1920:1984], attn2[:, 0, 1, :])
                        seq1_mms(p1)
                        seq1_dve(p1)
                        seq2_mms(p1)
                        seq2_dve(p1)
                        gen1_evacs(c1)
                    for li, lvl in enumerate((1, 2, 4, 8, 16, 32)):
                        chain_lvl_mms(c0, li, lvl)
                        chain_lvl_mms(c1, li, lvl)
                        chain_lvl_evacs(c0, li, lvl)
                        chain_lvl_evacs(c1, li, lvl)
                    attnT_mms(c0)
                    attnT_mms(c1)
                    attnT_evac(c0)
                    attnT_evac(c1)
                    pend = (c0, c1)
                for p in (pend or ()):
                    seq1_mms(p)
                    seq1_dve(p)
                    seq2_mms(p)
                    seq2_dve(p)
                if DUMP_QK and hp == 0 and r == 0:
                    wait("sync", "dv", counts.get("dv", 0))
                    wait("sync", "ac", counts.get("ac", 0))
                    stx("dbg", qk_dump[0], qT_sb[:])
                    stx("dbg", qk_dump[1], kTn_sb[:])
                    stx("dbg", qk_dump[2], vT_sb[:])
                    v_qkd = stx("dbg", qk_dump[3], coT_sb[:, 0, :])
                    wait("sync", "dbg", v_qkd)
                    wait("vector", "dbg", v_qkd)
                    wait("scalar", "dbg", v_qkd)
                    wait("tensor", "dbg", v_qkd)
                    wait("gpsimd", "dbg", v_qkd)
                if dbg and hp == 0:
                    wait("sync", "dv", counts.get("dv", 0))
                    stx("dbg", dbg_out["coT0"][r], coT_sb[:, 0, :])

            ckpt(5)
            mark_phase(f'wo_r{r}')
            all_engine_barrier()
            ar.reset(mk_wo)
            wo_sb = ar.al([128, 8, D], BF16)
            v_wo = gdma("wo", wo_sb[:], woT_d.rearrange("(k p) o -> p k o", p=128))
            pe_dv_barrier()
            wait("tensor", "wo", v_wo)
            for m in range(NT):
                for n in range(2):
                    b = pb[6 + ((m * 2 + n) % 2)]
                    pe_dv_barrier()
                    for kt in range(8):
                        pe(lambda e, b=b, m=m, n=n, kt=kt: e.matmul(
                            b[:, :], coT_sb[:, kt, m * 128:(m + 1) * 128],
                            wo_sb[:, kt, n * 512:(n + 1) * 512],
                            start=(kt == 0), stop=(kt == 7)))
                    v = counts.get("pe", 0)
                    wait("vector", "pe", v)
                    dve(lambda e, b=b, m=m, n=n: e.tensor_copy(s_sb[:, m, n * 512:(n + 1) * 512], b[:, :]))
            v_s_done = counts.get("dv", 0)
            if dbg:
                wait("sync", "dv", v_s_done)
                for m in range(NT):
                    stx("dbg", dbg_out["s_attn"][r, m], s_sb[:, m, :])

            ckpt(6)
            mark_phase(f'ffn_rms1_r{r}')
            all_engine_barrier()
            ar.reset(0)
            xselT_sb = ar.al([128, 8, CAP], BF16)
            hNT_sb = ar.al([128, 8, CAP], BF16)
            zT_sb = ar.al([128, 8, CAP], F32)
            h2T_sb = ar.al([128, 8, CAP], BF16)
            oT_sb = ar.al([128, 8, CAP], BF16)
            mk_ffn1 = ar.mark()
            xsel_sb = ar.al([128, NCAP, D], F32)
            fsq_sb = ar.al([128, NCAP, D], F32)
            hnat_sb = ar.al([128, NCAP, D], BF16)

            wait("sync", "sc0", counts.get("sc0", 0))
            wait("sync", "sc1", counts.get("sc1", 0))
            v_xsel = sdma("scm", xsel_sb[:], xsel_dram[r].rearrange("(n p) d -> p n d", p=128))
            if dbg:
                wait("sync", "scm", v_xsel)
                stx("dbg", dbg_out["xsel"][r].rearrange("(n p) d -> p n d", p=128), xsel_sb[:])
            wait("scalar", "scm", v_xsel)
            v = act(lambda e: e.activation(fsq_sb[:], xsel_sb[:], AF.Square))
            wait("vector", "ac", v)
            wait("vector", "scm", v_xsel)
            dve(lambda e: e.tensor_reduce(fst_sb[:, :, 0:1], fsq_sb[:], axis=AX.X, op=ALU.add))
            v = dve(lambda e: e.tensor_scalar(fst_sb[:, :, 1:2], fst_sb[:, :, 0:1], 1.0 / D, 1e-6, ALU.mult, ALU.add))
            wait("scalar", "dv", v)
            v = act(lambda e: e.activation(fst_sb[:, :, 2:3], fst_sb[:, :, 1:2], AF.Ln))
            wait("vector", "ac", v)
            v = dve(lambda e: e.tensor_scalar(fst_sb[:, :, 2:3], fst_sb[:, :, 2:3], -0.5, None, ALU.mult))
            wait("scalar", "dv", v)
            v = act(lambda e: e.activation(fst_sb[:, :, 3:4], fst_sb[:, :, 2:3], AF.Exp))
            wait("vector", "ac", v)
            v_hn = dve(lambda e: e.tensor_tensor(hnat_sb[:], xsel_sb[:],
                                                 fst_sb[:, :, 3:4].to_broadcast([128, NCAP, D]), ALU.mult))
            wait("gpsimd", "dv", v_hn)
            gdma("gh", hnb_dram[r].rearrange("(n p) d -> p n d", p=128), hnat_sb[:])
            v2 = gdma("gh", xselb_dram[r].rearrange("(n p) d -> p n d", p=128), xsel_sb[:])
            wait("scalar", "gh", v2)
            v_hT = 0
            for j in range(8):
                adma("fft", hNT_sb[:, j, :], hnb_dram[r, :, j * 128:(j + 1) * 128], transpose=True)
                v_hT = adma("fft", xselT_sb[:, j, :], xselb_dram[r, :, j * 128:(j + 1) * 128], transpose=True)

            ckpt(7)
            all_engine_barrier()
            ar.reset(mk_ffn1)
            wqkv_sb = ar.al([128, 8, 1536], BF16)
            qkvT_sb = ar.al([128, 24, CAP], BF16)
            fattnT_sb = ar.al([128, NKB, 2, CAP], BF16)

            wait("tensor", "fft", v_hT)
            for half in range(2):
                v_wqkv = gdma("wqv", wqkv_sb[:], wqkvT_d.rearrange("(k p) o -> p k o", p=128)[:, :, half * 1536:(half + 1) * 1536])
                wait("vector", "wqv", v_wqkv)
                wait("vector", "dv", v_g12)
                for kt in range(8):
                    dve(lambda e, kt=kt: e.tensor_tensor(
                        wqkv_sb[:, kt, :], wqkv_sb[:, kt, :],
                        g1c_sb[:, kt:kt + 1].to_broadcast([128, 1536]), ALU.mult))
                if half == 0:
                    dve(lambda e: e.tensor_scalar(wqkv_sb[:, :, 0:D], wqkv_sb[:, :, 0:D], 0.125, None, ALU.mult))
                v_fold = counts.get("dv", 0)
                wait("tensor", "dv", v_fold)
                for oo in range(12):
                    o = half * 12 + oo
                    b = pb[6 + (o % 2)]
                    pe_dv_barrier()
                    for kt in range(8):
                        pe(lambda e, b=b, oo=oo, kt=kt: e.matmul(
                            b[:, 0:CAP], wqkv_sb[:, kt, oo * 128:(oo + 1) * 128],
                            hNT_sb[:, kt, :], start=(kt == 0), stop=(kt == 7)))
                    v = counts.get("pe", 0)
                    wait("vector", "pe", v)
                    dve(lambda e, b=b, o=o: e.tensor_copy(qkvT_sb[:, o, :], b[:, 0:CAP]))
                wait("gpsimd", "pe", counts.get("pe", 0))
            v_qkv = counts.get("dv", 0)
            if dbg:
                wait("sync", "dv", v_qkv)
                stx("dbg", dbg_out["qkvT0"][r][:, 0:CAP], qkvT_sb[:, 0, :])
                stx("dbg", dbg_out["qkvT0"][r][:, CAP:2 * CAP], qkvT_sb[:, 8, :])
                stx("dbg", dbg_out["qkvT0"][r][:, 2 * CAP:3 * CAP], qkvT_sb[:, 16, :])

            ckpt(8)
            mark_phase(f'ffn_attn_r{r}')
            wait("tensor", "dv", v_qkv)
            for hp in range(8):
                pe_dv_barrier()
                for kb in range(NKB):
                    MM(pb[1][0:128, kb * 64:kb * 64 + 64], qkvT_sb[0:64, 16 + hp, kb * 128:(kb + 1) * 128],
                       id_bf[0:64, 0:64], kb == 0, (0, 0))
                    v_mm = MM(pb[3][0:128, kb * 64:kb * 64 + 64], qkvT_sb[64:128, 16 + hp, kb * 128:(kb + 1) * 128],
                              id_bf[64:128, 64:128], kb == 0, (64, 0))
                wait("vector", "pe", v_mm)
                dve(lambda e: e.tensor_copy(vnat_sb[:, :, 0, :],
                                            pb[1][0:128, 0:NKB * 64].rearrange("p (k c) -> p k c", k=NKB)))
                dve(lambda e: e.tensor_copy(vnat_sb[:, :, 1, :],
                                            pb[3][0:128, 0:NKB * 64].rearrange("p (k c) -> p k c", k=NKB)))
                v_stt_prev = counts.get("dv", 0)
                for qs in range(NQS):
                    wait("tensor", "dv", v_stt_prev)
                    MM(pb[0][0:64, 0:CAP], qkvT_sb[0:64, hp, qs * 64:(qs + 1) * 64],
                       qkvT_sb[0:64, 8 + hp, :], True, (0, 0))
                    v_sb_ = MM(pb[2][0:64, 0:CAP], qkvT_sb[64:128, hp, qs * 64:(qs + 1) * 64],
                               qkvT_sb[64:128, 8 + hp, :], True, (64, 0))
                    wait("vector", "pe", v_sb_)
                    moff = 64 * (NQS - 1 - qs)
                    dve(lambda e, qs=qs, moff=moff: e.scalar_tensor_tensor(
                        fS_sb[:, 0, :], pb[0][0:64, 0:CAP], 1.0, fmb_sb[:, moff:moff + CAP], ALU.mult, ALU.add))
                    v_stt_prev = dve(lambda e, qs=qs, moff=moff: e.scalar_tensor_tensor(
                        fS_sb[:, 1, :], pb[2][0:64, 0:CAP], 1.0, fmb_sb[:, moff:moff + CAP], ALU.mult, ALU.add))
                    wait("scalar", "dv", counts.get("dv", 0))
                    act(lambda e: e.activation(fS_sb[:, 0, :], fS_sb[:, 0, :], AF.Exp,
                                               accum_out=fsum_sb[:, 0:1]))
                    v = act(lambda e: e.activation(fS_sb[:, 1, :], fS_sb[:, 1, :], AF.Exp,
                                                   accum_out=fsum_sb[:, 1:2]))
                    wait("vector", "ac", v)
                    dve(lambda e: e.reciprocal(frec_sb[:], fsum_sb[:]))
                    v_fat = dve(lambda e: e.tensor_tensor(fattn_sb[:], fS_sb[:],
                                                          frec_sb[:].unsqueeze(2).to_broadcast([64, 2, CAP]), ALU.mult))
                    if dbg and hp == 0 and qs == 0 and r == 0:
                        wait("sync", "dv", v_fat)
                        stx("dbg", dbg_out["probe32"][0:64, 128:128 + CAP], fS_sb[:, 0, :])
                        stx("dbg", dbg_out["probe16"][0:64, 448:448 + CAP], fattn_sb[:, 0, :])
                        stx("dbg", dbg_out["probe16"][0:64, 2048:2048 + CAP], fattn_sb[:, 1, :])
                        wait("vector", "dbg", counts["dbg"])
                    wait("tensor", "dv", v_fat)
                    for kb in range(NKB):
                        MM(pb[1][0:128, 256 + kb * 64:256 + kb * 64 + 64],
                           fattn_sb[:, 0, kb * 128:(kb + 1) * 128], id_bf[0:64, 0:64], kb == 0, (0, 0))
                        v_mm = MM(pb[3][0:128, 256 + kb * 64:256 + kb * 64 + 64],
                                  fattn_sb[:, 1, kb * 128:(kb + 1) * 128], id_bf[0:64, 0:64], kb == 0, (0, 0))
                    wait("vector", "pe", v_mm)
                    dve(lambda e, qs=qs: e.tensor_copy(
                        fattnT_sb[:, :, 0, qs * 64:(qs + 1) * 64],
                        pb[1][0:128, 256:256 + NKB * 64].rearrange("p (k c) -> p k c", k=NKB)))
                    dve(lambda e, qs=qs: e.tensor_copy(
                        fattnT_sb[:, :, 1, qs * 64:(qs + 1) * 64],
                        pb[3][0:128, 256:256 + NKB * 64].rearrange("p (k c) -> p k c", k=NKB)))
                v_fT = counts.get("dv", 0)
                wait("tensor", "dv", v_fT)
                for kb in range(NKB):
                    MM(pb[0][0:64, 0:CAP], vnat_sb[:, kb, 0, :], fattnT_sb[:, kb, 0, :], kb == 0, (0, 0))
                    v_mm = MM(pb[4][64:128, 0:CAP], vnat_sb[:, kb, 1, :], fattnT_sb[:, kb, 1, :], kb == 0, (0, 64))
                wait("vector", "pe", v_mm)
                dve(lambda e, hp=hp: e.tensor_copy(oT_sb[0:64, hp, :], pb[0][0:64, 0:CAP]))
                dve(lambda e, hp=hp: e.tensor_copy(oT_sb[64:128, hp, :], pb[4][64:128, 0:CAP]))
                if dbg and hp == 0 and r == 0:
                    wait("sync", "dv", counts["dv"])
                    stx("dbg", dbg_out["probe16"][:, 1024:1024 + CAP], fattnT_sb[:, 0, 0, :])
                    stx("dbg", dbg_out["probe16"][:, 1536:1536 + 64], vnat_sb[:, 0, 0, :])
                    wait("vector", "dbg", counts["dbg"])
            v_oT = counts.get("dv", 0)
            if dbg:
                wait("sync", "dv", v_oT)
                stx("dbg", dbg_out["oT0"][r], oT_sb[:, 0, :])

            ckpt(9)
            mark_phase(f'ffn_wout_r{r}')
            all_engine_barrier()
            ar.reset(mk_ffn1)
            wout_sb = ar.al([128, 8, D], BF16)
            v_wout = gdma("wou", wout_sb[:], woutT_d.rearrange("(k p) o -> p k o", p=128))
            wait("tensor", "dv", v_oT)
            wait("tensor", "wou", v_wout)
            for o in range(8):
                b = pb[6 + (o % 2)]
                pe_dv_barrier()
                for kt in range(8):
                    pe(lambda e, b=b, o=o, kt=kt: e.matmul(
                        b[:, 0:CAP], wout_sb[:, kt, o * 128:(o + 1) * 128],
                        oT_sb[:, kt, :], start=(kt == 0), stop=(kt == 7)))
                v = counts.get("pe", 0)
                wait("vector", "pe", v)
                dve(lambda e, b=b, o=o: e.scalar_tensor_tensor(
                    zT_sb[:, o, :], b[:, 0:CAP], 1.0, xselT_sb[:, o, :], ALU.mult, ALU.add))
            v_z = counts.get("dv", 0)
            if dbg:
                wait("sync", "dv", v_z)
                stx("dbg", dbg_out["zT0"][r], zT_sb[:, 0, :])

            v_mm_prev = [0, 0]
            for kt in range(8):
                zb = zsq_sb[kt % 2]
                wait("scalar", "dv", v_z)
                wait("scalar", "pe", v_mm_prev[kt % 2])
                v = act(lambda e, kt=kt, zb=zb: e.activation(zb[:], zT_sb[:, kt, :], AF.Square))
                wait("tensor", "ac", v)
                v_mm_prev[kt % 2] = pe(lambda e, kt=kt, zb=zb: e.matmul(
                    pb[5][0:1, 0:CAP], ones128_sb[:], zb[:],
                    start=(kt == 0), stop=(kt == 7)))
            v = v_mm_prev[1]
            wait("vector", "pe", v)
            v = dve(lambda e: e.tensor_scalar(ssq2_sb[:], pb[5][0:1, 0:CAP], 1.0 / D, 1e-6, ALU.mult, ALU.add))
            wait("scalar", "dv", v)
            v = act(lambda e: e.activation(ssq2_sb[:], ssq2_sb[:], AF.Ln))
            wait("vector", "ac", v)
            v = dve(lambda e: e.tensor_scalar(ssq2_sb[:], ssq2_sb[:], -0.5, None, ALU.mult))
            wait("scalar", "dv", v)
            v = act(lambda e: e.activation(rstd2_sb[:], ssq2_sb[:], AF.Exp))
            wait("tensor", "ac", v)
            pe_dv_barrier()
            v = pe(lambda e: e.matmul(pb[5][:, 0:CAP], onesrow_sb[0:1, :], rstd2_sb[0:1, :],
                                      start=True, stop=True, skip_group_check=True))
            wait("vector", "pe", v)
            dve(lambda e: e.tensor_copy(rbc2_sb[:], pb[5][:, 0:CAP]))
            for kt in range(8):
                dve(lambda e, kt=kt: e.tensor_tensor(h2T_sb[:, kt, :], zT_sb[:, kt, :], rbc2_sb[:], ALU.mult))
            v_h2 = counts.get("dv", 0)

            ckpt(10)
            mark_phase(f'ffn_mlp_r{r}')
            all_engine_barrier()
            ar.reset(mk_ffn1)
            xpT_sb = ar.al([128, 8, CAP], F32)
            duT_sb = ar.al([128, 8, CAP], F32)
            a1_sb = ar.al([128, CAP], F32)
            mk_mlp = ar.mark()
            gT_sb = ar.al([128, KFF, CAP], BF16)
            w12s = [ar.al([128, 8, 256], BF16) for _ in range(2)]
            w3s = [ar.al([128, KFF, 128], BF16) for _ in range(2)]

            w12_free = {0: 0, 1: 0}
            wait("vector", "dv", v_h2)
            for o in range(KFF):
                bi = o % 2
                wait("gpsimd", "pe", w12_free[bi])
                gdma(f"wf{bi}", w12s[bi][:, :, 0:128], w1T_d.rearrange("(k p) o -> p k o", p=128)[:, :, o * 128:(o + 1) * 128])
                v_w = gdma(f"wf{bi}", w12s[bi][:, :, 128:256], w2T_d.rearrange("(k p) o -> p k o", p=128)[:, :, o * 128:(o + 1) * 128])
                wait("vector", f"wf{bi}", v_w)
                for kt in range(8):
                    dve(lambda e, kt=kt, bi=bi: e.tensor_tensor(
                        w12s[bi][:, kt, :], w12s[bi][:, kt, :],
                        g2c_sb[:, kt:kt + 1].to_broadcast([128, 256]), ALU.mult))
                v_wf = counts.get("dv", 0)
                wait("tensor", "dv", v_wf)
                for kt in range(8):
                    pe(lambda e, o=o, kt=kt, bi=bi: e.matmul(
                        pb[6][:, 0:CAP], w12s[bi][:, kt, 0:128],
                        h2T_sb[:, kt, :], start=(kt == 0), stop=(kt == 7)))
                v_u = counts.get("pe", 0)
                for kt in range(8):
                    pe(lambda e, o=o, kt=kt, bi=bi: e.matmul(
                        pb[7][:, 0:CAP], w12s[bi][:, kt, 128:256],
                        h2T_sb[:, kt, :], start=(kt == 0), stop=(kt == 7)))
                v_v2 = counts.get("pe", 0)
                w12_free[bi] = v_v2
                wait("scalar", "pe", v_u)
                wait("scalar", "dv", counts.get("dv", 0) if o > 0 else 0)
                v_su = act(lambda e: e.activation(su_sb[:], pb[6][:, 0:CAP], AF.Silu))
                wait("vector", "pe", v_v2)
                dve(lambda e: e.tensor_copy(vv_sb[:], pb[7][:, 0:CAP]))
                wait("vector", "ac", v_su)
                v_g = dve(lambda e, o=o: e.tensor_tensor(gT_sb[:, o, :], su_sb[:], vv_sb[:], ALU.mult))
                wait("tensor", "dv", v_g)
                pe_ac_barrier()
            v_gT = counts.get("dv", 0)

            w3_free = {0: 0, 1: 0}
            wait("tensor", "dv", v_gT)
            for o in range(8):
                bi = o % 2
                b = pb[6 + bi]
                wait("gpsimd", "pe", w3_free[bi])
                v_w = gdma(f"w3{bi}", w3s[bi][:], w3T_d.rearrange("(k p) o -> p k o", p=128)[:, :, o * 128:(o + 1) * 128])
                wait("tensor", f"w3{bi}", v_w)
                pe_dv_barrier()
                for kt in range(KFF):
                    pe(lambda e, b=b, o=o, kt=kt, bi=bi: e.matmul(
                        b[:, 0:CAP], w3s[bi][:, kt, :],
                        gT_sb[:, kt, :], start=(kt == 0), stop=(kt == KFF - 1)))
                v = counts.get("pe", 0)
                w3_free[bi] = v
                wait("vector", "pe", v)
                dve(lambda e, b=b, o=o: e.scalar_tensor_tensor(
                    xpT_sb[:, o, :], b[:, 0:CAP], 1.0, zT_sb[:, o, :], ALU.mult, ALU.add))
            v_xp = counts.get("dv", 0)

            ckpt(11)
            wait("scalar", "dv", v_xp)
            v_du_prev = 0
            for o in range(8):
                wait("scalar", "dv", v_du_prev)
                act(lambda e, o=o: e.activation(duT_sb[:, o, :], xselT_sb[:, o, :], AF.Silu))
                v_a = act(lambda e, o=o: e.activation(a1_sb[:], xpT_sb[:, o, :], AF.Silu))
                wait("vector", "ac", v_a)
                v_du_prev = dve(lambda e, o=o: e.tensor_tensor(duT_sb[:, o, :], a1_sb[:], duT_sb[:, o, :], ALU.subtract))
            v_duT = counts.get("dv", 0)

            all_engine_barrier()
            ar.reset(mk_mlp)
            dun_sb = ar.al([128, NCAP, D], F32)
            wait("tensor", "dv", v_duT)
            for o in range(8):
                for qb in range(NCAP):
                    b = pb[6 + ((o * NCAP + qb) % 2)]
                    pe_dv_barrier()
                    v = pe(lambda e, b=b, o=o, qb=qb: e.matmul(
                        b[:, 0:128], duT_sb[:, o, qb * 128:(qb + 1) * 128], id_f32[:, :],
                        start=True, stop=True))
                    wait("vector", "pe", v)
                    dve(lambda e, b=b, o=o, qb=qb: e.tensor_copy(
                        dun_sb[:, qb, o * 128:(o + 1) * 128], b[:, 0:128]))
            v_dun = counts.get("dv", 0)
            wait("sync", "dv", v_dun)
            v_dp = sdma("scm", dp_dram[r].rearrange("(n p) d -> p n d", p=128), dun_sb[:])
            if dbg:
                stx("dbg", dbg_out["dp"][r].rearrange("(n p) d -> p n d", p=128), dun_sb[:])

            ckpt(12)
            mark_phase(f'epilogue_r{r}')
            all_engine_barrier()
            ar.reset(0)
            g_t = [ar.al([128, D], F32) for _ in range(2)]
            xe_t = [ar.al([128, D], F32) for _ in range(2)]
            out_t = [ar.al([128, D], F32) for _ in range(2)]
            ep_free = {0: (0, 0), 1: (0, 0)}
            for j in range(NT):
                bi = j % 2
                dv0, st0 = ep_free[bi]
                wait("sync", "dv", dv0)
                wait("sync", f"ey{bi}", st0)
                v_xnl = sdma(f"eg{bi}", g_t[bi][:], xn_dram[r, j * 128:(j + 1) * 128, :])
                wait("scalar", f"eg{bi}", v_xnl)
                v_si = act(lambda e, bi=bi: e.activation(g_t[bi][:], g_t[bi][:], AF.Silu))
                wait("gpsimd", "ac", v_si)
                wait("gpsimd", "scm", v_dp)
                v_ga = _mk("gpsimd", f"ga{bi}", 16, lambda e, bi=bi, j=j, r=r: e.indirect_dma_start(
                    out=g_t[bi][:],
                    out_offset=None,
                    in_=dp_dram[r],
                    in_offset=IndirectOffsetOnAxis(rki32[:, j:j + 1], 0),
                    bounds_check=bc_cell[0],
                    oob_is_err=False,
                    compute_op=ALU.add))
                v_xl = sdma(f"ex{bi}", xe_t[bi][:], x_in[r, j * 128:(j + 1) * 128, :])
                wait("vector", f"ga{bi}", v_ga)
                wait("vector", f"ex{bi}", v_xl)
                dve(lambda e, bi=bi, j=j: e.tensor_tensor(sg_t[:], s_sb[:, j, :], g_t[bi][:], ALU.mult))
                v_o = dve(lambda e, bi=bi: e.tensor_tensor(out_t[bi][:], xe_t[bi][:], sg_t[:], ALU.add))
                wait("sync", "dv", v_o)
                v_y = stx(f"ey{bi}", y_out[r, j * 128:(j + 1) * 128, :], out_t[bi][:])
                ep_free[bi] = (v_o, v_y)
        except StopEmission:
            pass
        mark_phase('end')
        for _nm in list(counts):
            wait("sync", _nm, counts[_nm])

        block = ctx.enter_context(nc.Block())
        for name in ("sync", "vector", "scalar", "gpsimd", "tensor"):
            fns = ops[name]
            def mk(fns):
                def body(eng):
                    for f in fns:
                        f(eng)
                return body
            getattr(block, name)(mk(fns))

    return nc, dbg_out


def make_consts(T, CAP):
    import ml_dtypes
    C = 64
    NQS = CAP // 64
    ident_bf = np.eye(128).astype(ml_dtypes.bfloat16)
    ident_f32 = np.eye(128, dtype=np.float32)
    onesrow = np.ones((1, 128), np.float32)
    ones128 = np.ones((128, 1), np.float32)
    ones64c = np.zeros((128, 2), np.float32)
    ones64c[0:64, 0] = 1.0
    ones64c[64:128, 1] = 1.0
    ones2T = np.ascontiguousarray(ones64c.T)
    t_i = np.arange(C)
    causU = (t_i[None, :] > t_i[:, None]).astype(np.float32)
    causL = (t_i[None, :] < t_i[:, None]).astype(np.float32)
    dmask = np.where(t_i[None, :] > t_i[:, None], NEG, 0.0).astype(np.float32)
    p_i = np.arange(64)[:, None]
    j_i = np.arange(CAP + 64 * NQS)[None, :]
    fmbase = np.where(j_i - 64 * (NQS - 1) <= p_i, 0.0, NEG).astype(np.float32)
    return dict(ident_bf=ident_bf, ident_f32=ident_f32, onesrow=onesrow,
                ones128=ones128, ones64c=ones64c, ones2T=ones2T,
                causU=causU, causL=causL, dmask=dmask, fmbase=fmbase)


def make_inputs_map(cfg, xshard, Wq, Wk, Wv, Wo, alpha_raw, beta_raw, w_router,
                    W_qkv, W_out, g1, g2, W1, W2, W3):
    T, R, DFFP = cfg["T"], cfg["R"], cfg["DFFP"]
    CAP = T // 4
    D = 1024
    tr = lambda w: np.ascontiguousarray(np.asarray(w, np.float32).T)
    DFF = W1.shape[0]
    W1p = np.zeros((DFFP, D), np.float32); W1p[:DFF] = W1
    W2p = np.zeros((DFFP, D), np.float32); W2p[:DFF] = W2
    W3p = np.zeros((D, DFFP), np.float32); W3p[:, :DFF] = W3
    g12 = np.concatenate([np.asarray(g1, np.float32).reshape(8, 128),
                          np.asarray(g2, np.float32).reshape(8, 128)], axis=0)
    ab = np.concatenate([np.asarray(alpha_raw).reshape(-1),
                         np.asarray(beta_raw).reshape(-1)]).reshape(1, 32).astype(np.float32)
    m = dict(x=np.ascontiguousarray(np.asarray(xshard, np.float32)),
             wqT=tr(Wq), wkT=tr(Wk), wvT=tr(Wv), woT=tr(Wo),
             ab_raw=ab, wrouter=np.asarray(w_router, np.float32).reshape(1, D),
             wqkvT=tr(W_qkv), woutT=tr(W_out), g12=g12,
             w1T=tr(W1p), w2T=tr(W2p), w3T=np.ascontiguousarray(W3p.T))
    m.update(make_consts(T, CAP))
    return m


_CFG = dict(T=2048, R=2, DFFP=2816, debug=False, strict=True, no_act_evac=True, no_weave=True)
_N_CORES = 8
_BUILT = None


def _get_program():
    global _BUILT
    if _BUILT is None:
        _BUILT = build_program(_CFG)
    return _BUILT


def kernel(L0_attn_x, L0_attn_Wq, L0_attn_Wk, L0_attn_Wv, L0_attn_Wo,
           L0_attn_alpha_raw, L0_attn_beta_raw, L0_ffn_w_router, L0_ffn_W_qkv,
           L0_ffn_W_out, L0_ffn_g1, L0_ffn_g2, L0_ffn_W1, L0_ffn_W2, L0_ffn_W3):
    x = np.asarray(L0_attn_x, np.float32)
    nc, _ = _get_program()
    base = make_inputs_map(_CFG, x[0:2], L0_attn_Wq, L0_attn_Wk, L0_attn_Wv,
                           L0_attn_Wo, L0_attn_alpha_raw, L0_attn_beta_raw,
                           L0_ffn_w_router, L0_ffn_W_qkv, L0_ffn_W_out,
                           L0_ffn_g1, L0_ffn_g2, L0_ffn_W1, L0_ffn_W2, L0_ffn_W3)
    in_maps = []
    for c in range(_N_CORES):
        m = dict(base)
        m["x"] = np.ascontiguousarray(x[2 * c:2 * c + 2])
        in_maps.append(m)
    res = run_bass_kernel_spmd(nc, in_maps, core_ids=list(range(_N_CORES)))
    return np.concatenate([np.asarray(r["y"], np.float32) for r in res.results], axis=0)
